# version 13
# baseline (speedup 1.0000x reference)
import numpy as np
import concourse.bass as bass
import concourse.mybir as mybir
from concourse.bass_utils import run_bass_kernel_spmd

F32 = mybir.dt.float32
BF16 = mybir.dt.bfloat16
AF = mybir.ActivationFunctionType
ALU = mybir.AluOpType

D = 1024
DC = 8
S = 2048
H = 16
W = 512
NW = 4
EPS = 1e-5
ENGS = ("sync", "gpsimd", "tensor", "vector", "scalar")

_CACHE = {}
DEBUG = False
HILO = True
PARANOID = 0


class Sem:
    __slots__ = ("h", "n", "name")

    def __init__(self, h, name):
        self.h = h
        self.n = 0
        self.name = name


def build_program():
    nc = bass.Bass("TRN2", target_bir_lowering=False)
    ops = {e: [] for e in ENGS}
    seen = {e: {} for e in ENGS}

    def on(eng, fn):
        ops[eng].append(fn)

    def w(eng, sem, cnt=None):
        if cnt is None:
            cnt = sem.n
        if cnt <= 0:
            return
        if seen[eng].get(sem.name, 0) >= cnt:
            return
        seen[eng][sem.name] = cnt
        on(eng, lambda E, s=sem.h, c=cnt: E.wait_ge(s, c))

    dt = nc.dram_tensor
    xo_d = dt("xo", [2, D, S], F32, kind="ExternalInput").ap()
    xs_d = dt("xs", [D, S], F32, kind="ExternalInput").ap()
    wq_d = dt("wq", [D, D], F32, kind="ExternalInput").ap()
    wk_d = dt("wk", [D, D], F32, kind="ExternalInput").ap()
    wv_d = dt("wv", [D, D], F32, kind="ExternalInput").ap()
    wo_d = dt("wo", [D, D], F32, kind="ExternalInput").ap()
    w1_d = dt("w1", [D, 2 * D], F32, kind="ExternalInput").ap()
    w2_d = dt("w2", [2 * D, D], F32, kind="ExternalInput").ap()
    mw1_d = dt("mw1", [D, 8 * D], F32, kind="ExternalInput").ap()
    mw2_d = dt("mw2", [8 * D, D], F32, kind="ExternalInput").ap()
    ln_d = {}
    for nm in ("lnq_s", "lnk_s", "lnv_s", "lnm_s", "lnq_b", "lnk_b", "lnv_b", "lnm_b"):
        ln_d[nm] = dt(nm, [128, DC], F32, kind="ExternalInput").ap()
    bq_d = dt("bq", [1, D], F32, kind="ExternalInput").ap()
    bk_d = dt("bk", [1, D], F32, kind="ExternalInput").ap()
    bv_d = dt("bv", [1, D], F32, kind="ExternalInput").ap()
    bo_d = dt("bo", [1, D], F32, kind="ExternalInput").ap()
    b1_d = dt("b1", [1, 2 * D], F32, kind="ExternalInput").ap()
    b2_d = dt("b2", [1, D], F32, kind="ExternalInput").ap()
    ew_d = dt("ew", [1, 1024], F32, kind="ExternalInput").ap()
    mko_d = dt("mko", [128, 1], F32, kind="ExternalInput").ap()
    mks_d = dt("mks", [128, 1], F32, kind="ExternalInput").ap()

    oa_d = dt("oa", [2, D, S], F32, kind="ExternalOutput").ap()
    ob_d = dt("ob", [D, W], F32, kind="ExternalOutput").ap()

    zq_d = dt("zq", [D, D], BF16, kind="Internal").ap()
    zk_d = dt("zk", [D, D], BF16, kind="Internal").ap()
    zv_d = dt("zv", [D, D], BF16, kind="Internal").ap()
    zbv_d = dt("zbv", [1, D], BF16, kind="Internal").ap()

    A = nc.alloc_sbuf_tensor
    AR1 = A("AR1", [128, 16384], BF16)
    AR2 = A("AR2", [128, 16384], BF16)
    AR3 = A("AR3", [128, 16640], BF16)
    AR4 = A("AR4", [128, 16384], BF16)
    FR = A("FR", [128, 32768], BF16)
    ones_r_bf = A("ones_r_bf", [1, 512], BF16)
    ones_c_bf = A("ones_c_bf", [128, 1], BF16)
    ones_r_f = A("ones_r_f", [1, 128], F32)
    ones_c_f = A("ones_c_f", [128, 1], F32)
    eps_c = A("eps_c", [128, 1], F32)
    mko_s = A("mko_s", [128, 1], F32)
    mks_s = A("mks_s", [128, 1], F32)
    lncols = A("lncols", [128, 8 * DC], F32)
    lnb_bf = A("lnb_bf", [128, 4 * DC], BF16)
    r_mu = A("r_mu", [1, 512], F32)
    r_rsig = A("r_rsig", [1, 512], F32)
    r_G = A("r_G", [1, 512], F32)
    r_sd = A("r_sd", [1, 512], F32)
    r_var = A("r_var", [1, 512], F32)
    Rb_s = A("Rb_s", [64, 512], F32)
    bcol = A("bcol", [128, 6 * DC], F32)

    PSAL = nc.alloc_psum_tensor
    psS0 = PSAL("psS0", [128, 1024], F32)
    psS1 = PSAL("psS1", [128, 1024], F32)
    psA = PSAL("psA", [128, 512], F32)
    psB = PSAL("psB", [128, 512], F32)
    psC = PSAL("psC", [128, 512], F32)
    psD = PSAL("psD", [128, 512], F32)
    PSN = {"psS0": psS0, "psS1": psS1, "psA": psA, "psB": psB, "psC": psC, "psD": psD}

    AR1f = AR1.ap().bitcast(F32)
    AR2f = AR2.ap().bitcast(F32)
    AR3f = AR3.ap().bitcast(F32)
    AR4f = AR4.ap().bitcast(F32)
    FRf = FR.ap().bitcast(F32)

    pe = Sem(nc.alloc_semaphore(name="pe"), "pe")
    ve = Sem(nc.alloc_semaphore(name="ve"), "ve")
    sc = Sem(nc.alloc_semaphore(name="sc"), "sc")
    LD = [Sem(nc.alloc_semaphore(name=f"ld{i}"), f"ld{i}") for i in range(6)]
    LG = [Sem(nc.alloc_semaphore(name=f"lg{i}"), f"lg{i}") for i in range(4)]
    QST = {"sync": [LD, 0], "gpsimd": [LG, 0]}
    ALLSEMS = (pe, ve, sc) + tuple(LD) + tuple(LG)

    def dma(eng, out_ap, in_ap, slow=False):
        q = QST[eng]
        s = q[0][q[1] % len(q[0])]
        q[1] += 1
        w(eng, s, s.n)
        on(eng, lambda E, o=out_ap, i=in_ap, sh=s.h, sl=slow:
           E.dma_start(out=o, in_=i, allow_slow_non_contiguous=sl).then_inc(sh, 16))
        s.n += 16
        return (s, s.n)

    def wt(eng, tok):
        if tok is not None:
            w(eng, tok[0], tok[1])

    def mm(out_ap, lhsT, rhs, start, stop, inc=False):
        if inc:
            on("tensor", lambda E, o=out_ap, l=lhsT, r=rhs, s1=start, s2=stop:
               E.matmul(o, l, r, start=s1, stop=s2).then_inc(pe.h, 1))
            pe.n += 1
            return pe.n
        on("tensor", lambda E, o=out_ap, l=lhsT, r=rhs, s1=start, s2=stop:
           E.matmul(o, l, r, start=s1, stop=s2))
        return pe.n

    def vop(name, *args, inc=True):
        on("vector", lambda E, n=name, a=args: getattr(E, n)(*a).then_inc(ve.h, 1))
        ve.n += 1
        return ve.n

    def vchain():
        w("vector", ve, ve.n)

    def sact(out_ap, in_ap, func, bias=0.0, scale=1.0, inc=True):
        on("scalar", lambda E, o=out_ap, i=in_ap, f=func, bi=bias, s=scale:
           E.activation(o, i, f, bias=bi, scale=s).then_inc(sc.h, 1))
        sc.n += 1
        return sc.n

    def barrier(engs=ENGS):
        for e in engs:
            for s in ALLSEMS:
                w(e, s, s.n)

    def c3(ap2d, c, n):
        return ap2d.rearrange("p (c n) -> p c n", c=c)

    def dcpn(ap2d):
        return ap2d.rearrange("(c p) n -> p c n", p=128)

    XN = FR.ap()[:, 0:16384]
    E_BUF = [FR.ap()[:, 0:8192], FR.ap()[:, 8192:16384]]
    W1B = FR.ap()[:, 0:16384]
    WPS = FR.ap()[:, 16384:24576]
    SQB = FR.ap()[:, 24576:28672]
    ATTO = FR.ap()[:, 24576:28672]
    BIAS = FR.ap()[:, 28672:32768]
    S1BF = FR.ap()[:, 0:4096]
    MW2B = FR.ap()[:, 16384:20480]

    K_s = AR1.ap()
    Q_s = AR2.ap()
    V_s = AR3.ap()
    ATT = AR4.ap()
    W2B = AR1.ap()
    XW3 = AR2f
    H_F = AR3f
    XNH = AR3.ap()[:, 8192:12288]
    HID = AR3.ap()[:, 12288:16384]
    XW1 = AR4f
    OUT_STG = AR3f[:, 4096:8192]
    S1F = AR4f
    MW1B = AR3.ap()
    HID2 = [AR1.ap(), AR2.ap()]
    OBST = AR2f

    ps_free = {k: (ve, 0) for k in PSN}

    def ps_guard(name):
        s, t = ps_free[name]
        w("tensor", s, t)

    sqb_free = [pe, 0]
    hil_free = [0]

    HIL0 = r_sd.ap().bitcast(BF16)
    HIL1 = r_var.ap().bitcast(BF16)

    def bcast_hilo(row_f32, ps_out, hil, m):
        if not HILO:
            t = vop("memset", hil[0:1, 0:1], 0.0, inc=True)
            w("tensor", ve, t)
            return mm(ps_out, ones_r_f[0:1, 0:m], row_f32, True, True, inc=True)
        vchain()
        w("vector", pe, hil_free[0])
        vop("tensor_copy", hil[0:1, 0:512], row_f32)
        vchain()
        t = vop("tensor_tensor", hil[0:1, 512:1024], row_f32, hil[0:1, 0:512],
                ALU.subtract, inc=True)
        w("tensor", ve, t)
        mm(ps_out, ones_r_bf[0:1, 0:m], hil[0:1, 0:512], True, False)
        t_bc = mm(ps_out, ones_r_bf[0:1, 0:m], hil[0:1, 512:1024], False, True, inc=True)
        hil_free[0] = t_bc
        return t_bc

    vop("memset", ones_r_bf[:, :], 1.0)
    vop("memset", ones_c_bf[:, :], 1.0)
    vop("memset", ones_r_f[:, :], 1.0)
    vop("memset", ones_c_f[:, :], 1.0)
    ve_const = vop("memset", eps_c[:, :], EPS, inc=True)
    for e in ("tensor", "scalar", "sync", "gpsimd"):
        w(e, ve, ve_const)

    ln_toks = []
    for i, nm in enumerate(("lnq_s", "lnk_s", "lnv_s", "lnm_s", "lnq_b", "lnk_b", "lnv_b", "lnm_b")):
        ln_toks.append(dma("sync", lncols[:, i * DC:(i + 1) * DC], ln_d[nm]))
    t_mko = dma("sync", mko_s[:, :], mko_d)
    t_mks = dma("sync", mks_s[:, :], mks_d)
    t_bq = dma("sync", FRf[0:1, 6144:7168], bq_d)
    t_bk = dma("sync", FRf[0:1, 7168:8192], bk_d)
    t_bv = dma("sync", FRf[0:1, 8192:9216], bv_d)
    t_b1 = dma("sync", FRf[0:1, 9216:11264], b1_d)
    t_b2 = dma("sync", FRf[0:1, 11264:12288], b2_d)
    t_bo = dma("sync", FRf[0:1, 12288:13312], bo_d)
    for t in ln_toks[4:8]:
        wt("vector", t)
    ve_lnb = vop("tensor_copy", lnb_bf[:, :], lncols[:, 4 * DC:8 * DC], inc=True)
    def row_to_cols(row_f32, bcol_off, n, gate_dma=None, gate_ve=0):
        ps_guard("psD")
        wt("tensor", gate_dma)
        w("tensor", ve, gate_ve)
        for c in range(n):
            mm(psD[:, c:c + 1], row_f32[0:1, c * 128:(c + 1) * 128],
               ones_c_f[0:1, 0:1], True, True, inc=(c == n - 1))
        t = pe.n
        w("vector", pe, t)
        tk = vop("tensor_copy", bcol[:, bcol_off:bcol_off + n], psD[:, 0:n], inc=True)
        ps_free["psD"] = (ve, tk)
        return t

    row_to_cols(FRf[0:1, 11264:12288], 4 * DC, DC, gate_dma=t_b2)
    row_to_cols(FRf[0:1, 12288:13312], 5 * DC, DC, gate_dma=t_bo)

    prep = [
        (wq_d, zq_d, 0, 6144, D, 0, t_bq),
        (wk_d, zk_d, 1, 7168, D, DC, t_bk),
        (wv_d, zv_d, 2, 8192, D, None, t_bv),
    ]
    beff_ps = [psA, psB, psC, psD]
    frrow_dma = None
    frrow_pe = 0
    scale_tok = [0, 0]
    tmp_tok = [0, 0]
    store_tok = [None, None]
    for (src, zdst, gi, brow_off, dout, beff_dst, t_brow) in prep:
        nb = dout // 512
        mm_tok = 0
        for c in range(DC):
            par = c % 2
            stg = AR1f[:, par * 2048: par * 2048 + dout]
            tmp = AR2.ap()[:, par * 2048: par * 2048 + dout]
            sbf = AR2.ap()[:, 4096 + par * 2048: 4096 + par * 2048 + dout]
            w("sync", ve, scale_tok[par])
            w("sync", pe, mm_tok)
            t_ld = dma("sync", c3(stg, nb, 512), c3(src[c * 128:(c + 1) * 128, :], nb, 512))
            wt("vector", t_ld)
            wt("vector", ln_toks[gi])
            wt("vector", store_tok[par])
            w("vector", ve, ve_lnb)
            w("vector", pe, tmp_tok[par])
            ve_tmp = vop("tensor_copy", tmp, stg, inc=True)
            sc_col = lncols[:, gi * DC + c: gi * DC + c + 1]
            ve_scl = vop("tensor_scalar", sbf, stg, sc_col, None, ALU.mult, inc=True)
            scale_tok[par] = ve_scl
            w("tensor", ve, ve_tmp)
            for nbi in range(nb):
                mm_tok = mm(beff_ps[nbi][0:1, :], lnb_bf[:, gi * DC + c: gi * DC + c + 1],
                            tmp[:, nbi * 512:(nbi + 1) * 512], c == 0, c == DC - 1,
                            inc=(nbi == nb - 1))
            tmp_tok[par] = mm_tok
            w("sync", ve, ve_scl)
            store_tok[par] = dma("sync", c3(zdst[c * 128:(c + 1) * 128, :], nb, 512),
                                 c3(sbf, nb, 512))
        w("vector", pe, mm_tok)
        wt("vector", t_brow)
        wt("vector", frrow_dma)
        w("vector", pe, frrow_pe)
        ve_beff = 0
        for nbi in range(nb):
            ve_beff = vop("tensor_tensor", FRf[0:1, nbi * 512:(nbi + 1) * 512],
                          beff_ps[nbi][0:1, :],
                          FRf[0:1, brow_off + nbi * 512: brow_off + (nbi + 1) * 512],
                          ALU.add, inc=(nbi == nb - 1))
        for k in ("psA", "psB", "psC", "psD")[:nb]:
            ps_free[k] = (ve, ve_beff)
        if beff_dst is None:
            w("gpsimd", ve, ve_beff)
            frrow_dma = dma("gpsimd", zbv_d, FRf[0:1, 0:dout])
        else:
            frrow_pe = row_to_cols(FRf[0:1, 0:dout], beff_dst, dout // 128,
                                   gate_ve=ve_beff)

    wt("gpsimd", frrow_dma)
    t_w1s = dma("gpsimd", c3(W1B, DC, 2048), dcpn(w1_d))
    wt("tensor", t_w1s)
    w("tensor", ve, ve_lnb)
    ps_guard("psA")
    ps_guard("psB")
    ps_guard("psC")
    ps_guard("psD")
    b1ps = [psA, psB, psC, psD]
    for c in range(DC):
        for nbi in range(4):
            mm(b1ps[nbi][0:1, :], lnb_bf[:, 3 * DC + c: 3 * DC + c + 1],
               W1B[:, c * 2048 + nbi * 512:(c * 2048 + (nbi + 1) * 512)], c == 0,
               c == DC - 1, inc=(c == DC - 1 and nbi == 3))
    t_b1mm = pe.n
    w("vector", pe, t_b1mm)
    wt("vector", t_b1)
    ve_b1 = 0
    for nbi in range(4):
        ve_b1 = vop("tensor_tensor", FRf[0:1, 14336 + nbi * 512: 14336 + (nbi + 1) * 512],
                    b1ps[nbi][0:1, :],
                    FRf[0:1, 9216 + nbi * 512: 9216 + (nbi + 1) * 512],
                    ALU.add, inc=(nbi == 3))
    for k in ("psA", "psB", "psC", "psD"):
        ps_free[k] = (ve, ve_b1)
    row_to_cols(FRf[0:1, 14336:16384], 2 * DC, 2 * DC, gate_ve=ve_b1)

    barrier()

    def stats_begin():
        ps_guard("psC")
        ps_guard("psD")

    st_sq = []

    def stats_feed(src_chunk, sq_chunk, t0, n, first, last, prod_ve=0, prod_dma=None):
        w("vector", sqb_free[0], sqb_free[1])
        if prod_dma is not None:
            wt("vector", prod_dma)
        if prod_ve:
            w("vector", ve, prod_ve)
        t_sq = vop("tensor_tensor", sq_chunk, src_chunk, src_chunk, ALU.mult, inc=True)
        st_sq.append((sq_chunk, t0, n, t_sq, first, last))
        if prod_dma is not None:
            wt("tensor", prod_dma)
        if prod_ve:
            w("tensor", ve, prod_ve)
        mm(psC[0:1, t0:t0 + n], ones_c_f[:, :], src_chunk, first, last, inc=last)

    def stats_rows(want_G):
        w("tensor", ve, st_sq[-1][3])
        for (sq_chunk, t0, n, t_sq, first, last) in st_sq:
            mm(psD[0:1, t0:t0 + n], ones_c_bf[:, :], sq_chunk, first, last, inc=last)
        st_sq.clear()
        t_sums = pe.n
        sqb_free[0], sqb_free[1] = pe, t_sums
        w("vector", pe, t_sums)
        vop("tensor_scalar", r_mu[:, :], psC[0:1, :], 1.0 / D, None, ALU.mult)
        vop("tensor_scalar", r_var[:, :], psD[0:1, :], 1.0 / D, None, ALU.mult)
        vchain()
        vop("tensor_tensor", r_sd[:, :], r_mu[:, :], r_mu[:, :], ALU.mult)
        vchain()
        t_var = vop("tensor_tensor", r_var[:, :], r_var[:, :], r_sd[:, :], ALU.subtract, inc=True)
        ps_free["psC"] = (ve, t_var)
        ps_free["psD"] = (ve, t_var)
        if want_G:
            vchain()
            vop("tensor_scalar", r_sd[:, :], r_var[:, :], 1.0 + EPS, None, ALU.mult)
            vchain()
            t_var = vop("tensor_scalar", r_var[:, :], r_sd[:, :], EPS * EPS, None,
                        ALU.add, inc=True)
        w("scalar", ve, t_var)
        w("scalar", pe, hil_free[0])
        sact(r_sd[:, :], r_var[:, :], AF.Ln,
             bias=0.0 if want_G else eps_c[0:1, :])
        w("scalar", sc, sc.n)
        t_apply_sc = sact(r_rsig[:, :], r_sd[:, :], AF.Exp, scale=-0.5, inc=True)
        return t_apply_sc

    def stats_bcast(t_apply_sc, psMu_name, psMu, psRs):
        w("vector", sc, t_apply_sc)
        ps_guard(psMu_name)
        bcast_hilo(r_mu[:, :], psMu, HIL0, 128)
        t_bc = bcast_hilo(r_rsig[:, :], psRs, HIL1, 128)
        return t_bc

    def stats_tail(want_G, psMu_name, psMu, psRs):
        return stats_bcast(stats_rows(want_G), psMu_name, psMu, psRs)

    last_oa_store = None
    px_tok = {}

    for wi in range(2):
        px_tok[(0, wi)] = dma("sync", c3(XW1[:, wi * 4096:(wi + 1) * 4096], DC, 512),
                              dcpn(xo_d[0])[:, :, wi * 512:(wi + 1) * 512])

    for bi in range(3):
        owned = bi < 2
        xsrc = xo_d[bi] if owned else xs_d
        mcol = mko_s if owned else mks_s
        qwins = list(range(NW)) if owned else [0]

        barrier()
        v_ones = vop("memset",
                     V_s.rearrange("p (t h c) -> p t h c", h=16, c=65)[:, :, :, 64:65],
                     1.0, inc=True)
        if "wv" in px_tok:
            t_wv = px_tok.pop("wv")
        else:
            t_wv = dma("sync", c3(WPS, DC, 1024), dcpn(zv_d))
        t_beff = dma("sync", BIAS[0:1, 0:1024], zbv_d)
        gidx = [0]

        def emit_V(wi):
            w("tensor", ve, v_ones)
            w("tensor", ve, xn_tok[wi % 2])
            wt("tensor", t_beff)
            wt("tensor", t_wv)
            for tt in range(4):
                tile = wi * 4 + tt
                for vb in range(2):
                    pname = "psA" if gidx[0] % 2 == 0 else "psB"
                    psP = PSN[pname]
                    ps_guard(pname)
                    for c in range(DC):
                        mm(psP[:, :],
                           XN[:, c * 2048 + tile * 128: c * 2048 + (tile + 1) * 128],
                           WPS[:, c * 1024 + vb * 512: c * 1024 + (vb + 1) * 512],
                           c == 0, False)
                    t_g = mm(psP[:, :], ones_r_bf[0:1, 0:128],
                             BIAS[0:1, vb * 512:(vb + 1) * 512], False, True, inc=True)
                    w("vector", pe, t_g)
                    dst = V_s.rearrange("p (t h c) -> p t h c", h=16, c=65)[
                        :, tile:tile + 1, 8 * vb:8 * (vb + 1), 0:64]
                    tk = vop("tensor_copy", dst,
                             psP.ap().rearrange("p (o h c) -> p o h c", o=1, c=64),
                             inc=True)
                    ps_free[pname] = (ve, tk)
                    gidx[0] += 1

        xn_tok = [0, 0]
        for wi in range(NW):
            par = wi % 2
            xw = XW1[:, par * 4096:(par + 1) * 4096]
            if (bi, wi) in px_tok:
                t_x = px_tok.pop((bi, wi))
            else:
                w("sync", ve, xn_tok[par])
                t_x = dma("sync", c3(xw, DC, 512), dcpn(xsrc)[:, :, wi * 512:(wi + 1) * 512])
            ps_guard("psS0")
            stats_begin()
            for c in range(DC):
                stats_feed(xw[:, c * 512:(c + 1) * 512], SQB[:, c * 512:(c + 1) * 512],
                           0, 512, c == 0, c == DC - 1, prod_dma=t_x)
            t_mid = stats_rows(False)
            if wi > 0:
                emit_V(wi - 1)
            t_bc = stats_bcast(t_mid, "psS0", psS0[:, 0:512], psS0[:, 512:1024])
            w("vector", pe, t_bc)
            tk = 0
            for c in range(DC):
                vop("tensor_tensor", xw[:, c * 512:(c + 1) * 512], xw[:, c * 512:(c + 1) * 512],
                    psS0[:, 0:512], ALU.subtract)
                vchain()
                tk = vop("tensor_tensor", XN[:, c * 2048 + wi * 512: c * 2048 + (wi + 1) * 512],
                         xw[:, c * 512:(c + 1) * 512], psS0[:, 512:1024], ALU.mult,
                         inc=(c == DC - 1))
            xn_tok[par] = tk
            ps_free["psS0"] = (ve, tk)
        emit_V(NW - 1)
        wp_free_pe = pe.n

        for pi, (zsrc, dest) in enumerate([(zq_d, Q_s), (zk_d, K_s)]):
            wins = [0] if (bi == 2 and pi == 0) else list(range(NW))
            w("sync", pe, wp_free_pe)
            t_w = dma("sync", c3(WPS, DC, 1024), dcpn(zsrc))
            wt("tensor", t_w)
            w("tensor", ve, max(xn_tok))
            for wi in wins:
                for oc in range(DC):
                    pname = "psA" if gidx[0] % 2 == 0 else "psB"
                    psP = PSN[pname]
                    ps_guard(pname)
                    for c in range(DC):
                        mm(psP[:, :], WPS[:, c * 1024 + oc * 128: c * 1024 + (oc + 1) * 128],
                           XN[:, c * 2048 + wi * 512: c * 2048 + (wi + 1) * 512], c == 0,
                           c == DC - 1, inc=(c == DC - 1))
                    t_g = pe.n
                    w("vector", pe, t_g)
                    tk = vop("tensor_scalar",
                             dest[:, oc * 2048 + wi * 512: oc * 2048 + (wi + 1) * 512],
                             psP[:, :], bcol[:, pi * DC + oc: pi * DC + oc + 1], None,
                             ALU.add, inc=True)
                    ps_free[pname] = (ve, tk)
                    gidx[0] += 1
            wp_free_pe = pe.n

        barrier()
        if DEBUG and bi == 0:
            dbg_q = dt("dbg_q", [128, 16384], BF16, kind="ExternalOutput").ap()
            dbg_k = dt("dbg_k", [128, 16384], BF16, kind="ExternalOutput").ap()
            dbg_v = dt("dbg_v", [128, 16640], BF16, kind="ExternalOutput").ap()
            dbg_bc = dt("dbg_bc", [128, 4 * DC], F32, kind="ExternalOutput").ap()
            dma("sync", dbg_q, Q_s[:, :])
            dma("sync", dbg_k, K_s[:, :])
            dma("sync", dbg_v, V_s[:, :])
            dma("sync", dbg_bc, bcol[:, :])
        w("gpsimd", pe, wp_free_pe)
        t_wo = dma("gpsimd", c3(WPS, DC, 1024), dcpn(wo_d))
        e_free = [0, 0]
        att_free = [ps_free["psA"], ps_free["psB"]]
        rb_read = [0, 0]
        shift_tok = None
        last_exp = [0, 0]
        pend = None
        for qb in qwins:
            odd_evac_tok = 0
            for hp in range(H // 2):
                oc = hp
                exp_toks = []
                for kt in range(16):
                    parS = kt % 2
                    psS = psS0 if parS == 0 else psS1
                    w("tensor", sc, last_exp[parS])
                    if qb == qwins[0] and hp == 0 and kt < 2:
                        ps_guard("psS0" if parS == 0 else "psS1")
                    mm(psS[:, 0:512],
                       K_s[0:64, oc * 2048 + kt * 128: oc * 2048 + (kt + 1) * 128],
                       Q_s[0:64, oc * 2048 + qb * 512: oc * 2048 + (qb + 1) * 512],
                       True, True)
                    mm(psS[:, 512:1024],
                       K_s[64:128, oc * 2048 + kt * 128: oc * 2048 + (kt + 1) * 128],
                       Q_s[64:128, oc * 2048 + qb * 512: oc * 2048 + (qb + 1) * 512],
                       True, True, inc=True)
                    s_tok = pe.n
                    mask_tok = 0
                    if qb == 0 and kt == 15:
                        w("vector", pe, s_tok)
                        wt("vector", t_mko if owned else t_mks)
                        vop("tensor_tensor", psS[:, 0:1], psS[:, 0:1],
                            mcol[:, :], ALU.add)
                        mask_tok = vop("tensor_tensor", psS[:, 512:513], psS[:, 512:513],
                                       mcol[:, :], ALU.add, inc=True)
                    w("scalar", pe, s_tok)
                    if mask_tok:
                        w("scalar", ve, mask_tok)
                    w("scalar", pe, e_free[kt // 8])
                    t_e = sact(E_BUF[kt // 8][:, (kt % 8) * 1024:(kt % 8 + 1) * 1024],
                               psS[:, :], AF.Exp, scale=1.0 / 32.0, inc=True)
                    exp_toks.append(t_e)
                    last_exp[parS] = t_e
                if pend is not None:
                    pend()
                    pend = None
                t_hilo = [0, 0]
                for h2 in range(2):
                    h = 2 * hp + h2
                    pnameA = "psA" if h2 == 0 else "psB"
                    psAtt = PSN[pnameA]
                    sA, tA = att_free[h2]
                    w("tensor", sA, tA)
                    for kt in range(16):
                        w("tensor", sc, exp_toks[kt])
                        inc = (kt == 15) or (h2 == 1 and kt == 7)
                        mm(psAtt[0:65, :],
                           V_s[:, kt * 1040 + h * 65: kt * 1040 + (h + 1) * 65],
                           E_BUF[kt // 8][:, (kt % 8) * 1024 + h2 * 512:
                                          (kt % 8) * 1024 + h2 * 512 + 512],
                           kt == 0, kt == 15, inc=inc)
                        if h2 == 1 and kt == 7:
                            e_free[0] = pe.n
                    pv_tok = pe.n
                    if h2 == 1:
                        e_free[1] = pv_tok
                    rrrow = r_G if h2 == 0 else r_rsig
                    hil = HIL0 if h2 == 0 else HIL1
                    w("vector", pe, pv_tok)
                    w("vector", pe, rb_read[h2])
                    vop("reciprocal", rrrow[0:1, :], psAtt[64:65, :])
                    vchain()
                    vop("tensor_copy", hil[0:1, 0:512], rrrow[0:1, :])
                    vchain()
                    t_hilo[h2] = vop("tensor_tensor", hil[0:1, 512:1024], rrrow[0:1, :],
                                     hil[0:1, 0:512], ALU.subtract, inc=True)

                def make_pend(hp=hp, oc=oc, t_hilo=t_hilo, qb=qb):
                    def pend_fn():
                        nonlocal odd_evac_tok, shift_tok
                        for h2 in range(2):
                            pnameA = "psA" if h2 == 0 else "psB"
                            psAtt = PSN[pnameA]
                            psbn = "psC" if h2 == 0 else "psD"
                            psb = PSN[psbn]
                            hil = HIL0 if h2 == 0 else HIL1
                            ps_guard(psbn)
                            w("tensor", ve, t_hilo[h2])
                            mm(psb[0:64, :], ones_r_bf[0:1, 0:64], hil[0:1, 0:512],
                               True, False)
                            tb = mm(psb[0:64, :], ones_r_bf[0:1, 0:64],
                                    hil[0:1, 512:1024], False, True, inc=True)
                            hil_free[0] = tb
                            rb_read[h2] = tb
                            w("vector", pe, tb)
                            vchain()
                            vop("tensor_copy", Rb_s[:, :], psb[0:64, :])
                            vchain()
                            if h2 == 0:
                                dst = ATT[0:64, oc * 2048 + qb * 512:
                                          oc * 2048 + (qb + 1) * 512]
                            else:
                                wt("vector", shift_tok)
                                dst = ATTO[0:64, oc * 512:(oc + 1) * 512]
                            tk = vop("tensor_tensor", dst, psAtt[0:64, :], Rb_s[:, :],
                                     ALU.mult, inc=True)
                            att_free[h2] = (ve, tk)
                            ps_free[pnameA] = (ve, tk)
                            ps_free[psbn] = (ve, tk)
                            if h2 == 1:
                                odd_evac_tok = tk
                    return pend_fn
                pend = make_pend()
            pend()
            pend = None
            w("sync", ve, odd_evac_tok)
            shift_tok = dma("sync",
                            c3(ATT[64:128, :], DC, 2048)[:, :, qb * 512:(qb + 1) * 512],
                            c3(ATTO[0:64, :], DC, 512))
        ps_free["psS0"] = (sc, last_exp[0])
        ps_free["psS1"] = (sc, last_exp[1])

        barrier()
        if DEBUG and bi == 0:
            dbg_att = dt("dbg_att", [128, 16384], BF16, kind="ExternalOutput").ap()
            dma("sync", dbg_att, ATT[:, :])
        t_w1 = dma("gpsimd", c3(W1B, DC, 2048), dcpn(w1_d))
        t_w2 = dma("gpsimd", c3(W2B, 16, 1024), dcpn(w2_d))
        x_tok = [0, 0]
        for qi, qb in enumerate(qwins):
            if PARANOID & 1:
                barrier()
            par = qi % 2
            xw = XW3[:, par * 4096:(par + 1) * 4096]
            scr = XW3[:, (1 - par) * 4096:(2 - par) * 4096]
            w("sync", ve, x_tok[par])
            t_x = dma("sync", c3(xw, DC, 512), dcpn(xsrc)[:, :, qb * 512:(qb + 1) * 512])
            wt("tensor", t_wo)
            h_tok = 0
            stats_begin()
            lagged = []
            for oc2 in range(DC):
                pname = "psA" if oc2 % 2 == 0 else "psB"
                psP = PSN[pname]
                ps_guard(pname)
                for c in range(DC):
                    mm(psP[:, :], WPS[:, c * 1024 + oc2 * 128: c * 1024 + (oc2 + 1) * 128],
                       ATT[:, c * 2048 + qb * 512: c * 2048 + (qb + 1) * 512], c == 0,
                       c == DC - 1, inc=(c == DC - 1))
                t_g = pe.n
                if lagged:
                    lagged.pop(0)()
                w("vector", pe, t_g)
                wt("vector", t_x)
                tk = vop("scalar_tensor_tensor", H_F[:, oc2 * 512:(oc2 + 1) * 512], psP[:, :],
                         bcol[:, 5 * DC + oc2: 5 * DC + oc2 + 1],
                         xw[:, oc2 * 512:(oc2 + 1) * 512], ALU.add, ALU.add, inc=True)
                ps_free[pname] = (ve, tk)
                h_tok = tk
                lagged.append(lambda oc2=oc2, tk=tk: stats_feed(
                    H_F[:, oc2 * 512:(oc2 + 1) * 512],
                    SQB[:, oc2 * 512:(oc2 + 1) * 512], 0, 512,
                    oc2 == 0, oc2 == DC - 1, prod_ve=tk))
            for f in lagged:
                f()
            if qi == len(qwins) - 1 and bi < 2:
                nx = xo_d[bi + 1] if bi < 1 else xs_d
                w("sync", pe, pe.n)
                for wj in range(2):
                    px_tok[(bi + 1, wj)] = dma(
                        "sync", c3(XW1[:, wj * 4096:(wj + 1) * 4096], DC, 512),
                        dcpn(nx)[:, :, wj * 512:(wj + 1) * 512])
                px_tok["wv"] = dma("sync", c3(WPS, DC, 1024), dcpn(zv_d))
            t_bc = stats_tail(False, "psS0", psS0[:, 0:512], psS0[:, 512:1024])
            w("vector", pe, t_bc)
            w("vector", ve, x_tok[1 - par])
            wt("vector", last_oa_store)
            xnh_tok = 0
            xnh_toks = []
            for c in range(DC):
                vop("tensor_tensor", scr[:, c * 512:(c + 1) * 512],
                    H_F[:, c * 512:(c + 1) * 512], psS0[:, 0:512], ALU.subtract)
                vchain()
                xnh_tok = vop("scalar_tensor_tensor", XNH[:, c * 512:(c + 1) * 512],
                              scr[:, c * 512:(c + 1) * 512],
                              lncols[:, 3 * DC + c: 3 * DC + c + 1],
                              psS0[:, 512:1024], ALU.mult, ALU.mult,
                              inc=(c == DC - 1))
                xnh_toks.append(ve.n)
            x_tok[1 - par] = xnh_tok
            ps_free["psS0"] = (ve, xnh_tok)
            wt("tensor", t_w1)
            wt("tensor", t_w2)
            pre_tok = 0
            hid_free_pe = 0
            stats_begin()
            lagged2 = []
            for th in range(2):
                gel_toks = []
                for ic in range(16):
                    pname = "psA" if ic % 2 == 0 else "psB"
                    psP = PSN[pname]
                    ps_guard(pname)
                    for c in range(DC):
                        w("tensor", ve, xnh_toks[c])
                        mm(psP[:, 0:256], W1B[:, c * 2048 + ic * 128: c * 2048 + (ic + 1) * 128],
                           XNH[:, c * 512 + th * 256: c * 512 + th * 256 + 256], c == 0,
                           c == DC - 1, inc=(c == DC - 1))
                    t_g = pe.n
                    w("scalar", pe, t_g)
                    w("scalar", pe, hid_free_pe)
                    wt("scalar", last_oa_store)
                    t_gel = sact(HID[:, ic * 256:(ic + 1) * 256], psP[:, 0:256],
                                 AF.Gelu_apprx_tanh,
                                 bias=bcol[:, 2 * DC + ic: 2 * DC + ic + 1], inc=True)
                    gel_toks.append(t_gel)
                    ps_free[pname] = (sc, t_gel)
                for oc2 in range(DC):
                    pname = "psA" if oc2 % 2 == 0 else "psB"
                    psP = PSN[pname]
                    ps_guard(pname)
                    w("tensor", sc, gel_toks[-1])
                    for ic in range(16):
                        mm(psP[:, 0:256],
                           W2B[:, ic * 1024 + oc2 * 128: ic * 1024 + (oc2 + 1) * 128],
                           HID[:, ic * 256:(ic + 1) * 256], ic == 0, ic == 15,
                           inc=(ic == 15))
                    t_g = pe.n
                    if lagged2:
                        lagged2.pop(0)()
                    w("vector", pe, t_g)
                    off = oc2 * 512 + th * 256
                    vop("scalar_tensor_tensor", H_F[:, off:off + 256], psP[:, 0:256],
                        bcol[:, 4 * DC + oc2: 4 * DC + oc2 + 1],
                        H_F[:, off:off + 256], ALU.add, ALU.add)
                    vchain()
                    tk = vop("tensor_tensor", H_F[:, off:off + 256], H_F[:, off:off + 256],
                             xw[:, off:off + 256], ALU.add, inc=True)
                    ps_free[pname] = (ve, tk)
                    pre_tok = tk
                    lagged2.append(lambda off=off, th=th, oc2=oc2, tk=tk: stats_feed(
                        H_F[:, off:off + 256], SQB[:, off:off + 256],
                        th * 256, 256, oc2 == 0, oc2 == DC - 1, prod_ve=tk))
                hid_free_pe = pe.n
                for f in lagged2:
                    f()
                lagged2 = []
            t_bc = stats_tail(owned, "psS1", psS1[:, 0:512], psS1[:, 512:1024])
            w("vector", pe, t_bc)
            wt("vector", last_oa_store)
            out_tok = 0
            for c in range(DC):
                vop("tensor_tensor", H_F[:, c * 512:(c + 1) * 512],
                    H_F[:, c * 512:(c + 1) * 512], psS1[:, 0:512], ALU.subtract)
                vchain()
                out_tok = vop("tensor_tensor", OUT_STG[:, c * 512:(c + 1) * 512],
                              H_F[:, c * 512:(c + 1) * 512], psS1[:, 512:1024], ALU.mult,
                              inc=(c == DC - 1))
            ps_free["psS1"] = (ve, out_tok)
            x_tok[par] = out_tok
            if owned:
                w("sync", ve, out_tok)
                last_oa_store = dma("sync",
                                    dcpn(oa_d[bi])[:, :, qb * 512:(qb + 1) * 512],
                                    c3(OUT_STG[:, 0:4096], DC, 512))
            else:
                vchain()
                for c in range(DC):
                    vop("tensor_copy", S1F[:, c * 512:(c + 1) * 512],
                        OUT_STG[:, c * 512:(c + 1) * 512], inc=(c == DC - 1))

    barrier()
    vchain()
    s1_tok = 0
    for c in range(DC):
        s1_tok = vop("tensor_copy", S1BF[:, c * 512:(c + 1) * 512],
                     S1F[:, c * 512:(c + 1) * 512], inc=(c == DC - 1))
    t_ew = dma("sync", FRf[0:1, 2048:3072], ew_d)
    wt("vector", t_ew)
    ewv = FRf[0:1, 2048:3072].rearrange("a (t k) -> a t k", k=2)
    t_ms = vop("tensor_tensor", r_G[0:1, :].rearrange("a (t k) -> a t k", k=1),
               ewv[:, :, 0:1], ewv[:, :, 1:2], ALU.add, inc=True)
    w("tensor", ve, t_ms)
    ps_guard("psC")
    t_bcm = bcast_hilo(r_G[0:1, :], psC[:, :], HIL0, 128)
    w("vector", pe, t_bcm)
    t_msb = vop("tensor_copy", AR4f[:, 4096:4608], psC[:, :], inc=True)
    ps_free["psC"] = (ve, t_msb)

    mw1_free = [0, 0]
    gel_tok = []
    for icb in range(8):
        par = icb % 2
        buf = MW1B[:, par * 8192:(par + 1) * 8192]
        w("gpsimd", pe, mw1_free[par])
        t_wb = dma("gpsimd", c3(buf, DC, 1024), dcpn(mw1_d[:, icb * 1024:(icb + 1) * 1024]))
        for icl in range(8):
            pname = "psA" if (icb * 8 + icl) % 2 == 0 else "psB"
            psP = PSN[pname]
            ps_guard(pname)
            wt("tensor", t_wb)
            w("tensor", ve, s1_tok)
            for c in range(DC):
                mm(psP[:, :], buf[:, c * 1024 + icl * 128: c * 1024 + (icl + 1) * 128],
                   S1BF[:, c * 512:(c + 1) * 512], c == 0, c == DC - 1,
                   inc=(c == DC - 1))
            t_g = pe.n
            w("scalar", pe, t_g)
            ic64 = icb * 8 + icl
            dstH = HID2[ic64 // 32][:, (ic64 % 32) * 512:(ic64 % 32 + 1) * 512]
            t_gel = sact(dstH, psP[:, :], AF.Gelu_apprx_tanh, inc=True)
            gel_tok.append(t_gel)
            ps_free[pname] = (sc, t_gel)
        mw1_free[par] = pe.n

    psY = [psS0.ap()[:, 0:512], psS0.ap()[:, 512:1024], psS1.ap()[:, 0:512],
           psS1.ap()[:, 512:1024], psA.ap(), psB.ap(), psC.ap(), psD.ap()]
    for k in PSN:
        ps_guard_eng = ps_free[k]
        w("tensor", ps_guard_eng[0], ps_guard_eng[1])
    mw2_free = [0, 0, 0, 0]
    w2toks = {}
    y2_tok = 0
    for ic64 in range(64):
        for pf in range(ic64 if ic64 else 0, min(ic64 + 4, 64)):
            if pf not in w2toks:
                par_pf = pf % 4
                w("gpsimd", pe, mw2_free[par_pf])
                w2toks[pf] = dma("gpsimd", MW2B[:, par_pf * 1024:(par_pf + 1) * 1024],
                                 mw2_d[pf * 128:(pf + 1) * 128, :])
        par = ic64 % 4
        wbuf = MW2B[:, par * 1024:(par + 1) * 1024]
        wt("tensor", w2toks.pop(ic64))
        w("tensor", sc, gel_tok[ic64])
        rhs = HID2[ic64 // 32][:, (ic64 % 32) * 512:(ic64 % 32 + 1) * 512]
        for oc in range(8):
            y2_tok = mm(psY[oc], wbuf[:, oc * 128:(oc + 1) * 128], rhs,
                        ic64 == 0, ic64 == 63, inc=(oc == 7))
        mw2_free[par] = y2_tok

    w("vector", pe, y2_tok)
    w("vector", ve, t_msb)
    z_tok = 0
    for oc in range(8):
        vop("tensor_tensor", psY[oc], psY[oc], AR4f[:, 4096:4608], ALU.mult)
        vchain()
        z_tok = vop("tensor_tensor", S1F[:, oc * 512:(oc + 1) * 512], psY[oc],
                    S1F[:, oc * 512:(oc + 1) * 512], ALU.add, inc=True)
    for k in PSN:
        ps_free[k] = (ve, z_tok)
    stats_begin()
    for c in range(DC):
        stats_feed(S1F[:, c * 512:(c + 1) * 512], SQB[:, c * 512:(c + 1) * 512],
                   0, 512, c == 0, c == DC - 1, prod_ve=z_tok)
    t_bc = stats_tail(False, "psS0", psS0[:, 0:512], psS0[:, 512:1024])
    w("vector", pe, t_bc)
    ob_tok = 0
    for c in range(DC):
        vop("tensor_tensor", S1F[:, c * 512:(c + 1) * 512],
            S1F[:, c * 512:(c + 1) * 512], psS0[:, 0:512], ALU.subtract)
        vchain()
        ob_tok = vop("tensor_tensor", OBST[:, c * 512:(c + 1) * 512],
                     S1F[:, c * 512:(c + 1) * 512], psS0[:, 512:1024], ALU.mult,
                     inc=(c == DC - 1))
    w("sync", ve, ob_tok)
    dma("sync", dcpn(ob_d), c3(OBST[:, 0:4096], DC, 512))

    for s in LD + LG:
        w("sync", s, s.n)

    with nc.Block() as block:
        @block.sync
        def _(E):
            for f in ops["sync"]:
                f(E)

        @block.gpsimd
        def _(E):
            for f in ops["gpsimd"]:
                f(E)

        @block.tensor
        def _(E):
            for f in ops["tensor"]:
                f(E)

        @block.vector
        def _(E):
            for f in ops["vector"]:
                f(E)

        @block.scalar
        def _(E):
            for f in ops["scalar"]:
                f(E)

    return nc


def _prep_host(inputs):
    f = lambda k: np.asarray(inputs[k], dtype=np.float32)
    xT = np.ascontiguousarray(f("L0_attn_x").transpose(0, 2, 1))
    shared = {
        "wq": np.ascontiguousarray(f("L0_attn_Wq")),
        "wk": np.ascontiguousarray(f("L0_attn_Wk")),
        "wv": np.ascontiguousarray(f("L0_attn_Wv")),
        "wo": np.ascontiguousarray(f("L0_attn_Wo")),
        "w1": np.ascontiguousarray(f("L0_attn_W1")),
        "w2": np.ascontiguousarray(f("L0_attn_W2")),
        "mw1": np.ascontiguousarray(f("L0_ffn_w1")),
        "mw2": np.ascontiguousarray(f("L0_ffn_w2")),
        "bq": f("L0_attn_bq").reshape(1, D),
        "bk": f("L0_attn_bk").reshape(1, D),
        "bv": f("L0_attn_bv").reshape(1, D),
        "bo": f("L0_attn_bo").reshape(1, D),
        "b1": f("L0_attn_b1").reshape(1, 2 * D),
        "b2": f("L0_attn_b2").reshape(1, D),
    }
    for nm, key in (("lnq_s", "L0_attn_lnq_s"), ("lnk_s", "L0_attn_lnk_s"),
                    ("lnv_s", "L0_attn_lnv_s"), ("lnm_s", "L0_attn_lnm_s"),
                    ("lnq_b", "L0_attn_lnq_b"), ("lnk_b", "L0_attn_lnk_b"),
                    ("lnv_b", "L0_attn_lnv_b"), ("lnm_b", "L0_attn_lnm_b")):
        shared[nm] = np.ascontiguousarray(f(key).reshape(DC, 128).T)
    mko = np.zeros((128, 1), np.float32)
    mko[127, 0] = -1e9
    shared["mko"] = mko
    ew_all = f("L0_ffn_expert_weights")
    in_maps = []
    for c in range(8):
        mb, sw = c // 4, c % 4
        m = dict(shared)
        m["xo"] = np.ascontiguousarray(xT[2 * c:2 * c + 2])
        m["xs"] = np.ascontiguousarray(np.roll(xT[mb], -512 * sw, axis=1))
        m["ew"] = np.ascontiguousarray(ew_all[512 * c:512 * (c + 1)].reshape(1, 1024))
        mks = np.zeros((128, 1), np.float32)
        if sw == 0:
            mks[127, 0] = -1e9
        m["mks"] = mks
        in_maps.append(m)
    return in_maps


def kernel(**inputs):
    if "nc" not in _CACHE:
        _CACHE["nc"] = build_program()
    nc = _CACHE["nc"]
    in_maps = _prep_host(inputs)
    res = run_bass_kernel_spmd(nc, in_maps, core_ids=list(range(8)))
    out = np.empty((16, S, D), np.float32)
    for c in range(8):
        oa = res.results[c]["oa"]
        if c > 0:
            out[2 * c] = oa[0].T
            out[2 * c + 1] = oa[1].T
    flat01 = out[:2].reshape(2 * S, D)
    for c in range(8):
        ob = res.results[c]["ob"]
        flat01[512 * c:512 * (c + 1)] = ob.T
    return out


# version 14
# speedup vs baseline: 1.0116x; 1.0116x over previous
import numpy as np
import concourse.bass as bass
import concourse.mybir as mybir
from concourse.bass_utils import run_bass_kernel_spmd

F32 = mybir.dt.float32
BF16 = mybir.dt.bfloat16
AF = mybir.ActivationFunctionType
ALU = mybir.AluOpType

D = 1024
DC = 8
S = 2048
H = 16
W = 512
NW = 4
EPS = 1e-5
ENGS = ("sync", "gpsimd", "tensor", "vector", "scalar")

_CACHE = {}
DEBUG = False
HILO = True
PARANOID = 0


class Sem:
    __slots__ = ("h", "n", "name")

    def __init__(self, h, name):
        self.h = h
        self.n = 0
        self.name = name


def build_program():
    nc = bass.Bass("TRN2", target_bir_lowering=False)
    ops = {e: [] for e in ENGS}
    seen = {e: {} for e in ENGS}

    def on(eng, fn):
        ops[eng].append(fn)

    def w(eng, sem, cnt=None):
        if cnt is None:
            cnt = sem.n
        if cnt <= 0:
            return
        if seen[eng].get(sem.name, 0) >= cnt:
            return
        seen[eng][sem.name] = cnt
        on(eng, lambda E, s=sem.h, c=cnt: E.wait_ge(s, c))

    dt = nc.dram_tensor
    xo_d = dt("xo", [2, D, S], F32, kind="ExternalInput").ap()
    xs_d = dt("xs", [D, S], F32, kind="ExternalInput").ap()
    wq_d = dt("wq", [D, D], F32, kind="ExternalInput").ap()
    wk_d = dt("wk", [D, D], F32, kind="ExternalInput").ap()
    wv_d = dt("wv", [D, D], F32, kind="ExternalInput").ap()
    wo_d = dt("wo", [D, D], F32, kind="ExternalInput").ap()
    w1_d = dt("w1", [D, 2 * D], F32, kind="ExternalInput").ap()
    w2_d = dt("w2", [2 * D, D], F32, kind="ExternalInput").ap()
    mw1_d = dt("mw1", [D, 8 * D], F32, kind="ExternalInput").ap()
    mw2_d = dt("mw2", [8 * D, D], F32, kind="ExternalInput").ap()
    ln_d = {}
    for nm in ("lnq_s", "lnk_s", "lnv_s", "lnm_s", "lnq_b", "lnk_b", "lnv_b", "lnm_b"):
        ln_d[nm] = dt(nm, [128, DC], F32, kind="ExternalInput").ap()
    bq_d = dt("bq", [1, D], F32, kind="ExternalInput").ap()
    bk_d = dt("bk", [1, D], F32, kind="ExternalInput").ap()
    bv_d = dt("bv", [1, D], F32, kind="ExternalInput").ap()
    bo_d = dt("bo", [1, D], F32, kind="ExternalInput").ap()
    b1_d = dt("b1", [1, 2 * D], F32, kind="ExternalInput").ap()
    b2_d = dt("b2", [1, D], F32, kind="ExternalInput").ap()
    ew_d = dt("ew", [1, 1024], F32, kind="ExternalInput").ap()
    mko_d = dt("mko", [128, 1], F32, kind="ExternalInput").ap()
    mks_d = dt("mks", [128, 1], F32, kind="ExternalInput").ap()

    oa_d = dt("oa", [2, D, S], F32, kind="ExternalOutput").ap()
    ob_d = dt("ob", [D, W], F32, kind="ExternalOutput").ap()

    zq_d = dt("zq", [D, D], BF16, kind="Internal").ap()
    zk_d = dt("zk", [D, D], BF16, kind="Internal").ap()
    zv_d = dt("zv", [D, D], BF16, kind="Internal").ap()
    zbv_d = dt("zbv", [1, D], BF16, kind="Internal").ap()

    A = nc.alloc_sbuf_tensor
    AR1 = A("AR1", [128, 16384], BF16)
    AR2 = A("AR2", [128, 16384], BF16)
    AR3 = A("AR3", [128, 16640], BF16)
    AR4 = A("AR4", [128, 16384], BF16)
    FR = A("FR", [128, 32768], BF16)
    ones_r_bf = A("ones_r_bf", [1, 512], BF16)
    ones_c_bf = A("ones_c_bf", [128, 1], BF16)
    ones_r_f = A("ones_r_f", [1, 128], F32)
    ones_c_f = A("ones_c_f", [128, 1], F32)
    eps_c = A("eps_c", [128, 1], F32)
    mko_s = A("mko_s", [128, 1], F32)
    mks_s = A("mks_s", [128, 1], F32)
    lncols = A("lncols", [128, 8 * DC], F32)
    lnb_bf = A("lnb_bf", [128, 4 * DC], BF16)
    r_mu = A("r_mu", [1, 512], F32)
    r_rsig = A("r_rsig", [1, 512], F32)
    r_G = A("r_G", [1, 512], F32)
    r_sd = A("r_sd", [1, 512], F32)
    r_var = A("r_var", [1, 512], F32)
    Rb_s = A("Rb_s", [64, 512], F32)
    bcol = A("bcol", [128, 6 * DC], F32)

    PSAL = nc.alloc_psum_tensor
    psS0 = PSAL("psS0", [128, 1024], F32)
    psS1 = PSAL("psS1", [128, 1024], F32)
    psA = PSAL("psA", [128, 512], F32)
    psB = PSAL("psB", [128, 512], F32)
    psC = PSAL("psC", [128, 512], F32)
    psD = PSAL("psD", [128, 512], F32)
    PSN = {"psS0": psS0, "psS1": psS1, "psA": psA, "psB": psB, "psC": psC, "psD": psD}

    AR1f = AR1.ap().bitcast(F32)
    AR2f = AR2.ap().bitcast(F32)
    AR3f = AR3.ap().bitcast(F32)
    AR4f = AR4.ap().bitcast(F32)
    FRf = FR.ap().bitcast(F32)

    pe = Sem(nc.alloc_semaphore(name="pe"), "pe")
    ve = Sem(nc.alloc_semaphore(name="ve"), "ve")
    sc = Sem(nc.alloc_semaphore(name="sc"), "sc")
    LD = [Sem(nc.alloc_semaphore(name=f"ld{i}"), f"ld{i}") for i in range(6)]
    LG = [Sem(nc.alloc_semaphore(name=f"lg{i}"), f"lg{i}") for i in range(4)]
    QST = {"sync": [LD, 0], "gpsimd": [LG, 0]}
    ALLSEMS = (pe, ve, sc) + tuple(LD) + tuple(LG)

    def dma(eng, out_ap, in_ap, slow=False):
        q = QST[eng]
        s = q[0][q[1] % len(q[0])]
        q[1] += 1
        w(eng, s, s.n)
        on(eng, lambda E, o=out_ap, i=in_ap, sh=s.h, sl=slow:
           E.dma_start(out=o, in_=i, allow_slow_non_contiguous=sl).then_inc(sh, 16))
        s.n += 16
        return (s, s.n)

    def wt(eng, tok):
        if tok is not None:
            w(eng, tok[0], tok[1])

    def mm(out_ap, lhsT, rhs, start, stop, inc=False):
        if inc:
            on("tensor", lambda E, o=out_ap, l=lhsT, r=rhs, s1=start, s2=stop:
               E.matmul(o, l, r, start=s1, stop=s2).then_inc(pe.h, 1))
            pe.n += 1
            return pe.n
        on("tensor", lambda E, o=out_ap, l=lhsT, r=rhs, s1=start, s2=stop:
           E.matmul(o, l, r, start=s1, stop=s2))
        return pe.n

    def vop(name, *args, inc=True):
        on("vector", lambda E, n=name, a=args: getattr(E, n)(*a).then_inc(ve.h, 1))
        ve.n += 1
        return ve.n

    def vchain():
        w("vector", ve, ve.n)

    def sact(out_ap, in_ap, func, bias=0.0, scale=1.0, inc=True):
        on("scalar", lambda E, o=out_ap, i=in_ap, f=func, bi=bias, s=scale:
           E.activation(o, i, f, bias=bi, scale=s).then_inc(sc.h, 1))
        sc.n += 1
        return sc.n

    def barrier(engs=ENGS):
        for e in engs:
            for s in ALLSEMS:
                w(e, s, s.n)

    def c3(ap2d, c, n):
        return ap2d.rearrange("p (c n) -> p c n", c=c)

    def dcpn(ap2d):
        return ap2d.rearrange("(c p) n -> p c n", p=128)

    XN = FR.ap()[:, 0:16384]
    E_BUF = [FR.ap()[:, 0:8192], FR.ap()[:, 8192:16384]]
    W1B = FR.ap()[:, 0:16384]
    WPS = FR.ap()[:, 16384:24576]
    SQB = FR.ap()[:, 24576:28672]
    ATTO = FR.ap()[:, 24576:28672]
    BIAS = FR.ap()[:, 28672:32768]
    S1BF = FR.ap()[:, 0:4096]
    MW2B = FR.ap()[:, 16384:20480]

    K_s = AR1.ap()
    Q_s = AR2.ap()
    V_s = AR3.ap()
    ATT = AR4.ap()
    W2B = AR1.ap()
    XW3 = AR2f
    H_F = AR3f
    XNH = AR3.ap()[:, 8192:12288]
    HID = AR3.ap()[:, 12288:16384]
    XW1 = AR4f
    OUT_STG = AR3f[:, 4096:8192]
    S1F = AR4f
    MW1B = AR3.ap()
    HID2 = [AR1.ap(), AR2.ap()]
    OBST = AR2f

    ps_free = {k: (ve, 0) for k in PSN}

    def ps_guard(name):
        s, t = ps_free[name]
        w("tensor", s, t)

    sqb_free = [pe, 0]
    hil_free = [0]

    HIL0 = r_sd.ap().bitcast(BF16)
    HIL1 = r_var.ap().bitcast(BF16)

    def bcast_hilo(row_f32, ps_out, hil, m):
        if not HILO:
            t = vop("memset", hil[0:1, 0:1], 0.0, inc=True)
            w("tensor", ve, t)
            return mm(ps_out, ones_r_f[0:1, 0:m], row_f32, True, True, inc=True)
        vchain()
        w("vector", pe, hil_free[0])
        vop("tensor_copy", hil[0:1, 0:512], row_f32)
        vchain()
        t = vop("tensor_tensor", hil[0:1, 512:1024], row_f32, hil[0:1, 0:512],
                ALU.subtract, inc=True)
        w("tensor", ve, t)
        mm(ps_out, ones_r_bf[0:1, 0:m], hil[0:1, 0:512], True, False)
        t_bc = mm(ps_out, ones_r_bf[0:1, 0:m], hil[0:1, 512:1024], False, True, inc=True)
        hil_free[0] = t_bc
        return t_bc

    vop("memset", ones_r_bf[:, :], 1.0)
    vop("memset", ones_c_bf[:, :], 1.0)
    vop("memset", ones_r_f[:, :], 1.0)
    vop("memset", ones_c_f[:, :], 1.0)
    ve_const = vop("memset", eps_c[:, :], EPS, inc=True)
    for e in ("tensor", "scalar", "sync", "gpsimd"):
        w(e, ve, ve_const)

    ln_toks = []
    for i, nm in enumerate(("lnq_s", "lnk_s", "lnv_s", "lnm_s", "lnq_b", "lnk_b", "lnv_b", "lnm_b")):
        ln_toks.append(dma("sync", lncols[:, i * DC:(i + 1) * DC], ln_d[nm]))
    t_mko = dma("sync", mko_s[:, :], mko_d)
    t_mks = dma("sync", mks_s[:, :], mks_d)
    t_bq = dma("sync", FRf[0:1, 6144:7168], bq_d)
    t_bk = dma("sync", FRf[0:1, 7168:8192], bk_d)
    t_bv = dma("sync", FRf[0:1, 8192:9216], bv_d)
    t_b1 = dma("sync", FRf[0:1, 9216:11264], b1_d)
    t_b2 = dma("sync", FRf[0:1, 11264:12288], b2_d)
    t_bo = dma("sync", FRf[0:1, 12288:13312], bo_d)
    for t in ln_toks[4:8]:
        wt("vector", t)
    ve_lnb = vop("tensor_copy", lnb_bf[:, :], lncols[:, 4 * DC:8 * DC], inc=True)
    def row_to_cols(row_f32, bcol_off, n, gate_dma=None, gate_ve=0):
        ps_guard("psD")
        wt("tensor", gate_dma)
        w("tensor", ve, gate_ve)
        for c in range(n):
            mm(psD[:, c:c + 1], row_f32[0:1, c * 128:(c + 1) * 128],
               ones_c_f[0:1, 0:1], True, True, inc=(c == n - 1))
        t = pe.n
        w("vector", pe, t)
        tk = vop("tensor_copy", bcol[:, bcol_off:bcol_off + n], psD[:, 0:n], inc=True)
        ps_free["psD"] = (ve, tk)
        return t

    row_to_cols(FRf[0:1, 11264:12288], 4 * DC, DC, gate_dma=t_b2)
    row_to_cols(FRf[0:1, 12288:13312], 5 * DC, DC, gate_dma=t_bo)

    prep = [
        (wq_d, zq_d, 0, 6144, D, 0, t_bq),
        (wk_d, zk_d, 1, 7168, D, DC, t_bk),
        (wv_d, zv_d, 2, 8192, D, None, t_bv),
    ]
    beff_ps = [psA, psB, psC, psD]
    frrow_dma = None
    frrow_pe = 0
    scale_tok = [0, 0]
    tmp_tok = [0, 0]
    store_tok = [None, None]
    for (src, zdst, gi, brow_off, dout, beff_dst, t_brow) in prep:
        nb = dout // 512
        mm_tok = 0
        for c in range(DC):
            par = c % 2
            stg = AR1f[:, par * 2048: par * 2048 + dout]
            tmp = AR2.ap()[:, par * 2048: par * 2048 + dout]
            sbf = AR2.ap()[:, 4096 + par * 2048: 4096 + par * 2048 + dout]
            w("sync", ve, scale_tok[par])
            w("sync", pe, mm_tok)
            t_ld = dma("sync", c3(stg, nb, 512), c3(src[c * 128:(c + 1) * 128, :], nb, 512))
            wt("vector", t_ld)
            wt("vector", ln_toks[gi])
            wt("vector", store_tok[par])
            w("vector", ve, ve_lnb)
            w("vector", pe, tmp_tok[par])
            ve_tmp = vop("tensor_copy", tmp, stg, inc=True)
            sc_col = lncols[:, gi * DC + c: gi * DC + c + 1]
            ve_scl = vop("tensor_scalar", sbf, stg, sc_col, None, ALU.mult, inc=True)
            scale_tok[par] = ve_scl
            w("tensor", ve, ve_tmp)
            for nbi in range(nb):
                mm_tok = mm(beff_ps[nbi][0:1, :], lnb_bf[:, gi * DC + c: gi * DC + c + 1],
                            tmp[:, nbi * 512:(nbi + 1) * 512], c == 0, c == DC - 1,
                            inc=(nbi == nb - 1))
            tmp_tok[par] = mm_tok
            w("sync", ve, ve_scl)
            store_tok[par] = dma("sync", c3(zdst[c * 128:(c + 1) * 128, :], nb, 512),
                                 c3(sbf, nb, 512))
        w("vector", pe, mm_tok)
        wt("vector", t_brow)
        wt("vector", frrow_dma)
        w("vector", pe, frrow_pe)
        ve_beff = 0
        for nbi in range(nb):
            ve_beff = vop("tensor_tensor", FRf[0:1, nbi * 512:(nbi + 1) * 512],
                          beff_ps[nbi][0:1, :],
                          FRf[0:1, brow_off + nbi * 512: brow_off + (nbi + 1) * 512],
                          ALU.add, inc=(nbi == nb - 1))
        for k in ("psA", "psB", "psC", "psD")[:nb]:
            ps_free[k] = (ve, ve_beff)
        if beff_dst is None:
            w("gpsimd", ve, ve_beff)
            frrow_dma = dma("gpsimd", zbv_d, FRf[0:1, 0:dout])
        else:
            frrow_pe = row_to_cols(FRf[0:1, 0:dout], beff_dst, dout // 128,
                                   gate_ve=ve_beff)

    wt("gpsimd", frrow_dma)
    t_w1s = dma("gpsimd", c3(W1B, DC, 2048), dcpn(w1_d))
    wt("tensor", t_w1s)
    w("tensor", ve, ve_lnb)
    ps_guard("psA")
    ps_guard("psB")
    ps_guard("psC")
    ps_guard("psD")
    b1ps = [psA, psB, psC, psD]
    for c in range(DC):
        for nbi in range(4):
            mm(b1ps[nbi][0:1, :], lnb_bf[:, 3 * DC + c: 3 * DC + c + 1],
               W1B[:, c * 2048 + nbi * 512:(c * 2048 + (nbi + 1) * 512)], c == 0,
               c == DC - 1, inc=(c == DC - 1 and nbi == 3))
    t_b1mm = pe.n
    w("vector", pe, t_b1mm)
    wt("vector", t_b1)
    ve_b1 = 0
    for nbi in range(4):
        ve_b1 = vop("tensor_tensor", FRf[0:1, 14336 + nbi * 512: 14336 + (nbi + 1) * 512],
                    b1ps[nbi][0:1, :],
                    FRf[0:1, 9216 + nbi * 512: 9216 + (nbi + 1) * 512],
                    ALU.add, inc=(nbi == 3))
    for k in ("psA", "psB", "psC", "psD"):
        ps_free[k] = (ve, ve_b1)
    row_to_cols(FRf[0:1, 14336:16384], 2 * DC, 2 * DC, gate_ve=ve_b1)

    barrier()

    def stats_begin():
        ps_guard("psC")
        ps_guard("psD")

    st_sq = []

    def stats_feed(src_chunk, sq_chunk, t0, n, first, last, prod_ve=0, prod_dma=None):
        w("vector", sqb_free[0], sqb_free[1])
        if prod_dma is not None:
            wt("vector", prod_dma)
        if prod_ve:
            w("vector", ve, prod_ve)
        t_sq = vop("tensor_tensor", sq_chunk, src_chunk, src_chunk, ALU.mult, inc=True)
        st_sq.append((sq_chunk, t0, n, t_sq, first, last))
        if prod_dma is not None:
            wt("tensor", prod_dma)
        if prod_ve:
            w("tensor", ve, prod_ve)
        mm(psC[0:1, t0:t0 + n], ones_c_f[:, :], src_chunk, first, last, inc=last)

    def stats_rows(want_G):
        w("tensor", ve, st_sq[-1][3])
        for (sq_chunk, t0, n, t_sq, first, last) in st_sq:
            mm(psD[0:1, t0:t0 + n], ones_c_bf[:, :], sq_chunk, first, last, inc=last)
        st_sq.clear()
        t_sums = pe.n
        sqb_free[0], sqb_free[1] = pe, t_sums
        w("vector", pe, t_sums)
        vop("tensor_scalar", r_mu[:, :], psC[0:1, :], 1.0 / D, None, ALU.mult)
        vop("tensor_scalar", r_var[:, :], psD[0:1, :], 1.0 / D, None, ALU.mult)
        vchain()
        vop("tensor_tensor", r_sd[:, :], r_mu[:, :], r_mu[:, :], ALU.mult)
        vchain()
        t_var = vop("tensor_tensor", r_var[:, :], r_var[:, :], r_sd[:, :], ALU.subtract, inc=True)
        ps_free["psC"] = (ve, t_var)
        ps_free["psD"] = (ve, t_var)
        if want_G:
            vchain()
            vop("tensor_scalar", r_sd[:, :], r_var[:, :], 1.0 + EPS, None, ALU.mult)
            vchain()
            t_var = vop("tensor_scalar", r_var[:, :], r_sd[:, :], EPS * EPS, None,
                        ALU.add, inc=True)
        w("scalar", ve, t_var)
        w("scalar", pe, hil_free[0])
        sact(r_sd[:, :], r_var[:, :], AF.Ln,
             bias=0.0 if want_G else eps_c[0:1, :])
        w("scalar", sc, sc.n)
        t_apply_sc = sact(r_rsig[:, :], r_sd[:, :], AF.Exp, scale=-0.5, inc=True)
        return t_apply_sc

    def stats_bcast(t_apply_sc, psMu_name, psMu, psRs):
        w("vector", sc, t_apply_sc)
        ps_guard(psMu_name)
        bcast_hilo(r_mu[:, :], psMu, HIL0, 128)
        t_bc = bcast_hilo(r_rsig[:, :], psRs, HIL1, 128)
        return t_bc

    def stats_tail(want_G, psMu_name, psMu, psRs):
        return stats_bcast(stats_rows(want_G), psMu_name, psMu, psRs)

    last_oa_store = None
    px_tok = {}

    for wi in range(2):
        px_tok[(0, wi)] = dma("sync", c3(XW1[:, wi * 4096:(wi + 1) * 4096], DC, 512),
                              dcpn(xo_d[0])[:, :, wi * 512:(wi + 1) * 512])

    for bi in range(3):
        owned = bi < 2
        xsrc = xo_d[bi] if owned else xs_d
        mcol = mko_s if owned else mks_s
        qwins = list(range(NW)) if owned else [0]

        barrier()
        v_ones = vop("memset",
                     V_s.rearrange("p (t h c) -> p t h c", h=16, c=65)[:, :, :, 64:65],
                     1.0, inc=True)
        if "wv" in px_tok:
            t_wv = px_tok.pop("wv")
        else:
            t_wv = dma("sync", c3(WPS, DC, 1024), dcpn(zv_d))
        t_beff = dma("sync", BIAS[0:1, 0:1024], zbv_d)
        gidx = [0]

        def emit_V(wi):
            w("tensor", ve, v_ones)
            w("tensor", ve, xn_tok[wi % 2])
            wt("tensor", t_beff)
            wt("tensor", t_wv)
            for tt in range(4):
                tile = wi * 4 + tt
                for vb in range(2):
                    pname = "psA" if gidx[0] % 2 == 0 else "psB"
                    psP = PSN[pname]
                    ps_guard(pname)
                    for c in range(DC):
                        mm(psP[:, :],
                           XN[:, c * 2048 + tile * 128: c * 2048 + (tile + 1) * 128],
                           WPS[:, c * 1024 + vb * 512: c * 1024 + (vb + 1) * 512],
                           c == 0, False)
                    t_g = mm(psP[:, :], ones_r_bf[0:1, 0:128],
                             BIAS[0:1, vb * 512:(vb + 1) * 512], False, True, inc=True)
                    w("vector", pe, t_g)
                    dst = V_s.rearrange("p (t h c) -> p t h c", h=16, c=65)[
                        :, tile:tile + 1, 8 * vb:8 * (vb + 1), 0:64]
                    tk = vop("tensor_copy", dst,
                             psP.ap().rearrange("p (o h c) -> p o h c", o=1, c=64),
                             inc=True)
                    ps_free[pname] = (ve, tk)
                    gidx[0] += 1

        xn_tok = [0, 0]
        for wi in range(NW):
            par = wi % 2
            xw = XW1[:, par * 4096:(par + 1) * 4096]
            if (bi, wi) in px_tok:
                t_x = px_tok.pop((bi, wi))
            else:
                w("sync", ve, xn_tok[par])
                t_x = dma("sync", c3(xw, DC, 512), dcpn(xsrc)[:, :, wi * 512:(wi + 1) * 512])
            ps_guard("psS0")
            stats_begin()
            for c in range(DC):
                stats_feed(xw[:, c * 512:(c + 1) * 512], SQB[:, c * 512:(c + 1) * 512],
                           0, 512, c == 0, c == DC - 1, prod_dma=t_x)
            t_mid = stats_rows(False)
            if wi > 0:
                emit_V(wi - 1)
            t_bc = stats_bcast(t_mid, "psS0", psS0[:, 0:512], psS0[:, 512:1024])
            w("vector", pe, t_bc)
            tk = 0
            for c in range(DC):
                vop("tensor_tensor", xw[:, c * 512:(c + 1) * 512], xw[:, c * 512:(c + 1) * 512],
                    psS0[:, 0:512], ALU.subtract)
                vchain()
                tk = vop("tensor_tensor", XN[:, c * 2048 + wi * 512: c * 2048 + (wi + 1) * 512],
                         xw[:, c * 512:(c + 1) * 512], psS0[:, 512:1024], ALU.mult,
                         inc=(c == DC - 1))
            xn_tok[par] = tk
            ps_free["psS0"] = (ve, tk)
        emit_V(NW - 1)
        wp_free_pe = pe.n

        for pi, (zsrc, dest) in enumerate([(zq_d, Q_s), (zk_d, K_s)]):
            wins = [0] if (bi == 2 and pi == 0) else list(range(NW))
            w("sync", pe, wp_free_pe)
            t_w = dma("sync", c3(WPS, DC, 1024), dcpn(zsrc))
            wt("tensor", t_w)
            w("tensor", ve, max(xn_tok))
            for wi in wins:
                for oc in range(DC):
                    pname = "psA" if gidx[0] % 2 == 0 else "psB"
                    psP = PSN[pname]
                    ps_guard(pname)
                    for c in range(DC):
                        mm(psP[:, :], WPS[:, c * 1024 + oc * 128: c * 1024 + (oc + 1) * 128],
                           XN[:, c * 2048 + wi * 512: c * 2048 + (wi + 1) * 512], c == 0,
                           c == DC - 1, inc=(c == DC - 1))
                    t_g = pe.n
                    w("vector", pe, t_g)
                    tk = vop("tensor_scalar",
                             dest[:, oc * 2048 + wi * 512: oc * 2048 + (wi + 1) * 512],
                             psP[:, :], bcol[:, pi * DC + oc: pi * DC + oc + 1], None,
                             ALU.add, inc=True)
                    ps_free[pname] = (ve, tk)
                    gidx[0] += 1
            wp_free_pe = pe.n

        barrier()
        if DEBUG and bi == 0:
            dbg_q = dt("dbg_q", [128, 16384], BF16, kind="ExternalOutput").ap()
            dbg_k = dt("dbg_k", [128, 16384], BF16, kind="ExternalOutput").ap()
            dbg_v = dt("dbg_v", [128, 16640], BF16, kind="ExternalOutput").ap()
            dbg_bc = dt("dbg_bc", [128, 4 * DC], F32, kind="ExternalOutput").ap()
            dma("sync", dbg_q, Q_s[:, :])
            dma("sync", dbg_k, K_s[:, :])
            dma("sync", dbg_v, V_s[:, :])
            dma("sync", dbg_bc, bcol[:, :])
        w("gpsimd", pe, wp_free_pe)
        t_wo = dma("gpsimd", c3(WPS, DC, 1024), dcpn(wo_d))
        e_free = [0, 0]
        att_free = [ps_free["psA"], ps_free["psB"]]
        rb_read = [0, 0]
        shift_tok = None
        last_exp = [0, 0]
        pend = None
        for qb in qwins:
            odd_evac_tok = 0
            for hp in range(H // 2):
                oc = hp
                exp_toks = []
                for kt in range(16):
                    parS = kt % 2
                    psS = psS0 if parS == 0 else psS1
                    w("tensor", sc, last_exp[parS])
                    if qb == qwins[0] and hp == 0 and kt < 2:
                        ps_guard("psS0" if parS == 0 else "psS1")
                    mm(psS[:, 0:512],
                       K_s[0:64, oc * 2048 + kt * 128: oc * 2048 + (kt + 1) * 128],
                       Q_s[0:64, oc * 2048 + qb * 512: oc * 2048 + (qb + 1) * 512],
                       True, True)
                    mm(psS[:, 512:1024],
                       K_s[64:128, oc * 2048 + kt * 128: oc * 2048 + (kt + 1) * 128],
                       Q_s[64:128, oc * 2048 + qb * 512: oc * 2048 + (qb + 1) * 512],
                       True, True, inc=True)
                    s_tok = pe.n
                    mask_tok = 0
                    if qb == 0 and kt == 15:
                        w("vector", pe, s_tok)
                        wt("vector", t_mko if owned else t_mks)
                        vop("tensor_tensor", psS[:, 0:1], psS[:, 0:1],
                            mcol[:, :], ALU.add)
                        mask_tok = vop("tensor_tensor", psS[:, 512:513], psS[:, 512:513],
                                       mcol[:, :], ALU.add, inc=True)
                    w("scalar", pe, s_tok)
                    if mask_tok:
                        w("scalar", ve, mask_tok)
                    w("scalar", pe, e_free[kt // 8])
                    t_e = sact(E_BUF[kt // 8][:, (kt % 8) * 1024:(kt % 8 + 1) * 1024],
                               psS[:, :], AF.Exp, scale=1.0 / 32.0, inc=True)
                    exp_toks.append(t_e)
                    last_exp[parS] = t_e
                if pend is not None:
                    pend()
                    pend = None
                t_hilo = [0, 0]
                for h2 in range(2):
                    h = 2 * hp + h2
                    pnameA = "psA" if h2 == 0 else "psB"
                    psAtt = PSN[pnameA]
                    sA, tA = att_free[h2]
                    w("tensor", sA, tA)
                    for kt in range(16):
                        w("tensor", sc, exp_toks[kt])
                        inc = (kt == 15) or (h2 == 1 and kt == 7)
                        mm(psAtt[0:65, :],
                           V_s[:, kt * 1040 + h * 65: kt * 1040 + (h + 1) * 65],
                           E_BUF[kt // 8][:, (kt % 8) * 1024 + h2 * 512:
                                          (kt % 8) * 1024 + h2 * 512 + 512],
                           kt == 0, kt == 15, inc=inc)
                        if h2 == 1 and kt == 7:
                            e_free[0] = pe.n
                    pv_tok = pe.n
                    if h2 == 1:
                        e_free[1] = pv_tok
                    rrrow = r_G if h2 == 0 else r_rsig
                    hil = HIL0 if h2 == 0 else HIL1
                    w("vector", pe, pv_tok)
                    w("vector", pe, rb_read[h2])
                    vop("reciprocal", rrrow[0:1, :], psAtt[64:65, :])
                    vchain()
                    vop("tensor_copy", hil[0:1, 0:512], rrrow[0:1, :])
                    vchain()
                    t_hilo[h2] = vop("tensor_tensor", hil[0:1, 512:1024], rrrow[0:1, :],
                                     hil[0:1, 0:512], ALU.subtract, inc=True)

                def make_pend(hp=hp, oc=oc, t_hilo=t_hilo, qb=qb):
                    def pend_fn():
                        nonlocal odd_evac_tok, shift_tok
                        for h2 in range(2):
                            pnameA = "psA" if h2 == 0 else "psB"
                            psAtt = PSN[pnameA]
                            psbn = "psC" if h2 == 0 else "psD"
                            psb = PSN[psbn]
                            hil = HIL0 if h2 == 0 else HIL1
                            ps_guard(psbn)
                            w("tensor", ve, t_hilo[h2])
                            mm(psb[0:64, :], ones_r_bf[0:1, 0:64], hil[0:1, 0:512],
                               True, False)
                            tb = mm(psb[0:64, :], ones_r_bf[0:1, 0:64],
                                    hil[0:1, 512:1024], False, True, inc=True)
                            hil_free[0] = tb
                            rb_read[h2] = tb
                            w("vector", pe, tb)
                            vchain()
                            vop("tensor_copy", Rb_s[:, :], psb[0:64, :])
                            vchain()
                            if h2 == 0:
                                dst = ATT[0:64, oc * 2048 + qb * 512:
                                          oc * 2048 + (qb + 1) * 512]
                            else:
                                wt("vector", shift_tok)
                                dst = ATTO[0:64, oc * 512:(oc + 1) * 512]
                            tk = vop("tensor_tensor", dst, psAtt[0:64, :], Rb_s[:, :],
                                     ALU.mult, inc=True)
                            att_free[h2] = (ve, tk)
                            ps_free[pnameA] = (ve, tk)
                            ps_free[psbn] = (ve, tk)
                            if h2 == 1:
                                odd_evac_tok = tk
                    return pend_fn
                pend = make_pend()
            pend()
            pend = None
            w("sync", ve, odd_evac_tok)
            shift_tok = dma("sync",
                            c3(ATT[64:128, :], DC, 2048)[:, :, qb * 512:(qb + 1) * 512],
                            c3(ATTO[0:64, :], DC, 512))
        ps_free["psS0"] = (sc, last_exp[0])
        ps_free["psS1"] = (sc, last_exp[1])

        barrier()
        if DEBUG and bi == 0:
            dbg_att = dt("dbg_att", [128, 16384], BF16, kind="ExternalOutput").ap()
            dma("sync", dbg_att, ATT[:, :])
        t_w1 = dma("gpsimd", c3(W1B, DC, 2048), dcpn(w1_d))
        t_w2 = dma("gpsimd", c3(W2B, 16, 1024), dcpn(w2_d))
        x_tok = [0, 0]
        for qi, qb in enumerate(qwins):
            if PARANOID & 1:
                barrier()
            par = qi % 2
            xw = XW3[:, par * 4096:(par + 1) * 4096]
            scr = XW3[:, (1 - par) * 4096:(2 - par) * 4096]
            w("sync", ve, x_tok[par])
            t_x = dma("sync", c3(xw, DC, 512), dcpn(xsrc)[:, :, qb * 512:(qb + 1) * 512])
            wt("tensor", t_wo)
            h_tok = 0
            stats_begin()
            lagged = []
            for oc2 in range(DC):
                pname = "psA" if oc2 % 2 == 0 else "psB"
                psP = PSN[pname]
                ps_guard(pname)
                for c in range(DC):
                    mm(psP[:, :], WPS[:, c * 1024 + oc2 * 128: c * 1024 + (oc2 + 1) * 128],
                       ATT[:, c * 2048 + qb * 512: c * 2048 + (qb + 1) * 512], c == 0,
                       c == DC - 1, inc=(c == DC - 1))
                t_g = pe.n
                if len(lagged) >= 4:
                    lagged.pop(0)()
                w("vector", pe, t_g)
                wt("vector", t_x)
                tk = vop("scalar_tensor_tensor", H_F[:, oc2 * 512:(oc2 + 1) * 512], psP[:, :],
                         bcol[:, 5 * DC + oc2: 5 * DC + oc2 + 1],
                         xw[:, oc2 * 512:(oc2 + 1) * 512], ALU.add, ALU.add, inc=True)
                ps_free[pname] = (ve, tk)
                h_tok = tk
                lagged.append(lambda oc2=oc2, tk=tk: stats_feed(
                    H_F[:, oc2 * 512:(oc2 + 1) * 512],
                    SQB[:, oc2 * 512:(oc2 + 1) * 512], 0, 512,
                    oc2 == 0, oc2 == DC - 1, prod_ve=tk))
            for f in lagged:
                f()
            if qi == len(qwins) - 1 and bi < 2:
                nx = xo_d[bi + 1] if bi < 1 else xs_d
                w("sync", pe, pe.n)
                for wj in range(2):
                    px_tok[(bi + 1, wj)] = dma(
                        "sync", c3(XW1[:, wj * 4096:(wj + 1) * 4096], DC, 512),
                        dcpn(nx)[:, :, wj * 512:(wj + 1) * 512])
                px_tok["wv"] = dma("sync", c3(WPS, DC, 1024), dcpn(zv_d))
            t_bc = stats_tail(False, "psS0", psS0[:, 0:512], psS0[:, 512:1024])
            w("vector", pe, t_bc)
            w("vector", ve, x_tok[1 - par])
            wt("vector", last_oa_store)
            xnh_tok = 0
            xnh_toks = []
            for c in range(DC):
                vop("tensor_tensor", scr[:, c * 512:(c + 1) * 512],
                    H_F[:, c * 512:(c + 1) * 512], psS0[:, 0:512], ALU.subtract)
                vchain()
                xnh_tok = vop("scalar_tensor_tensor", XNH[:, c * 512:(c + 1) * 512],
                              scr[:, c * 512:(c + 1) * 512],
                              lncols[:, 3 * DC + c: 3 * DC + c + 1],
                              psS0[:, 512:1024], ALU.mult, ALU.mult,
                              inc=(c == DC - 1))
                xnh_toks.append(ve.n)
            x_tok[1 - par] = xnh_tok
            ps_free["psS0"] = (ve, xnh_tok)
            wt("tensor", t_w1)
            wt("tensor", t_w2)
            pre_tok = 0
            hid_free_pe = 0
            stats_begin()
            lagged2 = []
            for th in range(2):
                gel_toks = []
                for ic in range(16):
                    pname = "psA" if ic % 2 == 0 else "psB"
                    psP = PSN[pname]
                    ps_guard(pname)
                    for c in range(DC):
                        w("tensor", ve, xnh_toks[c])
                        mm(psP[:, 0:256], W1B[:, c * 2048 + ic * 128: c * 2048 + (ic + 1) * 128],
                           XNH[:, c * 512 + th * 256: c * 512 + th * 256 + 256], c == 0,
                           c == DC - 1, inc=(c == DC - 1))
                    t_g = pe.n
                    w("scalar", pe, t_g)
                    w("scalar", pe, hid_free_pe)
                    wt("scalar", last_oa_store)
                    t_gel = sact(HID[:, ic * 256:(ic + 1) * 256], psP[:, 0:256],
                                 AF.Gelu_apprx_tanh,
                                 bias=bcol[:, 2 * DC + ic: 2 * DC + ic + 1], inc=True)
                    gel_toks.append(t_gel)
                    ps_free[pname] = (sc, t_gel)
                for oc2 in range(DC):
                    pname = "psA" if oc2 % 2 == 0 else "psB"
                    psP = PSN[pname]
                    ps_guard(pname)
                    w("tensor", sc, gel_toks[-1])
                    for ic in range(16):
                        mm(psP[:, 0:256],
                           W2B[:, ic * 1024 + oc2 * 128: ic * 1024 + (oc2 + 1) * 128],
                           HID[:, ic * 256:(ic + 1) * 256], ic == 0, ic == 15,
                           inc=(ic == 15))
                    t_g = pe.n
                    if lagged2:
                        lagged2.pop(0)()
                    w("vector", pe, t_g)
                    off = oc2 * 512 + th * 256
                    vop("scalar_tensor_tensor", H_F[:, off:off + 256], psP[:, 0:256],
                        bcol[:, 4 * DC + oc2: 4 * DC + oc2 + 1],
                        H_F[:, off:off + 256], ALU.add, ALU.add)
                    vchain()
                    tk = vop("tensor_tensor", H_F[:, off:off + 256], H_F[:, off:off + 256],
                             xw[:, off:off + 256], ALU.add, inc=True)
                    ps_free[pname] = (ve, tk)
                    pre_tok = tk
                    lagged2.append(lambda off=off, th=th, oc2=oc2, tk=tk: stats_feed(
                        H_F[:, off:off + 256], SQB[:, off:off + 256],
                        th * 256, 256, oc2 == 0, oc2 == DC - 1, prod_ve=tk))
                hid_free_pe = pe.n
                for f in lagged2:
                    f()
                lagged2 = []
            t_bc = stats_tail(owned, "psS1", psS1[:, 0:512], psS1[:, 512:1024])
            w("vector", pe, t_bc)
            wt("vector", last_oa_store)
            out_tok = 0
            for c in range(DC):
                vop("tensor_tensor", H_F[:, c * 512:(c + 1) * 512],
                    H_F[:, c * 512:(c + 1) * 512], psS1[:, 0:512], ALU.subtract)
                vchain()
                out_tok = vop("tensor_tensor", OUT_STG[:, c * 512:(c + 1) * 512],
                              H_F[:, c * 512:(c + 1) * 512], psS1[:, 512:1024], ALU.mult,
                              inc=(c == DC - 1))
            ps_free["psS1"] = (ve, out_tok)
            x_tok[par] = out_tok
            if owned:
                w("sync", ve, out_tok)
                last_oa_store = dma("sync",
                                    dcpn(oa_d[bi])[:, :, qb * 512:(qb + 1) * 512],
                                    c3(OUT_STG[:, 0:4096], DC, 512))
            else:
                vchain()
                for c in range(DC):
                    vop("tensor_copy", S1F[:, c * 512:(c + 1) * 512],
                        OUT_STG[:, c * 512:(c + 1) * 512], inc=(c == DC - 1))

    barrier()
    vchain()
    s1_tok = 0
    for c in range(DC):
        s1_tok = vop("tensor_copy", S1BF[:, c * 512:(c + 1) * 512],
                     S1F[:, c * 512:(c + 1) * 512], inc=(c == DC - 1))
    t_ew = dma("sync", FRf[0:1, 2048:3072], ew_d)
    wt("vector", t_ew)
    ewv = FRf[0:1, 2048:3072].rearrange("a (t k) -> a t k", k=2)
    t_ms = vop("tensor_tensor", r_G[0:1, :].rearrange("a (t k) -> a t k", k=1),
               ewv[:, :, 0:1], ewv[:, :, 1:2], ALU.add, inc=True)
    w("tensor", ve, t_ms)
    ps_guard("psC")
    t_bcm = bcast_hilo(r_G[0:1, :], psC[:, :], HIL0, 128)
    w("vector", pe, t_bcm)
    t_msb = vop("tensor_copy", AR4f[:, 4096:4608], psC[:, :], inc=True)
    ps_free["psC"] = (ve, t_msb)

    mw1_free = [0, 0]
    gel_tok = []
    for icb in range(8):
        par = icb % 2
        buf = MW1B[:, par * 8192:(par + 1) * 8192]
        w("gpsimd", pe, mw1_free[par])
        t_wb = dma("gpsimd", c3(buf, DC, 1024), dcpn(mw1_d[:, icb * 1024:(icb + 1) * 1024]))
        for icl in range(8):
            pname = "psA" if (icb * 8 + icl) % 2 == 0 else "psB"
            psP = PSN[pname]
            ps_guard(pname)
            wt("tensor", t_wb)
            w("tensor", ve, s1_tok)
            for c in range(DC):
                mm(psP[:, :], buf[:, c * 1024 + icl * 128: c * 1024 + (icl + 1) * 128],
                   S1BF[:, c * 512:(c + 1) * 512], c == 0, c == DC - 1,
                   inc=(c == DC - 1))
            t_g = pe.n
            w("scalar", pe, t_g)
            ic64 = icb * 8 + icl
            dstH = HID2[ic64 // 32][:, (ic64 % 32) * 512:(ic64 % 32 + 1) * 512]
            t_gel = sact(dstH, psP[:, :], AF.Gelu_apprx_tanh, inc=True)
            gel_tok.append(t_gel)
            ps_free[pname] = (sc, t_gel)
        mw1_free[par] = pe.n

    psY = [psS0.ap()[:, 0:512], psS0.ap()[:, 512:1024], psS1.ap()[:, 0:512],
           psS1.ap()[:, 512:1024], psA.ap(), psB.ap(), psC.ap(), psD.ap()]
    for k in PSN:
        ps_guard_eng = ps_free[k]
        w("tensor", ps_guard_eng[0], ps_guard_eng[1])
    mw2_free = [0, 0, 0, 0]
    w2toks = {}
    y2_tok = 0
    for ic64 in range(64):
        for pf in range(ic64 if ic64 else 0, min(ic64 + 4, 64)):
            if pf not in w2toks:
                par_pf = pf % 4
                w("gpsimd", pe, mw2_free[par_pf])
                w2toks[pf] = dma("gpsimd", MW2B[:, par_pf * 1024:(par_pf + 1) * 1024],
                                 mw2_d[pf * 128:(pf + 1) * 128, :])
        par = ic64 % 4
        wbuf = MW2B[:, par * 1024:(par + 1) * 1024]
        wt("tensor", w2toks.pop(ic64))
        w("tensor", sc, gel_tok[ic64])
        rhs = HID2[ic64 // 32][:, (ic64 % 32) * 512:(ic64 % 32 + 1) * 512]
        for oc in range(8):
            y2_tok = mm(psY[oc], wbuf[:, oc * 128:(oc + 1) * 128], rhs,
                        ic64 == 0, ic64 == 63, inc=(oc == 7))
        mw2_free[par] = y2_tok

    w("vector", pe, y2_tok)
    w("vector", ve, t_msb)
    z_tok = 0
    for oc in range(8):
        vop("tensor_tensor", psY[oc], psY[oc], AR4f[:, 4096:4608], ALU.mult)
        vchain()
        z_tok = vop("tensor_tensor", S1F[:, oc * 512:(oc + 1) * 512], psY[oc],
                    S1F[:, oc * 512:(oc + 1) * 512], ALU.add, inc=True)
    for k in PSN:
        ps_free[k] = (ve, z_tok)
    stats_begin()
    for c in range(DC):
        stats_feed(S1F[:, c * 512:(c + 1) * 512], SQB[:, c * 512:(c + 1) * 512],
                   0, 512, c == 0, c == DC - 1, prod_ve=z_tok)
    t_bc = stats_tail(False, "psS0", psS0[:, 0:512], psS0[:, 512:1024])
    w("vector", pe, t_bc)
    ob_tok = 0
    for c in range(DC):
        vop("tensor_tensor", S1F[:, c * 512:(c + 1) * 512],
            S1F[:, c * 512:(c + 1) * 512], psS0[:, 0:512], ALU.subtract)
        vchain()
        ob_tok = vop("tensor_tensor", OBST[:, c * 512:(c + 1) * 512],
                     S1F[:, c * 512:(c + 1) * 512], psS0[:, 512:1024], ALU.mult,
                     inc=(c == DC - 1))
    w("sync", ve, ob_tok)
    dma("sync", dcpn(ob_d), c3(OBST[:, 0:4096], DC, 512))

    for s in LD + LG:
        w("sync", s, s.n)

    with nc.Block() as block:
        @block.sync
        def _(E):
            for f in ops["sync"]:
                f(E)

        @block.gpsimd
        def _(E):
            for f in ops["gpsimd"]:
                f(E)

        @block.tensor
        def _(E):
            for f in ops["tensor"]:
                f(E)

        @block.vector
        def _(E):
            for f in ops["vector"]:
                f(E)

        @block.scalar
        def _(E):
            for f in ops["scalar"]:
                f(E)

    return nc


def _prep_host(inputs):
    f = lambda k: np.asarray(inputs[k], dtype=np.float32)
    xT = np.ascontiguousarray(f("L0_attn_x").transpose(0, 2, 1))
    shared = {
        "wq": np.ascontiguousarray(f("L0_attn_Wq")),
        "wk": np.ascontiguousarray(f("L0_attn_Wk")),
        "wv": np.ascontiguousarray(f("L0_attn_Wv")),
        "wo": np.ascontiguousarray(f("L0_attn_Wo")),
        "w1": np.ascontiguousarray(f("L0_attn_W1")),
        "w2": np.ascontiguousarray(f("L0_attn_W2")),
        "mw1": np.ascontiguousarray(f("L0_ffn_w1")),
        "mw2": np.ascontiguousarray(f("L0_ffn_w2")),
        "bq": f("L0_attn_bq").reshape(1, D),
        "bk": f("L0_attn_bk").reshape(1, D),
        "bv": f("L0_attn_bv").reshape(1, D),
        "bo": f("L0_attn_bo").reshape(1, D),
        "b1": f("L0_attn_b1").reshape(1, 2 * D),
        "b2": f("L0_attn_b2").reshape(1, D),
    }
    for nm, key in (("lnq_s", "L0_attn_lnq_s"), ("lnk_s", "L0_attn_lnk_s"),
                    ("lnv_s", "L0_attn_lnv_s"), ("lnm_s", "L0_attn_lnm_s"),
                    ("lnq_b", "L0_attn_lnq_b"), ("lnk_b", "L0_attn_lnk_b"),
                    ("lnv_b", "L0_attn_lnv_b"), ("lnm_b", "L0_attn_lnm_b")):
        shared[nm] = np.ascontiguousarray(f(key).reshape(DC, 128).T)
    mko = np.zeros((128, 1), np.float32)
    mko[127, 0] = -1e9
    shared["mko"] = mko
    ew_all = f("L0_ffn_expert_weights")
    in_maps = []
    for c in range(8):
        mb, sw = c // 4, c % 4
        m = dict(shared)
        m["xo"] = np.ascontiguousarray(xT[2 * c:2 * c + 2])
        m["xs"] = np.ascontiguousarray(np.roll(xT[mb], -512 * sw, axis=1))
        m["ew"] = np.ascontiguousarray(ew_all[512 * c:512 * (c + 1)].reshape(1, 1024))
        mks = np.zeros((128, 1), np.float32)
        if sw == 0:
            mks[127, 0] = -1e9
        m["mks"] = mks
        in_maps.append(m)
    return in_maps


def kernel(**inputs):
    if "nc" not in _CACHE:
        _CACHE["nc"] = build_program()
    nc = _CACHE["nc"]
    in_maps = _prep_host(inputs)
    res = run_bass_kernel_spmd(nc, in_maps, core_ids=list(range(8)))
    out = np.empty((16, S, D), np.float32)
    for c in range(8):
        oa = res.results[c]["oa"]
        if c > 0:
            out[2 * c] = oa[0].T
            out[2 * c + 1] = oa[1].T
    flat01 = out[:2].reshape(2 * S, D)
    for c in range(8):
        ob = res.results[c]["ob"]
        flat01[512 * c:512 * (c + 1)] = ob.T
    return out


# version 15
# speedup vs baseline: 1.0162x; 1.0045x over previous
import numpy as np
import concourse.bass as bass
import concourse.mybir as mybir
from concourse.bass_utils import run_bass_kernel_spmd

F32 = mybir.dt.float32
BF16 = mybir.dt.bfloat16
AF = mybir.ActivationFunctionType
ALU = mybir.AluOpType

D = 1024
DC = 8
S = 2048
H = 16
W = 512
NW = 4
EPS = 1e-5
ENGS = ("sync", "gpsimd", "tensor", "vector", "scalar")

_CACHE = {}
DEBUG = False
HILO = True
PARANOID = 0


class Sem:
    __slots__ = ("h", "n", "name")

    def __init__(self, h, name):
        self.h = h
        self.n = 0
        self.name = name


def build_program():
    nc = bass.Bass("TRN2", target_bir_lowering=False)
    ops = {e: [] for e in ENGS}
    seen = {e: {} for e in ENGS}

    def on(eng, fn):
        ops[eng].append(fn)

    def w(eng, sem, cnt=None):
        if cnt is None:
            cnt = sem.n
        if cnt <= 0:
            return
        if seen[eng].get(sem.name, 0) >= cnt:
            return
        seen[eng][sem.name] = cnt
        on(eng, lambda E, s=sem.h, c=cnt: E.wait_ge(s, c))

    dt = nc.dram_tensor
    xo_d = dt("xo", [2, D, S], F32, kind="ExternalInput").ap()
    xs_d = dt("xs", [D, S], F32, kind="ExternalInput").ap()
    wq_d = dt("wq", [D, D], F32, kind="ExternalInput").ap()
    wk_d = dt("wk", [D, D], F32, kind="ExternalInput").ap()
    wv_d = dt("wv", [D, D], F32, kind="ExternalInput").ap()
    wo_d = dt("wo", [D, D], F32, kind="ExternalInput").ap()
    w1_d = dt("w1", [D, 2 * D], F32, kind="ExternalInput").ap()
    w2_d = dt("w2", [2 * D, D], F32, kind="ExternalInput").ap()
    mw1_d = dt("mw1", [D, 8 * D], F32, kind="ExternalInput").ap()
    mw2_d = dt("mw2", [8 * D, D], F32, kind="ExternalInput").ap()
    ln_d = {}
    for nm in ("lnq_s", "lnk_s", "lnv_s", "lnm_s", "lnq_b", "lnk_b", "lnv_b", "lnm_b"):
        ln_d[nm] = dt(nm, [128, DC], F32, kind="ExternalInput").ap()
    bq_d = dt("bq", [1, D], F32, kind="ExternalInput").ap()
    bk_d = dt("bk", [1, D], F32, kind="ExternalInput").ap()
    bv_d = dt("bv", [1, D], F32, kind="ExternalInput").ap()
    bo_d = dt("bo", [1, D], F32, kind="ExternalInput").ap()
    b1_d = dt("b1", [1, 2 * D], F32, kind="ExternalInput").ap()
    b2_d = dt("b2", [1, D], F32, kind="ExternalInput").ap()
    ew_d = dt("ew", [1, 1024], F32, kind="ExternalInput").ap()
    mko_d = dt("mko", [128, 1], F32, kind="ExternalInput").ap()
    mks_d = dt("mks", [128, 1], F32, kind="ExternalInput").ap()

    oa_d = dt("oa", [2, D, S], F32, kind="ExternalOutput").ap()
    ob_d = dt("ob", [D, W], F32, kind="ExternalOutput").ap()

    zq_d = dt("zq", [D, D], BF16, kind="Internal").ap()
    zk_d = dt("zk", [D, D], BF16, kind="Internal").ap()
    zv_d = dt("zv", [D, D], BF16, kind="Internal").ap()
    zbv_d = dt("zbv", [1, D], BF16, kind="Internal").ap()

    A = nc.alloc_sbuf_tensor
    AR1 = A("AR1", [128, 16384], BF16)
    AR2 = A("AR2", [128, 16384], BF16)
    AR3 = A("AR3", [128, 16640], BF16)
    AR4 = A("AR4", [128, 16384], BF16)
    FR = A("FR", [128, 32768], BF16)
    ones_r_bf = A("ones_r_bf", [1, 512], BF16)
    ones_c_bf = A("ones_c_bf", [128, 1], BF16)
    ones_r_f = A("ones_r_f", [1, 128], F32)
    ones_c_f = A("ones_c_f", [128, 1], F32)
    eps_c = A("eps_c", [128, 1], F32)
    mko_s = A("mko_s", [128, 1], F32)
    mks_s = A("mks_s", [128, 1], F32)
    lncols = A("lncols", [128, 8 * DC], F32)
    lnb_bf = A("lnb_bf", [128, 4 * DC], BF16)
    r_mu = A("r_mu", [1, 512], F32)
    r_rsig = A("r_rsig", [1, 512], F32)
    r_G = A("r_G", [1, 512], F32)
    r_sd = A("r_sd", [1, 512], F32)
    r_var = A("r_var", [1, 512], F32)
    Rb_s = A("Rb_s", [64, 512], F32)
    bcol = A("bcol", [128, 6 * DC], F32)

    PSAL = nc.alloc_psum_tensor
    psS0 = PSAL("psS0", [128, 1024], F32)
    psS1 = PSAL("psS1", [128, 1024], F32)
    psA = PSAL("psA", [128, 512], F32)
    psB = PSAL("psB", [128, 512], F32)
    psC = PSAL("psC", [128, 512], F32)
    psD = PSAL("psD", [128, 512], F32)
    PSN = {"psS0": psS0, "psS1": psS1, "psA": psA, "psB": psB, "psC": psC, "psD": psD}

    AR1f = AR1.ap().bitcast(F32)
    AR2f = AR2.ap().bitcast(F32)
    AR3f = AR3.ap().bitcast(F32)
    AR4f = AR4.ap().bitcast(F32)
    FRf = FR.ap().bitcast(F32)

    pe = Sem(nc.alloc_semaphore(name="pe"), "pe")
    ve = Sem(nc.alloc_semaphore(name="ve"), "ve")
    sc = Sem(nc.alloc_semaphore(name="sc"), "sc")
    LD = [Sem(nc.alloc_semaphore(name=f"ld{i}"), f"ld{i}") for i in range(6)]
    LG = [Sem(nc.alloc_semaphore(name=f"lg{i}"), f"lg{i}") for i in range(4)]
    QST = {"sync": [LD, 0], "gpsimd": [LG, 0]}
    ALLSEMS = (pe, ve, sc) + tuple(LD) + tuple(LG)

    def dma(eng, out_ap, in_ap, slow=False):
        q = QST[eng]
        s = q[0][q[1] % len(q[0])]
        q[1] += 1
        w(eng, s, s.n)
        on(eng, lambda E, o=out_ap, i=in_ap, sh=s.h, sl=slow:
           E.dma_start(out=o, in_=i, allow_slow_non_contiguous=sl).then_inc(sh, 16))
        s.n += 16
        return (s, s.n)

    def wt(eng, tok):
        if tok is not None:
            w(eng, tok[0], tok[1])

    def mm(out_ap, lhsT, rhs, start, stop, inc=False):
        if inc:
            on("tensor", lambda E, o=out_ap, l=lhsT, r=rhs, s1=start, s2=stop:
               E.matmul(o, l, r, start=s1, stop=s2).then_inc(pe.h, 1))
            pe.n += 1
            return pe.n
        on("tensor", lambda E, o=out_ap, l=lhsT, r=rhs, s1=start, s2=stop:
           E.matmul(o, l, r, start=s1, stop=s2))
        return pe.n

    def vop(name, *args, inc=True):
        on("vector", lambda E, n=name, a=args: getattr(E, n)(*a).then_inc(ve.h, 1))
        ve.n += 1
        return ve.n

    def vchain():
        w("vector", ve, ve.n)

    def sact(out_ap, in_ap, func, bias=0.0, scale=1.0, inc=True):
        on("scalar", lambda E, o=out_ap, i=in_ap, f=func, bi=bias, s=scale:
           E.activation(o, i, f, bias=bi, scale=s).then_inc(sc.h, 1))
        sc.n += 1
        return sc.n

    def barrier(engs=ENGS):
        for e in engs:
            for s in ALLSEMS:
                w(e, s, s.n)

    def c3(ap2d, c, n):
        return ap2d.rearrange("p (c n) -> p c n", c=c)

    def dcpn(ap2d):
        return ap2d.rearrange("(c p) n -> p c n", p=128)

    XN = FR.ap()[:, 0:16384]
    E_BUF = [FR.ap()[:, 0:8192], FR.ap()[:, 8192:16384]]
    W1B = FR.ap()[:, 0:16384]
    WPS = FR.ap()[:, 16384:24576]
    SQB = FR.ap()[:, 24576:28672]
    ATTO = FR.ap()[:, 24576:28672]
    BIAS = FR.ap()[:, 28672:32768]
    S1BF = FR.ap()[:, 0:4096]
    MW2B = FR.ap()[:, 16384:20480]

    K_s = AR1.ap()
    Q_s = AR2.ap()
    V_s = AR3.ap()
    ATT = AR4.ap()
    W2B = AR1.ap()
    XW3 = AR2f
    H_F = AR3f
    XNH = AR3.ap()[:, 8192:12288]
    HID = AR3.ap()[:, 12288:16384]
    XW1 = AR4f
    OUT_STG = AR3f[:, 4096:8192]
    S1F = AR4f
    MW1B = AR3.ap()
    HID2 = [AR1.ap(), AR2.ap()]
    OBST = AR2f

    ps_free = {k: (ve, 0) for k in PSN}

    def ps_guard(name):
        s, t = ps_free[name]
        w("tensor", s, t)

    sqb_free = [pe, 0]
    hil_free = [0]

    HIL0 = r_sd.ap().bitcast(BF16)
    HIL1 = r_var.ap().bitcast(BF16)

    def bcast_hilo(row_f32, ps_out, hil, m):
        if not HILO:
            t = vop("memset", hil[0:1, 0:1], 0.0, inc=True)
            w("tensor", ve, t)
            return mm(ps_out, ones_r_f[0:1, 0:m], row_f32, True, True, inc=True)
        vchain()
        w("vector", pe, hil_free[0])
        vop("tensor_copy", hil[0:1, 0:512], row_f32)
        vchain()
        t = vop("tensor_tensor", hil[0:1, 512:1024], row_f32, hil[0:1, 0:512],
                ALU.subtract, inc=True)
        w("tensor", ve, t)
        mm(ps_out, ones_r_bf[0:1, 0:m], hil[0:1, 0:512], True, False)
        t_bc = mm(ps_out, ones_r_bf[0:1, 0:m], hil[0:1, 512:1024], False, True, inc=True)
        hil_free[0] = t_bc
        return t_bc

    vop("memset", ones_r_bf[:, :], 1.0)
    vop("memset", ones_c_bf[:, :], 1.0)
    vop("memset", ones_r_f[:, :], 1.0)
    vop("memset", ones_c_f[:, :], 1.0)
    ve_const = vop("memset", eps_c[:, :], EPS, inc=True)
    for e in ("tensor", "scalar", "sync", "gpsimd"):
        w(e, ve, ve_const)

    ln_toks = []
    for i, nm in enumerate(("lnq_s", "lnk_s", "lnv_s", "lnm_s", "lnq_b", "lnk_b", "lnv_b", "lnm_b")):
        ln_toks.append(dma("sync", lncols[:, i * DC:(i + 1) * DC], ln_d[nm]))
    t_mko = dma("sync", mko_s[:, :], mko_d)
    t_mks = dma("sync", mks_s[:, :], mks_d)
    t_bq = dma("sync", FRf[0:1, 6144:7168], bq_d)
    t_bk = dma("sync", FRf[0:1, 7168:8192], bk_d)
    t_bv = dma("sync", FRf[0:1, 8192:9216], bv_d)
    t_b1 = dma("sync", FRf[0:1, 9216:11264], b1_d)
    t_b2 = dma("sync", FRf[0:1, 11264:12288], b2_d)
    t_bo = dma("sync", FRf[0:1, 12288:13312], bo_d)
    for t in ln_toks[4:8]:
        wt("vector", t)
    ve_lnb = vop("tensor_copy", lnb_bf[:, :], lncols[:, 4 * DC:8 * DC], inc=True)
    def row_to_cols(row_f32, bcol_off, n, gate_dma=None, gate_ve=0):
        ps_guard("psD")
        wt("tensor", gate_dma)
        w("tensor", ve, gate_ve)
        for c in range(n):
            mm(psD[:, c:c + 1], row_f32[0:1, c * 128:(c + 1) * 128],
               ones_c_f[0:1, 0:1], True, True, inc=(c == n - 1))
        t = pe.n
        w("vector", pe, t)
        tk = vop("tensor_copy", bcol[:, bcol_off:bcol_off + n], psD[:, 0:n], inc=True)
        ps_free["psD"] = (ve, tk)
        return t

    row_to_cols(FRf[0:1, 11264:12288], 4 * DC, DC, gate_dma=t_b2)
    row_to_cols(FRf[0:1, 12288:13312], 5 * DC, DC, gate_dma=t_bo)

    prep = [
        (wq_d, zq_d, 0, 6144, D, 0, t_bq),
        (wk_d, zk_d, 1, 7168, D, DC, t_bk),
        (wv_d, zv_d, 2, 8192, D, None, t_bv),
    ]
    beff_ps = [psA, psB, psC, psD]
    frrow_dma = None
    frrow_pe = 0
    scale_tok = [0, 0]
    tmp_tok = [0, 0]
    store_tok = [None, None]
    for (src, zdst, gi, brow_off, dout, beff_dst, t_brow) in prep:
        nb = dout // 512
        mm_tok = 0
        for c in range(DC):
            par = c % 2
            stg = AR1f[:, par * 2048: par * 2048 + dout]
            tmp = AR2.ap()[:, par * 2048: par * 2048 + dout]
            sbf = AR2.ap()[:, 4096 + par * 2048: 4096 + par * 2048 + dout]
            w("sync", ve, scale_tok[par])
            w("sync", pe, mm_tok)
            t_ld = dma("sync", c3(stg, nb, 512), c3(src[c * 128:(c + 1) * 128, :], nb, 512))
            wt("vector", t_ld)
            wt("vector", ln_toks[gi])
            wt("vector", store_tok[par])
            w("vector", ve, ve_lnb)
            w("vector", pe, tmp_tok[par])
            ve_tmp = vop("tensor_copy", tmp, stg, inc=True)
            sc_col = lncols[:, gi * DC + c: gi * DC + c + 1]
            ve_scl = vop("tensor_scalar", sbf, stg, sc_col, None, ALU.mult, inc=True)
            scale_tok[par] = ve_scl
            w("tensor", ve, ve_tmp)
            for nbi in range(nb):
                mm_tok = mm(beff_ps[nbi][0:1, :], lnb_bf[:, gi * DC + c: gi * DC + c + 1],
                            tmp[:, nbi * 512:(nbi + 1) * 512], c == 0, c == DC - 1,
                            inc=(nbi == nb - 1))
            tmp_tok[par] = mm_tok
            w("sync", ve, ve_scl)
            store_tok[par] = dma("sync", c3(zdst[c * 128:(c + 1) * 128, :], nb, 512),
                                 c3(sbf, nb, 512))
        w("vector", pe, mm_tok)
        wt("vector", t_brow)
        wt("vector", frrow_dma)
        w("vector", pe, frrow_pe)
        ve_beff = 0
        for nbi in range(nb):
            ve_beff = vop("tensor_tensor", FRf[0:1, nbi * 512:(nbi + 1) * 512],
                          beff_ps[nbi][0:1, :],
                          FRf[0:1, brow_off + nbi * 512: brow_off + (nbi + 1) * 512],
                          ALU.add, inc=(nbi == nb - 1))
        for k in ("psA", "psB", "psC", "psD")[:nb]:
            ps_free[k] = (ve, ve_beff)
        if beff_dst is None:
            w("gpsimd", ve, ve_beff)
            frrow_dma = dma("gpsimd", zbv_d, FRf[0:1, 0:dout])
        else:
            frrow_pe = row_to_cols(FRf[0:1, 0:dout], beff_dst, dout // 128,
                                   gate_ve=ve_beff)

    wt("gpsimd", frrow_dma)
    t_w1s = dma("gpsimd", c3(W1B, DC, 2048), dcpn(w1_d))
    wt("tensor", t_w1s)
    w("tensor", ve, ve_lnb)
    ps_guard("psA")
    ps_guard("psB")
    ps_guard("psC")
    ps_guard("psD")
    b1ps = [psA, psB, psC, psD]
    for c in range(DC):
        for nbi in range(4):
            mm(b1ps[nbi][0:1, :], lnb_bf[:, 3 * DC + c: 3 * DC + c + 1],
               W1B[:, c * 2048 + nbi * 512:(c * 2048 + (nbi + 1) * 512)], c == 0,
               c == DC - 1, inc=(c == DC - 1 and nbi == 3))
    t_b1mm = pe.n
    w("vector", pe, t_b1mm)
    wt("vector", t_b1)
    ve_b1 = 0
    for nbi in range(4):
        ve_b1 = vop("tensor_tensor", FRf[0:1, 14336 + nbi * 512: 14336 + (nbi + 1) * 512],
                    b1ps[nbi][0:1, :],
                    FRf[0:1, 9216 + nbi * 512: 9216 + (nbi + 1) * 512],
                    ALU.add, inc=(nbi == 3))
    for k in ("psA", "psB", "psC", "psD"):
        ps_free[k] = (ve, ve_b1)
    row_to_cols(FRf[0:1, 14336:16384], 2 * DC, 2 * DC, gate_ve=ve_b1)

    barrier()

    def stats_begin():
        ps_guard("psC")
        ps_guard("psD")

    st_sq = []

    def stats_feed(src_chunk, sq_chunk, t0, n, first, last, prod_ve=0, prod_dma=None):
        w("vector", sqb_free[0], sqb_free[1])
        if prod_dma is not None:
            wt("vector", prod_dma)
        if prod_ve:
            w("vector", ve, prod_ve)
        t_sq = vop("tensor_tensor", sq_chunk, src_chunk, src_chunk, ALU.mult, inc=True)
        st_sq.append((sq_chunk, t0, n, t_sq, first, last))
        if prod_dma is not None:
            wt("tensor", prod_dma)
        if prod_ve:
            w("tensor", ve, prod_ve)
        mm(psC[0:1, t0:t0 + n], ones_c_f[:, :], src_chunk, first, last, inc=last)

    def stats_rows(want_G):
        w("tensor", ve, st_sq[-1][3])
        ps_guard("psD")
        for (sq_chunk, t0, n, t_sq, first, last) in st_sq:
            mm(psD[0:1, t0:t0 + n], ones_c_bf[:, :], sq_chunk, first, last, inc=last)
        st_sq.clear()
        t_sums = pe.n
        sqb_free[0], sqb_free[1] = pe, t_sums
        w("vector", pe, t_sums)
        vop("tensor_scalar", r_mu[:, :], psC[0:1, :], 1.0 / D, None, ALU.mult)
        vop("tensor_scalar", r_var[:, :], psD[0:1, :], 1.0 / D, None, ALU.mult)
        vchain()
        vop("tensor_tensor", r_sd[:, :], r_mu[:, :], r_mu[:, :], ALU.mult)
        vchain()
        t_var = vop("tensor_tensor", r_var[:, :], r_var[:, :], r_sd[:, :], ALU.subtract, inc=True)
        ps_free["psC"] = (ve, t_var)
        ps_free["psD"] = (ve, t_var)
        if want_G:
            vchain()
            vop("tensor_scalar", r_sd[:, :], r_var[:, :], 1.0 + EPS, None, ALU.mult)
            vchain()
            t_var = vop("tensor_scalar", r_var[:, :], r_sd[:, :], EPS * EPS, None,
                        ALU.add, inc=True)
        w("scalar", ve, t_var)
        w("scalar", pe, hil_free[0])
        sact(r_sd[:, :], r_var[:, :], AF.Ln,
             bias=0.0 if want_G else eps_c[0:1, :])
        w("scalar", sc, sc.n)
        t_apply_sc = sact(r_rsig[:, :], r_sd[:, :], AF.Exp, scale=-0.5, inc=True)
        return t_apply_sc

    def stats_bcast(t_apply_sc, psMu_name, psMu, psRs):
        w("vector", sc, t_apply_sc)
        ps_guard(psMu_name)
        bcast_hilo(r_mu[:, :], psMu, HIL0, 128)
        t_bc = bcast_hilo(r_rsig[:, :], psRs, HIL1, 128)
        return t_bc

    def stats_tail(want_G, psMu_name, psMu, psRs):
        return stats_bcast(stats_rows(want_G), psMu_name, psMu, psRs)

    last_oa_store = None
    px_tok = {}

    for wi in range(2):
        px_tok[(0, wi)] = dma("sync", c3(XW1[:, wi * 4096:(wi + 1) * 4096], DC, 512),
                              dcpn(xo_d[0])[:, :, wi * 512:(wi + 1) * 512])

    for bi in range(3):
        owned = bi < 2
        xsrc = xo_d[bi] if owned else xs_d
        mcol = mko_s if owned else mks_s
        qwins = list(range(NW)) if owned else [0]

        barrier()
        v_ones = vop("memset",
                     V_s.rearrange("p (t h c) -> p t h c", h=16, c=65)[:, :, :, 64:65],
                     1.0, inc=True)
        if "wv" in px_tok:
            t_wv = px_tok.pop("wv")
        else:
            t_wv = dma("sync", c3(WPS, DC, 1024), dcpn(zv_d))
        t_beff = dma("sync", BIAS[0:1, 0:1024], zbv_d)
        gidx = [0]

        def emit_V(wi):
            w("tensor", ve, v_ones)
            w("tensor", ve, xn_tok4[wi])
            wt("tensor", t_beff)
            wt("tensor", t_wv)
            for tt in range(4):
                tile = wi * 4 + tt
                for vb in range(2):
                    pname = "psA" if gidx[0] % 2 == 0 else "psB"
                    psP = PSN[pname]
                    ps_guard(pname)
                    for c in range(DC):
                        mm(psP[:, :],
                           XN[:, c * 2048 + tile * 128: c * 2048 + (tile + 1) * 128],
                           WPS[:, c * 1024 + vb * 512: c * 1024 + (vb + 1) * 512],
                           c == 0, False)
                    t_g = mm(psP[:, :], ones_r_bf[0:1, 0:128],
                             BIAS[0:1, vb * 512:(vb + 1) * 512], False, True, inc=True)
                    w("vector", pe, t_g)
                    dst = V_s.rearrange("p (t h c) -> p t h c", h=16, c=65)[
                        :, tile:tile + 1, 8 * vb:8 * (vb + 1), 0:64]
                    tk = vop("tensor_copy", dst,
                             psP.ap().rearrange("p (o h c) -> p o h c", o=1, c=64),
                             inc=True)
                    ps_free[pname] = (ve, tk)
                    gidx[0] += 1

        xn_tok4 = [0, 0, 0, 0]
        xwbuf = [XW1[:, 0:4096], XW1[:, 4096:8192],
                 XW3[:, 0:4096], XW3[:, 4096:8192]]
        pre_x = {}
        for wj in range(2, NW):
            pre_x[wj] = dma("sync", c3(xwbuf[wj], DC, 512),
                            dcpn(xsrc)[:, :, wj * 512:(wj + 1) * 512])
        for wi in range(NW):
            xw = xwbuf[wi]
            if (bi, wi) in px_tok:
                t_x = px_tok.pop((bi, wi))
            else:
                t_x = pre_x[wi]
            ps_guard("psS0")
            stats_begin()
            for c in range(DC):
                stats_feed(xw[:, c * 512:(c + 1) * 512], SQB[:, c * 512:(c + 1) * 512],
                           0, 512, c == 0, c == DC - 1, prod_dma=t_x)
            t_mid = stats_rows(False)
            if wi > 0:
                emit_V(wi - 1)
            t_bc = stats_bcast(t_mid, "psS0", psS0[:, 0:512], psS0[:, 512:1024])
            w("vector", pe, t_bc)
            tk = 0
            for c in range(DC):
                vop("tensor_tensor", xw[:, c * 512:(c + 1) * 512], xw[:, c * 512:(c + 1) * 512],
                    psS0[:, 0:512], ALU.subtract)
                vchain()
                tk = vop("tensor_tensor", XN[:, c * 2048 + wi * 512: c * 2048 + (wi + 1) * 512],
                         xw[:, c * 512:(c + 1) * 512], psS0[:, 512:1024], ALU.mult,
                         inc=(c == DC - 1))
            xn_tok4[wi] = tk
            ps_free["psS0"] = (ve, tk)
        emit_V(NW - 1)
        wp_free_pe = pe.n

        for pi, (zsrc, dest) in enumerate([(zq_d, Q_s), (zk_d, K_s)]):
            wins = [0] if (bi == 2 and pi == 0) else list(range(NW))
            w("sync", pe, wp_free_pe)
            t_w = dma("sync", c3(WPS, DC, 1024), dcpn(zsrc))
            wt("tensor", t_w)
            w("tensor", ve, max(xn_tok4))
            for wi in wins:
                for oc in range(DC):
                    pname = "psA" if gidx[0] % 2 == 0 else "psB"
                    psP = PSN[pname]
                    ps_guard(pname)
                    for c in range(DC):
                        mm(psP[:, :], WPS[:, c * 1024 + oc * 128: c * 1024 + (oc + 1) * 128],
                           XN[:, c * 2048 + wi * 512: c * 2048 + (wi + 1) * 512], c == 0,
                           c == DC - 1, inc=(c == DC - 1))
                    t_g = pe.n
                    w("vector", pe, t_g)
                    tk = vop("tensor_scalar",
                             dest[:, oc * 2048 + wi * 512: oc * 2048 + (wi + 1) * 512],
                             psP[:, :], bcol[:, pi * DC + oc: pi * DC + oc + 1], None,
                             ALU.add, inc=True)
                    ps_free[pname] = (ve, tk)
                    gidx[0] += 1
            wp_free_pe = pe.n

        barrier()
        if DEBUG and bi == 0:
            dbg_q = dt("dbg_q", [128, 16384], BF16, kind="ExternalOutput").ap()
            dbg_k = dt("dbg_k", [128, 16384], BF16, kind="ExternalOutput").ap()
            dbg_v = dt("dbg_v", [128, 16640], BF16, kind="ExternalOutput").ap()
            dbg_bc = dt("dbg_bc", [128, 4 * DC], F32, kind="ExternalOutput").ap()
            dma("sync", dbg_q, Q_s[:, :])
            dma("sync", dbg_k, K_s[:, :])
            dma("sync", dbg_v, V_s[:, :])
            dma("sync", dbg_bc, bcol[:, :])
        w("gpsimd", pe, wp_free_pe)
        t_wo = dma("gpsimd", c3(WPS, DC, 1024), dcpn(wo_d))
        e_free = [0, 0]
        att_free = [ps_free["psA"], ps_free["psB"]]
        rb_read = [0, 0]
        shift_tok = None
        last_exp = [0, 0]
        pend = None
        for qb in qwins:
            odd_evac_tok = 0
            for hp in range(H // 2):
                oc = hp
                exp_toks = []
                for kt in range(16):
                    parS = kt % 2
                    psS = psS0 if parS == 0 else psS1
                    w("tensor", sc, last_exp[parS])
                    if qb == qwins[0] and hp == 0 and kt < 2:
                        ps_guard("psS0" if parS == 0 else "psS1")
                    mm(psS[:, 0:512],
                       K_s[0:64, oc * 2048 + kt * 128: oc * 2048 + (kt + 1) * 128],
                       Q_s[0:64, oc * 2048 + qb * 512: oc * 2048 + (qb + 1) * 512],
                       True, True)
                    mm(psS[:, 512:1024],
                       K_s[64:128, oc * 2048 + kt * 128: oc * 2048 + (kt + 1) * 128],
                       Q_s[64:128, oc * 2048 + qb * 512: oc * 2048 + (qb + 1) * 512],
                       True, True, inc=True)
                    s_tok = pe.n
                    mask_tok = 0
                    if qb == 0 and kt == 15:
                        w("vector", pe, s_tok)
                        wt("vector", t_mko if owned else t_mks)
                        vop("tensor_tensor", psS[:, 0:1], psS[:, 0:1],
                            mcol[:, :], ALU.add)
                        mask_tok = vop("tensor_tensor", psS[:, 512:513], psS[:, 512:513],
                                       mcol[:, :], ALU.add, inc=True)
                    w("scalar", pe, s_tok)
                    if mask_tok:
                        w("scalar", ve, mask_tok)
                    w("scalar", pe, e_free[kt // 8])
                    t_e = sact(E_BUF[kt // 8][:, (kt % 8) * 1024:(kt % 8 + 1) * 1024],
                               psS[:, :], AF.Exp, scale=1.0 / 32.0, inc=True)
                    exp_toks.append(t_e)
                    last_exp[parS] = t_e
                if pend is not None:
                    pend()
                    pend = None
                t_hilo = [0, 0]
                for h2 in range(2):
                    h = 2 * hp + h2
                    pnameA = "psA" if h2 == 0 else "psB"
                    psAtt = PSN[pnameA]
                    sA, tA = att_free[h2]
                    w("tensor", sA, tA)
                    for kt in range(16):
                        w("tensor", sc, exp_toks[kt])
                        inc = (kt == 15) or (h2 == 1 and kt == 7)
                        mm(psAtt[0:65, :],
                           V_s[:, kt * 1040 + h * 65: kt * 1040 + (h + 1) * 65],
                           E_BUF[kt // 8][:, (kt % 8) * 1024 + h2 * 512:
                                          (kt % 8) * 1024 + h2 * 512 + 512],
                           kt == 0, kt == 15, inc=inc)
                        if h2 == 1 and kt == 7:
                            e_free[0] = pe.n
                    pv_tok = pe.n
                    if h2 == 1:
                        e_free[1] = pv_tok
                    rrrow = r_G if h2 == 0 else r_rsig
                    hil = HIL0 if h2 == 0 else HIL1
                    w("vector", pe, pv_tok)
                    w("vector", pe, rb_read[h2])
                    vop("reciprocal", rrrow[0:1, :], psAtt[64:65, :])
                    vchain()
                    vop("tensor_copy", hil[0:1, 0:512], rrrow[0:1, :])
                    vchain()
                    t_hilo[h2] = vop("tensor_tensor", hil[0:1, 512:1024], rrrow[0:1, :],
                                     hil[0:1, 0:512], ALU.subtract, inc=True)

                def make_pend(hp=hp, oc=oc, t_hilo=t_hilo, qb=qb):
                    def pend_fn():
                        nonlocal odd_evac_tok, shift_tok
                        for h2 in range(2):
                            pnameA = "psA" if h2 == 0 else "psB"
                            psAtt = PSN[pnameA]
                            psbn = "psC" if h2 == 0 else "psD"
                            psb = PSN[psbn]
                            hil = HIL0 if h2 == 0 else HIL1
                            ps_guard(psbn)
                            w("tensor", ve, t_hilo[h2])
                            mm(psb[0:64, :], ones_r_bf[0:1, 0:64], hil[0:1, 0:512],
                               True, False)
                            tb = mm(psb[0:64, :], ones_r_bf[0:1, 0:64],
                                    hil[0:1, 512:1024], False, True, inc=True)
                            hil_free[0] = tb
                            rb_read[h2] = tb
                            w("vector", pe, tb)
                            vchain()
                            vop("tensor_copy", Rb_s[:, :], psb[0:64, :])
                            vchain()
                            if h2 == 0:
                                dst = ATT[0:64, oc * 2048 + qb * 512:
                                          oc * 2048 + (qb + 1) * 512]
                            else:
                                wt("vector", shift_tok)
                                dst = ATTO[0:64, oc * 512:(oc + 1) * 512]
                            tk = vop("tensor_tensor", dst, psAtt[0:64, :], Rb_s[:, :],
                                     ALU.mult, inc=True)
                            att_free[h2] = (ve, tk)
                            ps_free[pnameA] = (ve, tk)
                            ps_free[psbn] = (ve, tk)
                            if h2 == 1:
                                odd_evac_tok = tk
                    return pend_fn
                pend = make_pend()
            pend()
            pend = None
            w("sync", ve, odd_evac_tok)
            shift_tok = dma("sync",
                            c3(ATT[64:128, :], DC, 2048)[:, :, qb * 512:(qb + 1) * 512],
                            c3(ATTO[0:64, :], DC, 512))
        ps_free["psS0"] = (sc, last_exp[0])
        ps_free["psS1"] = (sc, last_exp[1])

        barrier()
        if DEBUG and bi == 0:
            dbg_att = dt("dbg_att", [128, 16384], BF16, kind="ExternalOutput").ap()
            dma("sync", dbg_att, ATT[:, :])
        t_w1 = dma("gpsimd", c3(W1B, DC, 2048), dcpn(w1_d))
        t_w2 = dma("gpsimd", c3(W2B, 16, 1024), dcpn(w2_d))
        x_tok = [xn_tok4[2], xn_tok4[3]]
        for qi, qb in enumerate(qwins):
            if PARANOID & 1:
                barrier()
            par = qi % 2
            xw = XW3[:, par * 4096:(par + 1) * 4096]
            scr = XW3[:, (1 - par) * 4096:(2 - par) * 4096]
            w("sync", ve, x_tok[par])
            t_x = dma("sync", c3(xw, DC, 512), dcpn(xsrc)[:, :, qb * 512:(qb + 1) * 512])
            wt("tensor", t_wo)
            h_tok = 0
            stats_begin()
            lagged = []
            wo_bufs = [("psA", psA), ("psB", psB), ("psD", psD),
                       ("psS0", psS0[:, 0:512])]
            for oc2 in range(DC):
                pname, psP = wo_bufs[oc2 % 4]
                ps_guard(pname)
                for c in range(DC):
                    mm(psP[:, :], WPS[:, c * 1024 + oc2 * 128: c * 1024 + (oc2 + 1) * 128],
                       ATT[:, c * 2048 + qb * 512: c * 2048 + (qb + 1) * 512], c == 0,
                       c == DC - 1, inc=(c == DC - 1))
                t_g = pe.n
                if len(lagged) >= 4:
                    lagged.pop(0)()
                w("vector", pe, t_g)
                wt("vector", t_x)
                tk = vop("scalar_tensor_tensor", H_F[:, oc2 * 512:(oc2 + 1) * 512], psP[:, :],
                         bcol[:, 5 * DC + oc2: 5 * DC + oc2 + 1],
                         xw[:, oc2 * 512:(oc2 + 1) * 512], ALU.add, ALU.add, inc=True)
                ps_free[pname] = (ve, tk)
                h_tok = tk
                lagged.append(lambda oc2=oc2, tk=tk: stats_feed(
                    H_F[:, oc2 * 512:(oc2 + 1) * 512],
                    SQB[:, oc2 * 512:(oc2 + 1) * 512], 0, 512,
                    oc2 == 0, oc2 == DC - 1, prod_ve=tk))
            for f in lagged:
                f()
            if qi == len(qwins) - 1 and bi < 2:
                nx = xo_d[bi + 1] if bi < 1 else xs_d
                w("sync", pe, pe.n)
                for wj in range(2):
                    px_tok[(bi + 1, wj)] = dma(
                        "sync", c3(XW1[:, wj * 4096:(wj + 1) * 4096], DC, 512),
                        dcpn(nx)[:, :, wj * 512:(wj + 1) * 512])
                px_tok["wv"] = dma("sync", c3(WPS, DC, 1024), dcpn(zv_d))
            t_bc = stats_tail(False, "psS0", psS0[:, 0:512], psS0[:, 512:1024])
            w("vector", pe, t_bc)
            w("vector", ve, x_tok[1 - par])
            wt("vector", last_oa_store)
            xnh_tok = 0
            xnh_toks = []
            for c in range(DC):
                vop("tensor_tensor", scr[:, c * 512:(c + 1) * 512],
                    H_F[:, c * 512:(c + 1) * 512], psS0[:, 0:512], ALU.subtract)
                vchain()
                xnh_tok = vop("scalar_tensor_tensor", XNH[:, c * 512:(c + 1) * 512],
                              scr[:, c * 512:(c + 1) * 512],
                              lncols[:, 3 * DC + c: 3 * DC + c + 1],
                              psS0[:, 512:1024], ALU.mult, ALU.mult,
                              inc=(c == DC - 1))
                xnh_toks.append(ve.n)
            x_tok[1 - par] = xnh_tok
            ps_free["psS0"] = (ve, xnh_tok)
            wt("tensor", t_w1)
            wt("tensor", t_w2)
            pre_tok = 0
            hid_free_pe = 0
            stats_begin()
            lagged2 = []
            for th in range(2):
                gel_toks = []
                for ic in range(16):
                    pname = "psA" if ic % 2 == 0 else "psB"
                    psP = PSN[pname]
                    ps_guard(pname)
                    for c in range(DC):
                        w("tensor", ve, xnh_toks[c])
                        mm(psP[:, 0:256], W1B[:, c * 2048 + ic * 128: c * 2048 + (ic + 1) * 128],
                           XNH[:, c * 512 + th * 256: c * 512 + th * 256 + 256], c == 0,
                           c == DC - 1, inc=(c == DC - 1))
                    t_g = pe.n
                    w("scalar", pe, t_g)
                    w("scalar", pe, hid_free_pe)
                    wt("scalar", last_oa_store)
                    t_gel = sact(HID[:, ic * 256:(ic + 1) * 256], psP[:, 0:256],
                                 AF.Gelu_apprx_tanh,
                                 bias=bcol[:, 2 * DC + ic: 2 * DC + ic + 1], inc=True)
                    gel_toks.append(t_gel)
                    ps_free[pname] = (sc, t_gel)
                for oc2 in range(DC):
                    pname, psP = wo_bufs[oc2 % 4]
                    ps_guard(pname)
                    w("tensor", sc, gel_toks[-1])
                    for ic in range(16):
                        mm(psP[:, 0:256],
                           W2B[:, ic * 1024 + oc2 * 128: ic * 1024 + (oc2 + 1) * 128],
                           HID[:, ic * 256:(ic + 1) * 256], ic == 0, ic == 15,
                           inc=(ic == 15))
                    t_g = pe.n
                    if len(lagged2) >= 4:
                        lagged2.pop(0)()
                    w("vector", pe, t_g)
                    off = oc2 * 512 + th * 256
                    vop("scalar_tensor_tensor", H_F[:, off:off + 256], psP[:, 0:256],
                        bcol[:, 4 * DC + oc2: 4 * DC + oc2 + 1],
                        H_F[:, off:off + 256], ALU.add, ALU.add)
                    vchain()
                    tk = vop("tensor_tensor", H_F[:, off:off + 256], H_F[:, off:off + 256],
                             xw[:, off:off + 256], ALU.add, inc=True)
                    ps_free[pname] = (ve, tk)
                    pre_tok = tk
                    lagged2.append(lambda off=off, th=th, oc2=oc2, tk=tk: stats_feed(
                        H_F[:, off:off + 256], SQB[:, off:off + 256],
                        th * 256, 256, oc2 == 0, oc2 == DC - 1, prod_ve=tk))
                hid_free_pe = pe.n
                for f in lagged2:
                    f()
                lagged2 = []
            t_bc = stats_tail(owned, "psS1", psS1[:, 0:512], psS1[:, 512:1024])
            w("vector", pe, t_bc)
            wt("vector", last_oa_store)
            out_tok = 0
            for c in range(DC):
                vop("tensor_tensor", H_F[:, c * 512:(c + 1) * 512],
                    H_F[:, c * 512:(c + 1) * 512], psS1[:, 0:512], ALU.subtract)
                vchain()
                out_tok = vop("tensor_tensor", OUT_STG[:, c * 512:(c + 1) * 512],
                              H_F[:, c * 512:(c + 1) * 512], psS1[:, 512:1024], ALU.mult,
                              inc=(c == DC - 1))
            ps_free["psS1"] = (ve, out_tok)
            x_tok[par] = out_tok
            if owned:
                w("sync", ve, out_tok)
                last_oa_store = dma("sync",
                                    dcpn(oa_d[bi])[:, :, qb * 512:(qb + 1) * 512],
                                    c3(OUT_STG[:, 0:4096], DC, 512))
            else:
                vchain()
                for c in range(DC):
                    vop("tensor_copy", S1F[:, c * 512:(c + 1) * 512],
                        OUT_STG[:, c * 512:(c + 1) * 512], inc=(c == DC - 1))

    barrier()
    vchain()
    s1_tok = 0
    for c in range(DC):
        s1_tok = vop("tensor_copy", S1BF[:, c * 512:(c + 1) * 512],
                     S1F[:, c * 512:(c + 1) * 512], inc=(c == DC - 1))
    t_ew = dma("sync", FRf[0:1, 2048:3072], ew_d)
    wt("vector", t_ew)
    ewv = FRf[0:1, 2048:3072].rearrange("a (t k) -> a t k", k=2)
    t_ms = vop("tensor_tensor", r_G[0:1, :].rearrange("a (t k) -> a t k", k=1),
               ewv[:, :, 0:1], ewv[:, :, 1:2], ALU.add, inc=True)
    w("tensor", ve, t_ms)
    ps_guard("psC")
    t_bcm = bcast_hilo(r_G[0:1, :], psC[:, :], HIL0, 128)
    w("vector", pe, t_bcm)
    t_msb = vop("tensor_copy", AR4f[:, 4096:4608], psC[:, :], inc=True)
    ps_free["psC"] = (ve, t_msb)

    mw1_free = [0, 0]
    gel_tok = []
    for icb in range(8):
        par = icb % 2
        buf = MW1B[:, par * 8192:(par + 1) * 8192]
        w("gpsimd", pe, mw1_free[par])
        t_wb = dma("gpsimd", c3(buf, DC, 1024), dcpn(mw1_d[:, icb * 1024:(icb + 1) * 1024]))
        for icl in range(8):
            pname = "psA" if (icb * 8 + icl) % 2 == 0 else "psB"
            psP = PSN[pname]
            ps_guard(pname)
            wt("tensor", t_wb)
            w("tensor", ve, s1_tok)
            for c in range(DC):
                mm(psP[:, :], buf[:, c * 1024 + icl * 128: c * 1024 + (icl + 1) * 128],
                   S1BF[:, c * 512:(c + 1) * 512], c == 0, c == DC - 1,
                   inc=(c == DC - 1))
            t_g = pe.n
            w("scalar", pe, t_g)
            ic64 = icb * 8 + icl
            dstH = HID2[ic64 // 32][:, (ic64 % 32) * 512:(ic64 % 32 + 1) * 512]
            t_gel = sact(dstH, psP[:, :], AF.Gelu_apprx_tanh, inc=True)
            gel_tok.append(t_gel)
            ps_free[pname] = (sc, t_gel)
        mw1_free[par] = pe.n

    psY = [psS0.ap()[:, 0:512], psS0.ap()[:, 512:1024], psS1.ap()[:, 0:512],
           psS1.ap()[:, 512:1024], psA.ap(), psB.ap(), psC.ap(), psD.ap()]
    for k in PSN:
        ps_guard_eng = ps_free[k]
        w("tensor", ps_guard_eng[0], ps_guard_eng[1])
    mw2_free = [0, 0, 0, 0]
    w2toks = {}
    y2_tok = 0
    for ic64 in range(64):
        for pf in range(ic64 if ic64 else 0, min(ic64 + 4, 64)):
            if pf not in w2toks:
                par_pf = pf % 4
                w("gpsimd", pe, mw2_free[par_pf])
                w2toks[pf] = dma("gpsimd", MW2B[:, par_pf * 1024:(par_pf + 1) * 1024],
                                 mw2_d[pf * 128:(pf + 1) * 128, :])
        par = ic64 % 4
        wbuf = MW2B[:, par * 1024:(par + 1) * 1024]
        wt("tensor", w2toks.pop(ic64))
        w("tensor", sc, gel_tok[ic64])
        rhs = HID2[ic64 // 32][:, (ic64 % 32) * 512:(ic64 % 32 + 1) * 512]
        for oc in range(8):
            y2_tok = mm(psY[oc], wbuf[:, oc * 128:(oc + 1) * 128], rhs,
                        ic64 == 0, ic64 == 63, inc=(oc == 7))
        mw2_free[par] = y2_tok

    w("vector", pe, y2_tok)
    w("vector", ve, t_msb)
    z_tok = 0
    for oc in range(8):
        vop("tensor_tensor", psY[oc], psY[oc], AR4f[:, 4096:4608], ALU.mult)
        vchain()
        z_tok = vop("tensor_tensor", S1F[:, oc * 512:(oc + 1) * 512], psY[oc],
                    S1F[:, oc * 512:(oc + 1) * 512], ALU.add, inc=True)
    for k in PSN:
        ps_free[k] = (ve, z_tok)
    stats_begin()
    for c in range(DC):
        stats_feed(S1F[:, c * 512:(c + 1) * 512], SQB[:, c * 512:(c + 1) * 512],
                   0, 512, c == 0, c == DC - 1, prod_ve=z_tok)
    t_bc = stats_tail(False, "psS0", psS0[:, 0:512], psS0[:, 512:1024])
    w("vector", pe, t_bc)
    ob_tok = 0
    for c in range(DC):
        vop("tensor_tensor", S1F[:, c * 512:(c + 1) * 512],
            S1F[:, c * 512:(c + 1) * 512], psS0[:, 0:512], ALU.subtract)
        vchain()
        ob_tok = vop("tensor_tensor", OBST[:, c * 512:(c + 1) * 512],
                     S1F[:, c * 512:(c + 1) * 512], psS0[:, 512:1024], ALU.mult,
                     inc=(c == DC - 1))
    w("sync", ve, ob_tok)
    dma("sync", dcpn(ob_d), c3(OBST[:, 0:4096], DC, 512))

    for s in LD + LG:
        w("sync", s, s.n)

    with nc.Block() as block:
        @block.sync
        def _(E):
            for f in ops["sync"]:
                f(E)

        @block.gpsimd
        def _(E):
            for f in ops["gpsimd"]:
                f(E)

        @block.tensor
        def _(E):
            for f in ops["tensor"]:
                f(E)

        @block.vector
        def _(E):
            for f in ops["vector"]:
                f(E)

        @block.scalar
        def _(E):
            for f in ops["scalar"]:
                f(E)

    return nc


def _prep_host(inputs):
    f = lambda k: np.asarray(inputs[k], dtype=np.float32)
    xT = np.ascontiguousarray(f("L0_attn_x").transpose(0, 2, 1))
    shared = {
        "wq": np.ascontiguousarray(f("L0_attn_Wq")),
        "wk": np.ascontiguousarray(f("L0_attn_Wk")),
        "wv": np.ascontiguousarray(f("L0_attn_Wv")),
        "wo": np.ascontiguousarray(f("L0_attn_Wo")),
        "w1": np.ascontiguousarray(f("L0_attn_W1")),
        "w2": np.ascontiguousarray(f("L0_attn_W2")),
        "mw1": np.ascontiguousarray(f("L0_ffn_w1")),
        "mw2": np.ascontiguousarray(f("L0_ffn_w2")),
        "bq": f("L0_attn_bq").reshape(1, D),
        "bk": f("L0_attn_bk").reshape(1, D),
        "bv": f("L0_attn_bv").reshape(1, D),
        "bo": f("L0_attn_bo").reshape(1, D),
        "b1": f("L0_attn_b1").reshape(1, 2 * D),
        "b2": f("L0_attn_b2").reshape(1, D),
    }
    for nm, key in (("lnq_s", "L0_attn_lnq_s"), ("lnk_s", "L0_attn_lnk_s"),
                    ("lnv_s", "L0_attn_lnv_s"), ("lnm_s", "L0_attn_lnm_s"),
                    ("lnq_b", "L0_attn_lnq_b"), ("lnk_b", "L0_attn_lnk_b"),
                    ("lnv_b", "L0_attn_lnv_b"), ("lnm_b", "L0_attn_lnm_b")):
        shared[nm] = np.ascontiguousarray(f(key).reshape(DC, 128).T)
    mko = np.zeros((128, 1), np.float32)
    mko[127, 0] = -1e9
    shared["mko"] = mko
    ew_all = f("L0_ffn_expert_weights")
    in_maps = []
    for c in range(8):
        mb, sw = c // 4, c % 4
        m = dict(shared)
        m["xo"] = np.ascontiguousarray(xT[2 * c:2 * c + 2])
        m["xs"] = np.ascontiguousarray(np.roll(xT[mb], -512 * sw, axis=1))
        m["ew"] = np.ascontiguousarray(ew_all[512 * c:512 * (c + 1)].reshape(1, 1024))
        mks = np.zeros((128, 1), np.float32)
        if sw == 0:
            mks[127, 0] = -1e9
        m["mks"] = mks
        in_maps.append(m)
    return in_maps


def kernel(**inputs):
    if "nc" not in _CACHE:
        _CACHE["nc"] = build_program()
    nc = _CACHE["nc"]
    in_maps = _prep_host(inputs)
    res = run_bass_kernel_spmd(nc, in_maps, core_ids=list(range(8)))
    out = np.empty((16, S, D), np.float32)
    for c in range(8):
        oa = res.results[c]["oa"]
        if c > 0:
            out[2 * c] = oa[0].T
            out[2 * c + 1] = oa[1].T
    flat01 = out[:2].reshape(2 * S, D)
    for c in range(8):
        ob = res.results[c]["ob"]
        flat01[512 * c:512 * (c + 1)] = ob.T
    return out


# version 16
# speedup vs baseline: 1.0266x; 1.0103x over previous
import numpy as np
import concourse.bass as bass
import concourse.mybir as mybir
from concourse.bass_utils import run_bass_kernel_spmd

F32 = mybir.dt.float32
BF16 = mybir.dt.bfloat16
AF = mybir.ActivationFunctionType
ALU = mybir.AluOpType

D = 1024
DC = 8
S = 2048
H = 16
W = 512
NW = 4
EPS = 1e-5
ENGS = ("sync", "gpsimd", "tensor", "vector", "scalar")

_CACHE = {}
DEBUG = False
HILO = True
PARANOID = 0


class Sem:
    __slots__ = ("h", "n", "name")

    def __init__(self, h, name):
        self.h = h
        self.n = 0
        self.name = name


def build_program():
    nc = bass.Bass("TRN2", target_bir_lowering=False)
    ops = {e: [] for e in ENGS}
    seen = {e: {} for e in ENGS}

    def on(eng, fn):
        ops[eng].append(fn)

    def w(eng, sem, cnt=None):
        if cnt is None:
            cnt = sem.n
        if cnt <= 0:
            return
        if seen[eng].get(sem.name, 0) >= cnt:
            return
        seen[eng][sem.name] = cnt
        on(eng, lambda E, s=sem.h, c=cnt: E.wait_ge(s, c))

    dt = nc.dram_tensor
    xo_d = dt("xo", [2, D, S], F32, kind="ExternalInput").ap()
    xs_d = dt("xs", [D, S], F32, kind="ExternalInput").ap()
    wq_d = dt("wq", [D, D], F32, kind="ExternalInput").ap()
    wk_d = dt("wk", [D, D], F32, kind="ExternalInput").ap()
    wv_d = dt("wv", [D, D], F32, kind="ExternalInput").ap()
    wo_d = dt("wo", [D, D], F32, kind="ExternalInput").ap()
    w1_d = dt("w1", [D, 2 * D], F32, kind="ExternalInput").ap()
    w2_d = dt("w2", [2 * D, D], F32, kind="ExternalInput").ap()
    mw1_d = dt("mw1", [D, 8 * D], F32, kind="ExternalInput").ap()
    mw2_d = dt("mw2", [8 * D, D], F32, kind="ExternalInput").ap()
    ln_d = {}
    for nm in ("lnq_s", "lnk_s", "lnv_s", "lnm_s", "lnq_b", "lnk_b", "lnv_b", "lnm_b"):
        ln_d[nm] = dt(nm, [128, DC], F32, kind="ExternalInput").ap()
    bq_d = dt("bq", [1, D], F32, kind="ExternalInput").ap()
    bk_d = dt("bk", [1, D], F32, kind="ExternalInput").ap()
    bv_d = dt("bv", [1, D], F32, kind="ExternalInput").ap()
    bo_d = dt("bo", [1, D], F32, kind="ExternalInput").ap()
    b1_d = dt("b1", [1, 2 * D], F32, kind="ExternalInput").ap()
    b2_d = dt("b2", [1, D], F32, kind="ExternalInput").ap()
    ew_d = dt("ew", [1, 1024], F32, kind="ExternalInput").ap()
    mko_d = dt("mko", [128, 1], F32, kind="ExternalInput").ap()
    mks_d = dt("mks", [128, 1], F32, kind="ExternalInput").ap()

    oa_d = dt("oa", [2, D, S], F32, kind="ExternalOutput").ap()
    ob_d = dt("ob", [D, W], F32, kind="ExternalOutput").ap()

    zq_d = dt("zq", [D, D], BF16, kind="Internal").ap()
    zk_d = dt("zk", [D, D], BF16, kind="Internal").ap()
    zv_d = dt("zv", [D, D], BF16, kind="Internal").ap()
    zbv_d = dt("zbv", [1, D], BF16, kind="Internal").ap()

    A = nc.alloc_sbuf_tensor
    AR1 = A("AR1", [128, 16384], BF16)
    AR2 = A("AR2", [128, 16384], BF16)
    AR3 = A("AR3", [128, 16640], BF16)
    AR4 = A("AR4", [128, 16384], BF16)
    FR = A("FR", [128, 32768], BF16)
    ones_r_bf = A("ones_r_bf", [1, 512], BF16)
    ones_c_bf = A("ones_c_bf", [128, 1], BF16)
    ones_r_f = A("ones_r_f", [1, 128], F32)
    ones_c_f = A("ones_c_f", [128, 1], F32)
    eps_c = A("eps_c", [128, 1], F32)
    mko_s = A("mko_s", [128, 1], F32)
    mks_s = A("mks_s", [128, 1], F32)
    lncols = A("lncols", [128, 8 * DC], F32)
    lnb_bf = A("lnb_bf", [128, 4 * DC], BF16)
    r_mu = A("r_mu", [1, 512], F32)
    r_rsig = A("r_rsig", [1, 512], F32)
    r_G = A("r_G", [1, 512], F32)
    r_sd = A("r_sd", [1, 512], F32)
    r_var = A("r_var", [1, 512], F32)
    Rb_s = A("Rb_s", [64, 512], F32)
    bcol = A("bcol", [128, 6 * DC], F32)

    PSAL = nc.alloc_psum_tensor
    psS0 = PSAL("psS0", [128, 1024], F32)
    psS1 = PSAL("psS1", [128, 1024], F32)
    psA = PSAL("psA", [128, 512], F32)
    psB = PSAL("psB", [128, 512], F32)
    psC = PSAL("psC", [128, 512], F32)
    psD = PSAL("psD", [128, 512], F32)
    PSN = {"psS0": psS0, "psS1": psS1, "psA": psA, "psB": psB, "psC": psC, "psD": psD}

    AR1f = AR1.ap().bitcast(F32)
    AR2f = AR2.ap().bitcast(F32)
    AR3f = AR3.ap().bitcast(F32)
    AR4f = AR4.ap().bitcast(F32)
    FRf = FR.ap().bitcast(F32)

    pe = Sem(nc.alloc_semaphore(name="pe"), "pe")
    ve = Sem(nc.alloc_semaphore(name="ve"), "ve")
    sc = Sem(nc.alloc_semaphore(name="sc"), "sc")
    LD = [Sem(nc.alloc_semaphore(name=f"ld{i}"), f"ld{i}") for i in range(6)]
    LG = [Sem(nc.alloc_semaphore(name=f"lg{i}"), f"lg{i}") for i in range(4)]
    QST = {"sync": [LD, 0], "gpsimd": [LG, 0]}
    ALLSEMS = (pe, ve, sc) + tuple(LD) + tuple(LG)

    def dma(eng, out_ap, in_ap, slow=False):
        q = QST[eng]
        s = q[0][q[1] % len(q[0])]
        q[1] += 1
        w(eng, s, s.n)
        on(eng, lambda E, o=out_ap, i=in_ap, sh=s.h, sl=slow:
           E.dma_start(out=o, in_=i, allow_slow_non_contiguous=sl).then_inc(sh, 16))
        s.n += 16
        return (s, s.n)

    def wt(eng, tok):
        if tok is not None:
            w(eng, tok[0], tok[1])

    def mm(out_ap, lhsT, rhs, start, stop, inc=False):
        if inc:
            on("tensor", lambda E, o=out_ap, l=lhsT, r=rhs, s1=start, s2=stop:
               E.matmul(o, l, r, start=s1, stop=s2).then_inc(pe.h, 1))
            pe.n += 1
            return pe.n
        on("tensor", lambda E, o=out_ap, l=lhsT, r=rhs, s1=start, s2=stop:
           E.matmul(o, l, r, start=s1, stop=s2))
        return pe.n

    def vop(name, *args, inc=True):
        on("vector", lambda E, n=name, a=args: getattr(E, n)(*a).then_inc(ve.h, 1))
        ve.n += 1
        return ve.n

    def vchain():
        w("vector", ve, ve.n)

    def sact(out_ap, in_ap, func, bias=0.0, scale=1.0, inc=True):
        on("scalar", lambda E, o=out_ap, i=in_ap, f=func, bi=bias, s=scale:
           E.activation(o, i, f, bias=bi, scale=s).then_inc(sc.h, 1))
        sc.n += 1
        return sc.n

    def barrier(engs=ENGS):
        for e in engs:
            for s in ALLSEMS:
                w(e, s, s.n)

    def c3(ap2d, c, n):
        return ap2d.rearrange("p (c n) -> p c n", c=c)

    def dcpn(ap2d):
        return ap2d.rearrange("(c p) n -> p c n", p=128)

    XN = FR.ap()[:, 0:16384]
    E_BUF = [FR.ap()[:, 0:8192], FR.ap()[:, 8192:16384]]
    W1B = FR.ap()[:, 0:16384]
    WPS = FR.ap()[:, 16384:24576]
    SQB = FR.ap()[:, 24576:28672]
    ATTO = FR.ap()[:, 24576:28672]
    BIAS = FR.ap()[:, 28672:32768]
    S1BF = FR.ap()[:, 0:4096]
    MW2B = FR.ap()[:, 16384:20480]

    K_s = AR1.ap()
    Q_s = AR2.ap()
    V_s = AR3.ap()
    ATT = AR4.ap()
    W2B = AR1.ap()
    XW3 = AR2f
    H_F = AR3f
    XNH = AR3.ap()[:, 8192:12288]
    HID = AR3.ap()[:, 12288:16384]
    XW1 = AR4f
    OUT_STG = AR3f[:, 4096:8192]
    S1F = AR4f
    MW1B = AR3.ap()
    HID2 = [AR1.ap(), AR2.ap()]
    OBST = AR2f

    ps_free = {k: (ve, 0) for k in PSN}

    def ps_guard(name):
        s, t = ps_free[name]
        w("tensor", s, t)

    sqb_free = [pe, 0]
    hil_free = [0]

    HIL0 = r_sd.ap().bitcast(BF16)
    HIL1 = r_var.ap().bitcast(BF16)

    def bcast_hilo(row_f32, ps_out, hil, m):
        if not HILO:
            t = vop("memset", hil[0:1, 0:1], 0.0, inc=True)
            w("tensor", ve, t)
            return mm(ps_out, ones_r_f[0:1, 0:m], row_f32, True, True, inc=True)
        vchain()
        w("vector", pe, hil_free[0])
        vop("tensor_copy", hil[0:1, 0:512], row_f32)
        vchain()
        t = vop("tensor_tensor", hil[0:1, 512:1024], row_f32, hil[0:1, 0:512],
                ALU.subtract, inc=True)
        w("tensor", ve, t)
        mm(ps_out, ones_r_bf[0:1, 0:m], hil[0:1, 0:512], True, False)
        t_bc = mm(ps_out, ones_r_bf[0:1, 0:m], hil[0:1, 512:1024], False, True, inc=True)
        hil_free[0] = t_bc
        return t_bc

    vop("memset", ones_r_bf[:, :], 1.0)
    vop("memset", ones_c_bf[:, :], 1.0)
    vop("memset", ones_r_f[:, :], 1.0)
    vop("memset", ones_c_f[:, :], 1.0)
    ve_const = vop("memset", eps_c[:, :], EPS, inc=True)
    for e in ("tensor", "scalar", "sync", "gpsimd"):
        w(e, ve, ve_const)

    ln_toks = []
    for i, nm in enumerate(("lnq_s", "lnk_s", "lnv_s", "lnm_s", "lnq_b", "lnk_b", "lnv_b", "lnm_b")):
        ln_toks.append(dma("sync", lncols[:, i * DC:(i + 1) * DC], ln_d[nm]))
    t_mko = dma("sync", mko_s[:, :], mko_d)
    t_mks = dma("sync", mks_s[:, :], mks_d)
    t_bq = dma("sync", FRf[0:1, 6144:7168], bq_d)
    t_bk = dma("sync", FRf[0:1, 7168:8192], bk_d)
    t_bv = dma("sync", FRf[0:1, 8192:9216], bv_d)
    t_b1 = dma("sync", FRf[0:1, 9216:11264], b1_d)
    t_b2 = dma("sync", FRf[0:1, 11264:12288], b2_d)
    t_bo = dma("sync", FRf[0:1, 12288:13312], bo_d)
    for t in ln_toks[4:8]:
        wt("vector", t)
    ve_lnb = vop("tensor_copy", lnb_bf[:, :], lncols[:, 4 * DC:8 * DC], inc=True)
    def row_to_cols(row_f32, bcol_off, n, gate_dma=None, gate_ve=0):
        ps_guard("psD")
        wt("tensor", gate_dma)
        w("tensor", ve, gate_ve)
        for c in range(n):
            mm(psD[:, c:c + 1], row_f32[0:1, c * 128:(c + 1) * 128],
               ones_c_f[0:1, 0:1], True, True, inc=(c == n - 1))
        t = pe.n
        w("vector", pe, t)
        tk = vop("tensor_copy", bcol[:, bcol_off:bcol_off + n], psD[:, 0:n], inc=True)
        ps_free["psD"] = (ve, tk)
        return t

    row_to_cols(FRf[0:1, 11264:12288], 4 * DC, DC, gate_dma=t_b2)
    row_to_cols(FRf[0:1, 12288:13312], 5 * DC, DC, gate_dma=t_bo)

    prep = [
        (wq_d, zq_d, 0, 6144, D, 0, t_bq),
        (wk_d, zk_d, 1, 7168, D, DC, t_bk),
        (wv_d, zv_d, 2, 8192, D, None, t_bv),
    ]
    beff_ps = [psA, psB, psC, psD]
    frrow_dma = None
    frrow_pe = 0
    scale_tok = [0, 0]
    tmp_tok = [0, 0]
    store_tok = [None, None]
    for (src, zdst, gi, brow_off, dout, beff_dst, t_brow) in prep:
        nb = dout // 512
        mm_tok = 0
        for c in range(DC):
            par = c % 2
            stg = AR1f[:, par * 2048: par * 2048 + dout]
            tmp = AR2.ap()[:, par * 2048: par * 2048 + dout]
            sbf = AR2.ap()[:, 4096 + par * 2048: 4096 + par * 2048 + dout]
            w("sync", ve, scale_tok[par])
            w("sync", pe, mm_tok)
            t_ld = dma("sync", c3(stg, nb, 512), c3(src[c * 128:(c + 1) * 128, :], nb, 512))
            wt("vector", t_ld)
            wt("vector", ln_toks[gi])
            wt("vector", store_tok[par])
            w("vector", ve, ve_lnb)
            w("vector", pe, tmp_tok[par])
            ve_tmp = vop("tensor_copy", tmp, stg, inc=True)
            sc_col = lncols[:, gi * DC + c: gi * DC + c + 1]
            ve_scl = vop("tensor_scalar", sbf, stg, sc_col, None, ALU.mult, inc=True)
            scale_tok[par] = ve_scl
            w("tensor", ve, ve_tmp)
            for nbi in range(nb):
                mm_tok = mm(beff_ps[nbi][0:1, :], lnb_bf[:, gi * DC + c: gi * DC + c + 1],
                            tmp[:, nbi * 512:(nbi + 1) * 512], c == 0, c == DC - 1,
                            inc=(nbi == nb - 1))
            tmp_tok[par] = mm_tok
            w("sync", ve, ve_scl)
            store_tok[par] = dma("sync", c3(zdst[c * 128:(c + 1) * 128, :], nb, 512),
                                 c3(sbf, nb, 512))
        w("vector", pe, mm_tok)
        wt("vector", t_brow)
        wt("vector", frrow_dma)
        w("vector", pe, frrow_pe)
        ve_beff = 0
        for nbi in range(nb):
            ve_beff = vop("tensor_tensor", FRf[0:1, nbi * 512:(nbi + 1) * 512],
                          beff_ps[nbi][0:1, :],
                          FRf[0:1, brow_off + nbi * 512: brow_off + (nbi + 1) * 512],
                          ALU.add, inc=(nbi == nb - 1))
        for k in ("psA", "psB", "psC", "psD")[:nb]:
            ps_free[k] = (ve, ve_beff)
        if beff_dst is None:
            w("gpsimd", ve, ve_beff)
            frrow_dma = dma("gpsimd", zbv_d, FRf[0:1, 0:dout])
        else:
            frrow_pe = row_to_cols(FRf[0:1, 0:dout], beff_dst, dout // 128,
                                   gate_ve=ve_beff)

    wt("gpsimd", frrow_dma)
    t_w1s = dma("gpsimd", c3(W1B, DC, 2048), dcpn(w1_d))
    wt("tensor", t_w1s)
    w("tensor", ve, ve_lnb)
    ps_guard("psA")
    ps_guard("psB")
    ps_guard("psC")
    ps_guard("psD")
    b1ps = [psA, psB, psC, psD]
    for c in range(DC):
        for nbi in range(4):
            mm(b1ps[nbi][0:1, :], lnb_bf[:, 3 * DC + c: 3 * DC + c + 1],
               W1B[:, c * 2048 + nbi * 512:(c * 2048 + (nbi + 1) * 512)], c == 0,
               c == DC - 1, inc=(c == DC - 1 and nbi == 3))
    t_b1mm = pe.n
    w("vector", pe, t_b1mm)
    wt("vector", t_b1)
    ve_b1 = 0
    for nbi in range(4):
        ve_b1 = vop("tensor_tensor", FRf[0:1, 14336 + nbi * 512: 14336 + (nbi + 1) * 512],
                    b1ps[nbi][0:1, :],
                    FRf[0:1, 9216 + nbi * 512: 9216 + (nbi + 1) * 512],
                    ALU.add, inc=(nbi == 3))
    for k in ("psA", "psB", "psC", "psD"):
        ps_free[k] = (ve, ve_b1)
    row_to_cols(FRf[0:1, 14336:16384], 2 * DC, 2 * DC, gate_ve=ve_b1)

    barrier()

    def stats_begin():
        ps_guard("psC")
        ps_guard("psD")

    st_sq = []

    def stats_feed(src_chunk, sq_chunk, t0, n, first, last, prod_ve=0, prod_dma=None):
        w("vector", sqb_free[0], sqb_free[1])
        if prod_dma is not None:
            wt("vector", prod_dma)
        if prod_ve:
            w("vector", ve, prod_ve)
        t_sq = vop("tensor_tensor", sq_chunk, src_chunk, src_chunk, ALU.mult, inc=True)
        st_sq.append((sq_chunk, t0, n, t_sq, first, last))
        if prod_dma is not None:
            wt("tensor", prod_dma)
        if prod_ve:
            w("tensor", ve, prod_ve)
        mm(psC[0:1, t0:t0 + n], ones_c_f[:, :], src_chunk, first, last, inc=last)

    def stats_rows(want_G):
        w("tensor", ve, st_sq[-1][3])
        ps_guard("psD")
        for (sq_chunk, t0, n, t_sq, first, last) in st_sq:
            mm(psD[0:1, t0:t0 + n], ones_c_bf[:, :], sq_chunk, first, last, inc=last)
        st_sq.clear()
        t_sums = pe.n
        sqb_free[0], sqb_free[1] = pe, t_sums
        w("vector", pe, t_sums)
        vop("tensor_scalar", r_mu[:, :], psC[0:1, :], 1.0 / D, None, ALU.mult)
        vop("tensor_scalar", r_var[:, :], psD[0:1, :], 1.0 / D, None, ALU.mult)
        vchain()
        vop("tensor_tensor", r_sd[:, :], r_mu[:, :], r_mu[:, :], ALU.mult)
        vchain()
        t_var = vop("tensor_tensor", r_var[:, :], r_var[:, :], r_sd[:, :], ALU.subtract, inc=True)
        ps_free["psC"] = (ve, t_var)
        ps_free["psD"] = (ve, t_var)
        if want_G:
            vchain()
            vop("tensor_scalar", r_sd[:, :], r_var[:, :], 1.0 + EPS, None, ALU.mult)
            vchain()
            t_var = vop("tensor_scalar", r_var[:, :], r_sd[:, :], EPS * EPS, None,
                        ALU.add, inc=True)
        w("scalar", ve, t_var)
        w("scalar", pe, hil_free[0])
        sact(r_sd[:, :], r_var[:, :], AF.Ln,
             bias=0.0 if want_G else eps_c[0:1, :])
        w("scalar", sc, sc.n)
        t_apply_sc = sact(r_rsig[:, :], r_sd[:, :], AF.Exp, scale=-0.5, inc=True)
        return t_apply_sc

    def stats_bcast(t_apply_sc, psMu_name, psMu, psRs):
        w("vector", sc, t_apply_sc)
        ps_guard(psMu_name)
        bcast_hilo(r_mu[:, :], psMu, HIL0, 128)
        t_bc = bcast_hilo(r_rsig[:, :], psRs, HIL1, 128)
        return t_bc

    def stats_tail(want_G, psMu_name, psMu, psRs):
        return stats_bcast(stats_rows(want_G), psMu_name, psMu, psRs)

    last_oa_store = None
    px_tok = {}

    for wi in range(2):
        px_tok[(0, wi)] = dma("sync", c3(XW1[:, wi * 4096:(wi + 1) * 4096], DC, 512),
                              dcpn(xo_d[0])[:, :, wi * 512:(wi + 1) * 512])

    for bi in range(3):
        owned = bi < 2
        xsrc = xo_d[bi] if owned else xs_d
        mcol = mko_s if owned else mks_s
        qwins = list(range(NW)) if owned else [0]

        barrier()
        v_ones = vop("memset",
                     V_s.rearrange("p (t h c) -> p t h c", h=16, c=65)[:, :, :, 64:65],
                     1.0, inc=True)
        if "wv" in px_tok:
            t_wv = px_tok.pop("wv")
        else:
            t_wv = dma("sync", c3(WPS, DC, 1024), dcpn(zv_d))
        t_beff = dma("sync", BIAS[0:1, 0:1024], zbv_d)
        gidx = [0]

        def emit_V(wi):
            w("tensor", ve, v_ones)
            w("tensor", ve, xn_tok4[wi])
            wt("tensor", t_beff)
            wt("tensor", t_wv)
            for tt in range(4):
                tile = wi * 4 + tt
                for vb in range(2):
                    pname = "psA" if gidx[0] % 2 == 0 else "psB"
                    psP = PSN[pname]
                    ps_guard(pname)
                    for c in range(DC):
                        mm(psP[:, :],
                           XN[:, c * 2048 + tile * 128: c * 2048 + (tile + 1) * 128],
                           WPS[:, c * 1024 + vb * 512: c * 1024 + (vb + 1) * 512],
                           c == 0, False)
                    t_g = mm(psP[:, :], ones_r_bf[0:1, 0:128],
                             BIAS[0:1, vb * 512:(vb + 1) * 512], False, True, inc=True)
                    w("vector", pe, t_g)
                    dst = V_s.rearrange("p (t h c) -> p t h c", h=16, c=65)[
                        :, tile:tile + 1, 8 * vb:8 * (vb + 1), 0:64]
                    tk = vop("tensor_copy", dst,
                             psP.ap().rearrange("p (o h c) -> p o h c", o=1, c=64),
                             inc=True)
                    ps_free[pname] = (ve, tk)
                    gidx[0] += 1

        xn_tok4 = [0, 0, 0, 0]
        xwbuf = [XW1[:, 0:4096], XW1[:, 4096:8192],
                 XW3[:, 0:4096], XW3[:, 4096:8192]]
        pre_x = {}
        for wj in range(2, NW):
            pre_x[wj] = dma("sync", c3(xwbuf[wj], DC, 512),
                            dcpn(xsrc)[:, :, wj * 512:(wj + 1) * 512])
        xn_pend = None
        for wi in range(NW):
            xw = xwbuf[wi]
            if (bi, wi) in px_tok:
                t_x = px_tok.pop((bi, wi))
            else:
                t_x = pre_x[wi]
            stats_begin()
            for c in range(DC):
                stats_feed(xw[:, c * 512:(c + 1) * 512], SQB[:, c * 512:(c + 1) * 512],
                           0, 512, c == 0, c == DC - 1, prod_dma=t_x)
            if xn_pend is not None:
                xn_pend()
                xn_pend = None
            t_mid = stats_rows(False)
            if wi > 0:
                emit_V(wi - 1)
            t_bc = stats_bcast(t_mid, "psS0", psS0[:, 0:512], psS0[:, 512:1024])

            def xn_pend(wi=wi, xw=xw, t_bc=t_bc):
                w("vector", pe, t_bc)
                tk = 0
                for c in range(DC):
                    vop("tensor_tensor", xw[:, c * 512:(c + 1) * 512],
                        xw[:, c * 512:(c + 1) * 512], psS0[:, 0:512], ALU.subtract)
                    vchain()
                    tk = vop("tensor_tensor",
                             XN[:, c * 2048 + wi * 512: c * 2048 + (wi + 1) * 512],
                             xw[:, c * 512:(c + 1) * 512], psS0[:, 512:1024], ALU.mult,
                             inc=(c == DC - 1))
                xn_tok4[wi] = tk
                ps_free["psS0"] = (ve, tk)
        xn_pend()
        emit_V(NW - 1)
        wp_free_pe = pe.n

        for pi, (zsrc, dest) in enumerate([(zq_d, Q_s), (zk_d, K_s)]):
            wins = [0] if (bi == 2 and pi == 0) else list(range(NW))
            w("sync", pe, wp_free_pe)
            t_w = dma("sync", c3(WPS, DC, 1024), dcpn(zsrc))
            wt("tensor", t_w)
            w("tensor", ve, max(xn_tok4))
            for wi in wins:
                for oc in range(DC):
                    pname = "psA" if gidx[0] % 2 == 0 else "psB"
                    psP = PSN[pname]
                    ps_guard(pname)
                    for c in range(DC):
                        mm(psP[:, :], WPS[:, c * 1024 + oc * 128: c * 1024 + (oc + 1) * 128],
                           XN[:, c * 2048 + wi * 512: c * 2048 + (wi + 1) * 512], c == 0,
                           c == DC - 1, inc=(c == DC - 1))
                    t_g = pe.n
                    w("vector", pe, t_g)
                    tk = vop("tensor_scalar",
                             dest[:, oc * 2048 + wi * 512: oc * 2048 + (wi + 1) * 512],
                             psP[:, :], bcol[:, pi * DC + oc: pi * DC + oc + 1], None,
                             ALU.add, inc=True)
                    ps_free[pname] = (ve, tk)
                    gidx[0] += 1
            wp_free_pe = pe.n

        barrier()
        if DEBUG and bi == 0:
            dbg_q = dt("dbg_q", [128, 16384], BF16, kind="ExternalOutput").ap()
            dbg_k = dt("dbg_k", [128, 16384], BF16, kind="ExternalOutput").ap()
            dbg_v = dt("dbg_v", [128, 16640], BF16, kind="ExternalOutput").ap()
            dbg_bc = dt("dbg_bc", [128, 4 * DC], F32, kind="ExternalOutput").ap()
            dma("sync", dbg_q, Q_s[:, :])
            dma("sync", dbg_k, K_s[:, :])
            dma("sync", dbg_v, V_s[:, :])
            dma("sync", dbg_bc, bcol[:, :])
        w("gpsimd", pe, wp_free_pe)
        t_wo = dma("gpsimd", c3(WPS, DC, 1024), dcpn(wo_d))
        e_free = [0, 0]
        att_free = [ps_free["psA"], ps_free["psB"]]
        rb_read = [0, 0]
        shift_tok = None
        last_exp = [0, 0]
        pend = None
        for qb in qwins:
            odd_evac_tok = 0
            for hp in range(H // 2):
                oc = hp
                exp_toks = []
                for kt in range(16):
                    parS = kt % 2
                    psS = psS0 if parS == 0 else psS1
                    w("tensor", sc, last_exp[parS])
                    if qb == qwins[0] and hp == 0 and kt < 2:
                        ps_guard("psS0" if parS == 0 else "psS1")
                    mm(psS[:, 0:512],
                       K_s[0:64, oc * 2048 + kt * 128: oc * 2048 + (kt + 1) * 128],
                       Q_s[0:64, oc * 2048 + qb * 512: oc * 2048 + (qb + 1) * 512],
                       True, True)
                    mm(psS[:, 512:1024],
                       K_s[64:128, oc * 2048 + kt * 128: oc * 2048 + (kt + 1) * 128],
                       Q_s[64:128, oc * 2048 + qb * 512: oc * 2048 + (qb + 1) * 512],
                       True, True, inc=True)
                    s_tok = pe.n
                    mask_tok = 0
                    if qb == 0 and kt == 15:
                        w("vector", pe, s_tok)
                        wt("vector", t_mko if owned else t_mks)
                        vop("tensor_tensor", psS[:, 0:1], psS[:, 0:1],
                            mcol[:, :], ALU.add)
                        mask_tok = vop("tensor_tensor", psS[:, 512:513], psS[:, 512:513],
                                       mcol[:, :], ALU.add, inc=True)
                    w("scalar", pe, s_tok)
                    if mask_tok:
                        w("scalar", ve, mask_tok)
                    w("scalar", pe, e_free[kt // 8])
                    t_e = sact(E_BUF[kt // 8][:, (kt % 8) * 1024:(kt % 8 + 1) * 1024],
                               psS[:, :], AF.Exp, scale=1.0 / 32.0, inc=True)
                    exp_toks.append(t_e)
                    last_exp[parS] = t_e
                if pend is not None:
                    pend()
                    pend = None
                t_hilo = [0, 0]
                for h2 in range(2):
                    h = 2 * hp + h2
                    pnameA = "psA" if h2 == 0 else "psB"
                    psAtt = PSN[pnameA]
                    sA, tA = att_free[h2]
                    w("tensor", sA, tA)
                    for kt in range(16):
                        w("tensor", sc, exp_toks[kt])
                        inc = (kt == 15) or (h2 == 1 and kt == 7)
                        mm(psAtt[0:65, :],
                           V_s[:, kt * 1040 + h * 65: kt * 1040 + (h + 1) * 65],
                           E_BUF[kt // 8][:, (kt % 8) * 1024 + h2 * 512:
                                          (kt % 8) * 1024 + h2 * 512 + 512],
                           kt == 0, kt == 15, inc=inc)
                        if h2 == 1 and kt == 7:
                            e_free[0] = pe.n
                    pv_tok = pe.n
                    if h2 == 1:
                        e_free[1] = pv_tok
                    rrrow = r_G if h2 == 0 else r_rsig
                    hil = HIL0 if h2 == 0 else HIL1
                    w("vector", pe, pv_tok)
                    w("vector", pe, rb_read[h2])
                    vop("reciprocal", rrrow[0:1, :], psAtt[64:65, :])
                    vchain()
                    vop("tensor_copy", hil[0:1, 0:512], rrrow[0:1, :])
                    vchain()
                    t_hilo[h2] = vop("tensor_tensor", hil[0:1, 512:1024], rrrow[0:1, :],
                                     hil[0:1, 0:512], ALU.subtract, inc=True)

                def make_pend(hp=hp, oc=oc, t_hilo=t_hilo, qb=qb):
                    def pend_fn():
                        nonlocal odd_evac_tok, shift_tok
                        for h2 in range(2):
                            pnameA = "psA" if h2 == 0 else "psB"
                            psAtt = PSN[pnameA]
                            psbn = "psC" if h2 == 0 else "psD"
                            psb = PSN[psbn]
                            hil = HIL0 if h2 == 0 else HIL1
                            ps_guard(psbn)
                            w("tensor", ve, t_hilo[h2])
                            mm(psb[0:64, :], ones_r_bf[0:1, 0:64], hil[0:1, 0:512],
                               True, False)
                            tb = mm(psb[0:64, :], ones_r_bf[0:1, 0:64],
                                    hil[0:1, 512:1024], False, True, inc=True)
                            hil_free[0] = tb
                            rb_read[h2] = tb
                            w("vector", pe, tb)
                            vchain()
                            vop("tensor_copy", Rb_s[:, :], psb[0:64, :])
                            vchain()
                            if h2 == 0:
                                dst = ATT[0:64, oc * 2048 + qb * 512:
                                          oc * 2048 + (qb + 1) * 512]
                            else:
                                wt("vector", shift_tok)
                                dst = ATTO[0:64, oc * 512:(oc + 1) * 512]
                            tk = vop("tensor_tensor", dst, psAtt[0:64, :], Rb_s[:, :],
                                     ALU.mult, inc=True)
                            att_free[h2] = (ve, tk)
                            ps_free[pnameA] = (ve, tk)
                            ps_free[psbn] = (ve, tk)
                            if h2 == 1:
                                odd_evac_tok = tk
                    return pend_fn
                pend = make_pend()
            pend()
            pend = None
            w("sync", ve, odd_evac_tok)
            shift_tok = dma("sync",
                            c3(ATT[64:128, :], DC, 2048)[:, :, qb * 512:(qb + 1) * 512],
                            c3(ATTO[0:64, :], DC, 512))
        ps_free["psS0"] = (sc, last_exp[0])
        ps_free["psS1"] = (sc, last_exp[1])

        barrier()
        if DEBUG and bi == 0:
            dbg_att = dt("dbg_att", [128, 16384], BF16, kind="ExternalOutput").ap()
            dma("sync", dbg_att, ATT[:, :])
        t_w1 = dma("gpsimd", c3(W1B, DC, 2048), dcpn(w1_d))
        t_w2 = dma("gpsimd", c3(W2B, 16, 1024), dcpn(w2_d))
        x_tok = [xn_tok4[2], xn_tok4[3]]
        for qi, qb in enumerate(qwins):
            if PARANOID & 1:
                barrier()
            par = qi % 2
            xw = XW3[:, par * 4096:(par + 1) * 4096]
            scr = XW3[:, (1 - par) * 4096:(2 - par) * 4096]
            w("sync", ve, x_tok[par])
            t_x = dma("sync", c3(xw, DC, 512), dcpn(xsrc)[:, :, qb * 512:(qb + 1) * 512])
            wt("tensor", t_wo)
            h_tok = 0
            stats_begin()
            lagged = []
            wo_bufs = [("psA", psA), ("psB", psB), ("psD", psD),
                       ("psS0", psS0[:, 0:512])]
            for oc2 in range(DC):
                pname, psP = wo_bufs[oc2 % 4]
                ps_guard(pname)
                for c in range(DC):
                    mm(psP[:, :], WPS[:, c * 1024 + oc2 * 128: c * 1024 + (oc2 + 1) * 128],
                       ATT[:, c * 2048 + qb * 512: c * 2048 + (qb + 1) * 512], c == 0,
                       c == DC - 1, inc=(c == DC - 1))
                t_g = pe.n
                if len(lagged) >= 4:
                    lagged.pop(0)()
                w("vector", pe, t_g)
                wt("vector", t_x)
                tk = vop("scalar_tensor_tensor", H_F[:, oc2 * 512:(oc2 + 1) * 512], psP[:, :],
                         bcol[:, 5 * DC + oc2: 5 * DC + oc2 + 1],
                         xw[:, oc2 * 512:(oc2 + 1) * 512], ALU.add, ALU.add, inc=True)
                ps_free[pname] = (ve, tk)
                h_tok = tk
                lagged.append(lambda oc2=oc2, tk=tk: stats_feed(
                    H_F[:, oc2 * 512:(oc2 + 1) * 512],
                    SQB[:, oc2 * 512:(oc2 + 1) * 512], 0, 512,
                    oc2 == 0, oc2 == DC - 1, prod_ve=tk))
            for f in lagged:
                f()
            if qi == len(qwins) - 1 and bi < 2:
                nx = xo_d[bi + 1] if bi < 1 else xs_d
                w("sync", pe, pe.n)
                for wj in range(2):
                    px_tok[(bi + 1, wj)] = dma(
                        "sync", c3(XW1[:, wj * 4096:(wj + 1) * 4096], DC, 512),
                        dcpn(nx)[:, :, wj * 512:(wj + 1) * 512])
                px_tok["wv"] = dma("sync", c3(WPS, DC, 1024), dcpn(zv_d))
            t_bc = stats_tail(False, "psS0", psS0[:, 0:512], psS0[:, 512:1024])
            w("vector", pe, t_bc)
            w("vector", ve, x_tok[1 - par])
            wt("vector", last_oa_store)
            xnh_tok = 0
            xnh_toks = []
            for c in range(DC):
                vop("tensor_tensor", scr[:, c * 512:(c + 1) * 512],
                    H_F[:, c * 512:(c + 1) * 512], psS0[:, 0:512], ALU.subtract)
                vchain()
                xnh_tok = vop("scalar_tensor_tensor", XNH[:, c * 512:(c + 1) * 512],
                              scr[:, c * 512:(c + 1) * 512],
                              lncols[:, 3 * DC + c: 3 * DC + c + 1],
                              psS0[:, 512:1024], ALU.mult, ALU.mult,
                              inc=(c == DC - 1))
                xnh_toks.append(ve.n)
            x_tok[1 - par] = xnh_tok
            ps_free["psS0"] = (ve, xnh_tok)
            wt("tensor", t_w1)
            wt("tensor", t_w2)
            pre_tok = 0
            hid_free_pe = 0
            stats_begin()
            lagged2 = []
            for th in range(2):
                gel_toks = []
                for ic in range(16):
                    pname = "psA" if ic % 2 == 0 else "psB"
                    psP = PSN[pname]
                    ps_guard(pname)
                    for c in range(DC):
                        w("tensor", ve, xnh_toks[c])
                        mm(psP[:, 0:256], W1B[:, c * 2048 + ic * 128: c * 2048 + (ic + 1) * 128],
                           XNH[:, c * 512 + th * 256: c * 512 + th * 256 + 256], c == 0,
                           c == DC - 1, inc=(c == DC - 1))
                    t_g = pe.n
                    w("scalar", pe, t_g)
                    w("scalar", pe, hid_free_pe)
                    wt("scalar", last_oa_store)
                    t_gel = sact(HID[:, ic * 256:(ic + 1) * 256], psP[:, 0:256],
                                 AF.Gelu_apprx_tanh,
                                 bias=bcol[:, 2 * DC + ic: 2 * DC + ic + 1], inc=True)
                    gel_toks.append(t_gel)
                    ps_free[pname] = (sc, t_gel)
                for oc2 in range(DC):
                    pname, psP = wo_bufs[oc2 % 4]
                    ps_guard(pname)
                    w("tensor", sc, gel_toks[-1])
                    for ic in range(16):
                        mm(psP[:, 0:256],
                           W2B[:, ic * 1024 + oc2 * 128: ic * 1024 + (oc2 + 1) * 128],
                           HID[:, ic * 256:(ic + 1) * 256], ic == 0, ic == 15,
                           inc=(ic == 15))
                    t_g = pe.n
                    if len(lagged2) >= 4:
                        lagged2.pop(0)()
                    w("vector", pe, t_g)
                    off = oc2 * 512 + th * 256
                    vop("scalar_tensor_tensor", H_F[:, off:off + 256], psP[:, 0:256],
                        bcol[:, 4 * DC + oc2: 4 * DC + oc2 + 1],
                        H_F[:, off:off + 256], ALU.add, ALU.add)
                    vchain()
                    tk = vop("tensor_tensor", H_F[:, off:off + 256], H_F[:, off:off + 256],
                             xw[:, off:off + 256], ALU.add, inc=True)
                    ps_free[pname] = (ve, tk)
                    pre_tok = tk
                    lagged2.append(lambda off=off, th=th, oc2=oc2, tk=tk: stats_feed(
                        H_F[:, off:off + 256], SQB[:, off:off + 256],
                        th * 256, 256, oc2 == 0, oc2 == DC - 1, prod_ve=tk))
                hid_free_pe = pe.n
                for f in lagged2:
                    f()
                lagged2 = []
            t_bc = stats_tail(owned, "psS1", psS1[:, 0:512], psS1[:, 512:1024])
            w("vector", pe, t_bc)
            wt("vector", last_oa_store)
            out_tok = 0
            for c in range(DC):
                vop("tensor_tensor", H_F[:, c * 512:(c + 1) * 512],
                    H_F[:, c * 512:(c + 1) * 512], psS1[:, 0:512], ALU.subtract)
                vchain()
                out_tok = vop("tensor_tensor", OUT_STG[:, c * 512:(c + 1) * 512],
                              H_F[:, c * 512:(c + 1) * 512], psS1[:, 512:1024], ALU.mult,
                              inc=(c == DC - 1))
            ps_free["psS1"] = (ve, out_tok)
            x_tok[par] = out_tok
            if owned:
                w("sync", ve, out_tok)
                last_oa_store = dma("sync",
                                    dcpn(oa_d[bi])[:, :, qb * 512:(qb + 1) * 512],
                                    c3(OUT_STG[:, 0:4096], DC, 512))
            else:
                vchain()
                for c in range(DC):
                    vop("tensor_copy", S1F[:, c * 512:(c + 1) * 512],
                        OUT_STG[:, c * 512:(c + 1) * 512], inc=(c == DC - 1))

    barrier()
    vchain()
    s1_tok = 0
    for c in range(DC):
        s1_tok = vop("tensor_copy", S1BF[:, c * 512:(c + 1) * 512],
                     S1F[:, c * 512:(c + 1) * 512], inc=(c == DC - 1))
    t_ew = dma("sync", FRf[0:1, 2048:3072], ew_d)
    wt("vector", t_ew)
    ewv = FRf[0:1, 2048:3072].rearrange("a (t k) -> a t k", k=2)
    t_ms = vop("tensor_tensor", r_G[0:1, :].rearrange("a (t k) -> a t k", k=1),
               ewv[:, :, 0:1], ewv[:, :, 1:2], ALU.add, inc=True)
    w("tensor", ve, t_ms)
    ps_guard("psC")
    t_bcm = bcast_hilo(r_G[0:1, :], psC[:, :], HIL0, 128)
    w("vector", pe, t_bcm)
    t_msb = vop("tensor_copy", AR4f[:, 4096:4608], psC[:, :], inc=True)
    ps_free["psC"] = (ve, t_msb)

    mw1_free = [0, 0]
    gel_tok = []
    for icb in range(8):
        par = icb % 2
        buf = MW1B[:, par * 8192:(par + 1) * 8192]
        w("gpsimd", pe, mw1_free[par])
        t_wb = dma("gpsimd", c3(buf, DC, 1024), dcpn(mw1_d[:, icb * 1024:(icb + 1) * 1024]))
        for icl in range(8):
            pname = "psA" if (icb * 8 + icl) % 2 == 0 else "psB"
            psP = PSN[pname]
            ps_guard(pname)
            wt("tensor", t_wb)
            w("tensor", ve, s1_tok)
            for c in range(DC):
                mm(psP[:, :], buf[:, c * 1024 + icl * 128: c * 1024 + (icl + 1) * 128],
                   S1BF[:, c * 512:(c + 1) * 512], c == 0, c == DC - 1,
                   inc=(c == DC - 1))
            t_g = pe.n
            w("scalar", pe, t_g)
            ic64 = icb * 8 + icl
            dstH = HID2[ic64 // 32][:, (ic64 % 32) * 512:(ic64 % 32 + 1) * 512]
            t_gel = sact(dstH, psP[:, :], AF.Gelu_apprx_tanh, inc=True)
            gel_tok.append(t_gel)
            ps_free[pname] = (sc, t_gel)
        mw1_free[par] = pe.n

    psY = [psS0.ap()[:, 0:512], psS0.ap()[:, 512:1024], psS1.ap()[:, 0:512],
           psS1.ap()[:, 512:1024], psA.ap(), psB.ap(), psC.ap(), psD.ap()]
    for k in PSN:
        ps_guard_eng = ps_free[k]
        w("tensor", ps_guard_eng[0], ps_guard_eng[1])
    mw2_free = [0, 0, 0, 0]
    w2toks = {}
    y2_tok = 0
    for ic64 in range(64):
        for pf in range(ic64 if ic64 else 0, min(ic64 + 4, 64)):
            if pf not in w2toks:
                par_pf = pf % 4
                w("gpsimd", pe, mw2_free[par_pf])
                w2toks[pf] = dma("gpsimd", MW2B[:, par_pf * 1024:(par_pf + 1) * 1024],
                                 mw2_d[pf * 128:(pf + 1) * 128, :])
        par = ic64 % 4
        wbuf = MW2B[:, par * 1024:(par + 1) * 1024]
        wt("tensor", w2toks.pop(ic64))
        w("tensor", sc, gel_tok[ic64])
        rhs = HID2[ic64 // 32][:, (ic64 % 32) * 512:(ic64 % 32 + 1) * 512]
        for oc in range(8):
            y2_tok = mm(psY[oc], wbuf[:, oc * 128:(oc + 1) * 128], rhs,
                        ic64 == 0, ic64 == 63, inc=(oc == 7))
        mw2_free[par] = y2_tok

    w("vector", pe, y2_tok)
    w("vector", ve, t_msb)
    z_tok = 0
    for oc in range(8):
        vop("tensor_tensor", psY[oc], psY[oc], AR4f[:, 4096:4608], ALU.mult)
        vchain()
        z_tok = vop("tensor_tensor", S1F[:, oc * 512:(oc + 1) * 512], psY[oc],
                    S1F[:, oc * 512:(oc + 1) * 512], ALU.add, inc=True)
    for k in PSN:
        ps_free[k] = (ve, z_tok)
    stats_begin()
    for c in range(DC):
        stats_feed(S1F[:, c * 512:(c + 1) * 512], SQB[:, c * 512:(c + 1) * 512],
                   0, 512, c == 0, c == DC - 1, prod_ve=z_tok)
    t_bc = stats_tail(False, "psS0", psS0[:, 0:512], psS0[:, 512:1024])
    w("vector", pe, t_bc)
    ob_tok = 0
    for c in range(DC):
        vop("tensor_tensor", S1F[:, c * 512:(c + 1) * 512],
            S1F[:, c * 512:(c + 1) * 512], psS0[:, 0:512], ALU.subtract)
        vchain()
        ob_tok = vop("tensor_tensor", OBST[:, c * 512:(c + 1) * 512],
                     S1F[:, c * 512:(c + 1) * 512], psS0[:, 512:1024], ALU.mult,
                     inc=(c == DC - 1))
    w("sync", ve, ob_tok)
    dma("sync", dcpn(ob_d), c3(OBST[:, 0:4096], DC, 512))

    for s in LD + LG:
        w("sync", s, s.n)

    with nc.Block() as block:
        @block.sync
        def _(E):
            for f in ops["sync"]:
                f(E)

        @block.gpsimd
        def _(E):
            for f in ops["gpsimd"]:
                f(E)

        @block.tensor
        def _(E):
            for f in ops["tensor"]:
                f(E)

        @block.vector
        def _(E):
            for f in ops["vector"]:
                f(E)

        @block.scalar
        def _(E):
            for f in ops["scalar"]:
                f(E)

    return nc


def _prep_host(inputs):
    f = lambda k: np.asarray(inputs[k], dtype=np.float32)
    xT = np.ascontiguousarray(f("L0_attn_x").transpose(0, 2, 1))
    shared = {
        "wq": np.ascontiguousarray(f("L0_attn_Wq")),
        "wk": np.ascontiguousarray(f("L0_attn_Wk")),
        "wv": np.ascontiguousarray(f("L0_attn_Wv")),
        "wo": np.ascontiguousarray(f("L0_attn_Wo")),
        "w1": np.ascontiguousarray(f("L0_attn_W1")),
        "w2": np.ascontiguousarray(f("L0_attn_W2")),
        "mw1": np.ascontiguousarray(f("L0_ffn_w1")),
        "mw2": np.ascontiguousarray(f("L0_ffn_w2")),
        "bq": f("L0_attn_bq").reshape(1, D),
        "bk": f("L0_attn_bk").reshape(1, D),
        "bv": f("L0_attn_bv").reshape(1, D),
        "bo": f("L0_attn_bo").reshape(1, D),
        "b1": f("L0_attn_b1").reshape(1, 2 * D),
        "b2": f("L0_attn_b2").reshape(1, D),
    }
    for nm, key in (("lnq_s", "L0_attn_lnq_s"), ("lnk_s", "L0_attn_lnk_s"),
                    ("lnv_s", "L0_attn_lnv_s"), ("lnm_s", "L0_attn_lnm_s"),
                    ("lnq_b", "L0_attn_lnq_b"), ("lnk_b", "L0_attn_lnk_b"),
                    ("lnv_b", "L0_attn_lnv_b"), ("lnm_b", "L0_attn_lnm_b")):
        shared[nm] = np.ascontiguousarray(f(key).reshape(DC, 128).T)
    mko = np.zeros((128, 1), np.float32)
    mko[127, 0] = -1e9
    shared["mko"] = mko
    ew_all = f("L0_ffn_expert_weights")
    in_maps = []
    for c in range(8):
        mb, sw = c // 4, c % 4
        m = dict(shared)
        m["xo"] = np.ascontiguousarray(xT[2 * c:2 * c + 2])
        m["xs"] = np.ascontiguousarray(np.roll(xT[mb], -512 * sw, axis=1))
        m["ew"] = np.ascontiguousarray(ew_all[512 * c:512 * (c + 1)].reshape(1, 1024))
        mks = np.zeros((128, 1), np.float32)
        if sw == 0:
            mks[127, 0] = -1e9
        m["mks"] = mks
        in_maps.append(m)
    return in_maps


def kernel(**inputs):
    if "nc" not in _CACHE:
        _CACHE["nc"] = build_program()
    nc = _CACHE["nc"]
    in_maps = _prep_host(inputs)
    res = run_bass_kernel_spmd(nc, in_maps, core_ids=list(range(8)))
    out = np.empty((16, S, D), np.float32)
    for c in range(8):
        oa = res.results[c]["oa"]
        if c > 0:
            out[2 * c] = oa[0].T
            out[2 * c + 1] = oa[1].T
    flat01 = out[:2].reshape(2 * S, D)
    for c in range(8):
        ob = res.results[c]["ob"]
        flat01[512 * c:512 * (c + 1)] = ob.T
    return out


# version 17
# speedup vs baseline: 1.0572x; 1.0297x over previous
import numpy as np
import concourse.bass as bass
import concourse.mybir as mybir
from concourse.bass_utils import run_bass_kernel_spmd

F32 = mybir.dt.float32
BF16 = mybir.dt.bfloat16
AF = mybir.ActivationFunctionType
ALU = mybir.AluOpType

D = 1024
DC = 8
S = 2048
H = 16
W = 512
NW = 4
EPS = 1e-5
ENGS = ("sync", "gpsimd", "tensor", "vector", "scalar")

_CACHE = {}
DEBUG = False
HILO = True
PARANOID = 0


class Sem:
    __slots__ = ("h", "n", "name")

    def __init__(self, h, name):
        self.h = h
        self.n = 0
        self.name = name


def build_program():
    nc = bass.Bass("TRN2", target_bir_lowering=False)
    ops = {e: [] for e in ENGS}
    seen = {e: {} for e in ENGS}

    def on(eng, fn):
        ops[eng].append(fn)

    def w(eng, sem, cnt=None):
        if cnt is None:
            cnt = sem.n
        if cnt <= 0:
            return
        if seen[eng].get(sem.name, 0) >= cnt:
            return
        seen[eng][sem.name] = cnt
        on(eng, lambda E, s=sem.h, c=cnt: E.wait_ge(s, c))

    dt = nc.dram_tensor
    xo_d = dt("xo", [2, D, S], F32, kind="ExternalInput").ap()
    xs_d = dt("xs", [D, S], F32, kind="ExternalInput").ap()
    wq_d = dt("wq", [D, D], F32, kind="ExternalInput").ap()
    wk_d = dt("wk", [D, D], F32, kind="ExternalInput").ap()
    wv_d = dt("wv", [D, D], F32, kind="ExternalInput").ap()
    wo_d = dt("wo", [D, D], F32, kind="ExternalInput").ap()
    w1_d = dt("w1", [D, 2 * D], F32, kind="ExternalInput").ap()
    w2_d = dt("w2", [2 * D, D], F32, kind="ExternalInput").ap()
    mw1_d = dt("mw1", [D, 8 * D], F32, kind="ExternalInput").ap()
    mw2_d = dt("mw2", [8 * D, D], F32, kind="ExternalInput").ap()
    ln_d = {}
    for nm in ("lnq_s", "lnk_s", "lnv_s", "lnm_s", "lnq_b", "lnk_b", "lnv_b", "lnm_b"):
        ln_d[nm] = dt(nm, [128, DC], F32, kind="ExternalInput").ap()
    bq_d = dt("bq", [1, D], F32, kind="ExternalInput").ap()
    bk_d = dt("bk", [1, D], F32, kind="ExternalInput").ap()
    bv_d = dt("bv", [1, D], F32, kind="ExternalInput").ap()
    bo_d = dt("bo", [1, D], F32, kind="ExternalInput").ap()
    b1_d = dt("b1", [1, 2 * D], F32, kind="ExternalInput").ap()
    b2_d = dt("b2", [1, D], F32, kind="ExternalInput").ap()
    ew_d = dt("ew", [1, 1024], F32, kind="ExternalInput").ap()
    mko_d = dt("mko", [128, 1], F32, kind="ExternalInput").ap()
    mks_d = dt("mks", [128, 1], F32, kind="ExternalInput").ap()

    oa_d = dt("oa", [2, D, S], F32, kind="ExternalOutput").ap()
    ob_d = dt("ob", [D, W], F32, kind="ExternalOutput").ap()

    zq_d = dt("zq", [D, D], BF16, kind="Internal").ap()
    zk_d = dt("zk", [D, D], BF16, kind="Internal").ap()
    zv_d = dt("zv", [D, D], BF16, kind="Internal").ap()
    zbv_d = dt("zbv", [1, D], BF16, kind="Internal").ap()

    A = nc.alloc_sbuf_tensor
    AR1 = A("AR1", [128, 16384], BF16)
    AR2 = A("AR2", [128, 16384], BF16)
    AR3 = A("AR3", [128, 16640], BF16)
    AR4 = A("AR4", [128, 16384], BF16)
    FR = A("FR", [128, 32768], BF16)
    ones_r_bf = A("ones_r_bf", [1, 512], BF16)
    ones_c_bf = A("ones_c_bf", [128, 1], BF16)
    ones_r_f = A("ones_r_f", [1, 128], F32)
    ones_c_f = A("ones_c_f", [128, 1], F32)
    eps_c = A("eps_c", [128, 1], F32)
    mko_s = A("mko_s", [128, 1], F32)
    mks_s = A("mks_s", [128, 1], F32)
    lncols = A("lncols", [128, 8 * DC], F32)
    lnb_bf = A("lnb_bf", [128, 4 * DC], BF16)
    r_mu = A("r_mu", [1, 512], F32)
    r_rsig = A("r_rsig", [1, 512], F32)
    r_G = A("r_G", [1, 512], F32)
    r_sd = A("r_sd", [1, 512], F32)
    r_var = A("r_var", [1, 512], F32)
    Rb_s = A("Rb_s", [64, 512], F32)
    bcol = A("bcol", [128, 6 * DC], F32)

    PSAL = nc.alloc_psum_tensor
    psS0 = PSAL("psS0", [128, 1024], F32)
    psS1 = PSAL("psS1", [128, 1024], F32)
    psA = PSAL("psA", [128, 512], F32)
    psB = PSAL("psB", [128, 512], F32)
    psC = PSAL("psC", [128, 512], F32)
    psD = PSAL("psD", [128, 512], F32)
    PSN = {"psS0": psS0, "psS1": psS1, "psA": psA, "psB": psB, "psC": psC, "psD": psD}

    AR1f = AR1.ap().bitcast(F32)
    AR2f = AR2.ap().bitcast(F32)
    AR3f = AR3.ap().bitcast(F32)
    AR4f = AR4.ap().bitcast(F32)
    FRf = FR.ap().bitcast(F32)

    pe = Sem(nc.alloc_semaphore(name="pe"), "pe")
    ve = Sem(nc.alloc_semaphore(name="ve"), "ve")
    sc = Sem(nc.alloc_semaphore(name="sc"), "sc")
    LD = [Sem(nc.alloc_semaphore(name=f"ld{i}"), f"ld{i}") for i in range(6)]
    LG = [Sem(nc.alloc_semaphore(name=f"lg{i}"), f"lg{i}") for i in range(4)]
    QST = {"sync": [LD, 0], "gpsimd": [LG, 0]}
    ALLSEMS = (pe, ve, sc) + tuple(LD) + tuple(LG)

    def dma(eng, out_ap, in_ap, slow=False):
        q = QST[eng]
        s = q[0][q[1] % len(q[0])]
        q[1] += 1
        w(eng, s, s.n)
        on(eng, lambda E, o=out_ap, i=in_ap, sh=s.h, sl=slow:
           E.dma_start(out=o, in_=i, allow_slow_non_contiguous=sl).then_inc(sh, 16))
        s.n += 16
        return (s, s.n)

    def wt(eng, tok):
        if tok is not None:
            w(eng, tok[0], tok[1])

    def mm(out_ap, lhsT, rhs, start, stop, inc=False):
        if inc:
            on("tensor", lambda E, o=out_ap, l=lhsT, r=rhs, s1=start, s2=stop:
               E.matmul(o, l, r, start=s1, stop=s2).then_inc(pe.h, 1))
            pe.n += 1
            return pe.n
        on("tensor", lambda E, o=out_ap, l=lhsT, r=rhs, s1=start, s2=stop:
           E.matmul(o, l, r, start=s1, stop=s2))
        return pe.n

    def vop(name, *args, inc=True):
        on("vector", lambda E, n=name, a=args: getattr(E, n)(*a).then_inc(ve.h, 1))
        ve.n += 1
        return ve.n

    def vchain():
        w("vector", ve, ve.n)

    def sact(out_ap, in_ap, func, bias=0.0, scale=1.0, inc=True):
        on("scalar", lambda E, o=out_ap, i=in_ap, f=func, bi=bias, s=scale:
           E.activation(o, i, f, bias=bi, scale=s).then_inc(sc.h, 1))
        sc.n += 1
        return sc.n

    def barrier(engs=ENGS):
        for e in engs:
            for s in ALLSEMS:
                w(e, s, s.n)

    def c3(ap2d, c, n):
        return ap2d.rearrange("p (c n) -> p c n", c=c)

    def dcpn(ap2d):
        return ap2d.rearrange("(c p) n -> p c n", p=128)

    XN = FR.ap()[:, 0:16384]
    E_BUF = [FR.ap()[:, 0:8192], FR.ap()[:, 8192:16384]]
    W1B = FR.ap()[:, 0:16384]
    WPS = FR.ap()[:, 16384:24576]
    SQB = FR.ap()[:, 24576:28672]
    ATTO = FR.ap()[:, 24576:28672]
    BIAS = FR.ap()[:, 28672:32768]
    S1BF = FR.ap()[:, 0:4096]
    MW2B = FR.ap()[:, 16384:20480]

    K_s = AR1.ap()
    Q_s = AR2.ap()
    V_s = AR3.ap()
    ATT = AR4.ap()
    W2B = AR1.ap()
    XW3 = AR2f
    H_F = AR3f
    XNH = AR3.ap()[:, 8192:12288]
    HID = AR3.ap()[:, 12288:16384]
    XW1 = AR4f
    OUT_STG = AR3f[:, 4096:8192]
    S1F = AR4f
    MW1B = AR3.ap()
    HID2 = [AR1.ap(), AR2.ap()]
    OBST = AR2f

    ps_free = {k: (ve, 0) for k in PSN}

    def ps_guard(name):
        s, t = ps_free[name]
        w("tensor", s, t)

    sqb_free = [pe, 0]
    hil_free = [0]

    HIL0 = r_sd.ap().bitcast(BF16)
    HIL1 = r_var.ap().bitcast(BF16)

    def bcast_hilo(row_f32, ps_out, hil, m):
        if not HILO:
            t = vop("memset", hil[0:1, 0:1], 0.0, inc=True)
            w("tensor", ve, t)
            return mm(ps_out, ones_r_f[0:1, 0:m], row_f32, True, True, inc=True)
        vchain()
        w("vector", pe, hil_free[0])
        vop("tensor_copy", hil[0:1, 0:512], row_f32)
        vchain()
        t = vop("tensor_tensor", hil[0:1, 512:1024], row_f32, hil[0:1, 0:512],
                ALU.subtract, inc=True)
        w("tensor", ve, t)
        mm(ps_out, ones_r_bf[0:1, 0:m], hil[0:1, 0:512], True, False)
        t_bc = mm(ps_out, ones_r_bf[0:1, 0:m], hil[0:1, 512:1024], False, True, inc=True)
        hil_free[0] = t_bc
        return t_bc

    vop("memset", ones_r_bf[:, :], 1.0)
    vop("memset", ones_c_bf[:, :], 1.0)
    vop("memset", ones_r_f[:, :], 1.0)
    vop("memset", ones_c_f[:, :], 1.0)
    ve_const = vop("memset", eps_c[:, :], EPS, inc=True)
    for e in ("tensor", "scalar", "sync", "gpsimd"):
        w(e, ve, ve_const)

    ln_toks = []
    for i, nm in enumerate(("lnq_s", "lnk_s", "lnv_s", "lnm_s", "lnq_b", "lnk_b", "lnv_b", "lnm_b")):
        ln_toks.append(dma("sync", lncols[:, i * DC:(i + 1) * DC], ln_d[nm]))
    t_mko = dma("sync", mko_s[:, :], mko_d)
    t_mks = dma("sync", mks_s[:, :], mks_d)
    t_bq = dma("sync", FRf[0:1, 6144:7168], bq_d)
    t_bk = dma("sync", FRf[0:1, 7168:8192], bk_d)
    t_bv = dma("sync", FRf[0:1, 8192:9216], bv_d)
    t_b1 = dma("sync", FRf[0:1, 9216:11264], b1_d)
    t_b2 = dma("sync", FRf[0:1, 11264:12288], b2_d)
    t_bo = dma("sync", FRf[0:1, 12288:13312], bo_d)
    for t in ln_toks[4:8]:
        wt("vector", t)
    ve_lnb = vop("tensor_copy", lnb_bf[:, :], lncols[:, 4 * DC:8 * DC], inc=True)
    def row_to_cols(row_f32, bcol_off, n, gate_dma=None, gate_ve=0):
        ps_guard("psD")
        wt("tensor", gate_dma)
        w("tensor", ve, gate_ve)
        for c in range(n):
            mm(psD[:, c:c + 1], row_f32[0:1, c * 128:(c + 1) * 128],
               ones_c_f[0:1, 0:1], True, True, inc=(c == n - 1))
        t = pe.n
        w("vector", pe, t)
        tk = vop("tensor_copy", bcol[:, bcol_off:bcol_off + n], psD[:, 0:n], inc=True)
        ps_free["psD"] = (ve, tk)
        return t

    row_to_cols(FRf[0:1, 11264:12288], 4 * DC, DC, gate_dma=t_b2)
    row_to_cols(FRf[0:1, 12288:13312], 5 * DC, DC, gate_dma=t_bo)

    prep = [
        (wq_d, zq_d, 0, 6144, D, 0, t_bq),
        (wk_d, zk_d, 1, 7168, D, DC, t_bk),
        (wv_d, zv_d, 2, 8192, D, None, t_bv),
    ]
    beff_ps = [psA, psB, psC, psD]
    frrow_dma = None
    frrow_pe = 0
    scale_tok = [0, 0]
    tmp_tok = [0, 0]
    store_tok = [None, None]
    for (src, zdst, gi, brow_off, dout, beff_dst, t_brow) in prep:
        nb = dout // 512
        mm_tok = 0
        for c in range(DC):
            par = c % 2
            stg = AR1f[:, par * 2048: par * 2048 + dout]
            tmp = AR2.ap()[:, par * 2048: par * 2048 + dout]
            sbf = AR2.ap()[:, 4096 + par * 2048: 4096 + par * 2048 + dout]
            w("sync", ve, scale_tok[par])
            w("sync", pe, mm_tok)
            t_ld = dma("sync", c3(stg, nb, 512), c3(src[c * 128:(c + 1) * 128, :], nb, 512))
            wt("vector", t_ld)
            wt("vector", ln_toks[gi])
            wt("vector", store_tok[par])
            w("vector", ve, ve_lnb)
            w("vector", pe, tmp_tok[par])
            ve_tmp = vop("tensor_copy", tmp, stg, inc=True)
            sc_col = lncols[:, gi * DC + c: gi * DC + c + 1]
            ve_scl = vop("tensor_scalar", sbf, stg, sc_col, None, ALU.mult, inc=True)
            scale_tok[par] = ve_scl
            w("tensor", ve, ve_tmp)
            for nbi in range(nb):
                mm_tok = mm(beff_ps[nbi][0:1, :], lnb_bf[:, gi * DC + c: gi * DC + c + 1],
                            tmp[:, nbi * 512:(nbi + 1) * 512], c == 0, c == DC - 1,
                            inc=(nbi == nb - 1))
            tmp_tok[par] = mm_tok
            w("sync", ve, ve_scl)
            store_tok[par] = dma("sync", c3(zdst[c * 128:(c + 1) * 128, :], nb, 512),
                                 c3(sbf, nb, 512))
        w("vector", pe, mm_tok)
        wt("vector", t_brow)
        wt("vector", frrow_dma)
        w("vector", pe, frrow_pe)
        ve_beff = 0
        for nbi in range(nb):
            ve_beff = vop("tensor_tensor", FRf[0:1, nbi * 512:(nbi + 1) * 512],
                          beff_ps[nbi][0:1, :],
                          FRf[0:1, brow_off + nbi * 512: brow_off + (nbi + 1) * 512],
                          ALU.add, inc=(nbi == nb - 1))
        for k in ("psA", "psB", "psC", "psD")[:nb]:
            ps_free[k] = (ve, ve_beff)
        if beff_dst is None:
            w("gpsimd", ve, ve_beff)
            frrow_dma = dma("gpsimd", zbv_d, FRf[0:1, 0:dout])
        else:
            frrow_pe = row_to_cols(FRf[0:1, 0:dout], beff_dst, dout // 128,
                                   gate_ve=ve_beff)

    wt("gpsimd", frrow_dma)
    t_w1s = dma("gpsimd", c3(W1B, DC, 2048), dcpn(w1_d))
    wt("tensor", t_w1s)
    w("tensor", ve, ve_lnb)
    ps_guard("psA")
    ps_guard("psB")
    ps_guard("psC")
    ps_guard("psD")
    b1ps = [psA, psB, psC, psD]
    for c in range(DC):
        for nbi in range(4):
            mm(b1ps[nbi][0:1, :], lnb_bf[:, 3 * DC + c: 3 * DC + c + 1],
               W1B[:, c * 2048 + nbi * 512:(c * 2048 + (nbi + 1) * 512)], c == 0,
               c == DC - 1, inc=(c == DC - 1 and nbi == 3))
    t_b1mm = pe.n
    w("vector", pe, t_b1mm)
    wt("vector", t_b1)
    ve_b1 = 0
    for nbi in range(4):
        ve_b1 = vop("tensor_tensor", FRf[0:1, 14336 + nbi * 512: 14336 + (nbi + 1) * 512],
                    b1ps[nbi][0:1, :],
                    FRf[0:1, 9216 + nbi * 512: 9216 + (nbi + 1) * 512],
                    ALU.add, inc=(nbi == 3))
    for k in ("psA", "psB", "psC", "psD"):
        ps_free[k] = (ve, ve_b1)
    row_to_cols(FRf[0:1, 14336:16384], 2 * DC, 2 * DC, gate_ve=ve_b1)

    barrier()

    def stats_begin():
        ps_guard("psC")
        ps_guard("psD")

    st_sq = []

    def stats_feed(src_chunk, sq_chunk, t0, n, first, last, prod_ve=0, prod_dma=None):
        w("vector", sqb_free[0], sqb_free[1])
        if prod_dma is not None:
            wt("vector", prod_dma)
        if prod_ve:
            w("vector", ve, prod_ve)
        t_sq = vop("tensor_tensor", sq_chunk, src_chunk, src_chunk, ALU.mult, inc=True)
        st_sq.append((sq_chunk, t0, n, t_sq, first, last))
        if prod_dma is not None:
            wt("tensor", prod_dma)
        if prod_ve:
            w("tensor", ve, prod_ve)
        mm(psC[0:1, t0:t0 + n], ones_c_f[:, :], src_chunk, first, last, inc=last)

    def stats_rows(want_G):
        w("tensor", ve, st_sq[-1][3])
        ps_guard("psD")
        for (sq_chunk, t0, n, t_sq, first, last) in st_sq:
            mm(psD[0:1, t0:t0 + n], ones_c_bf[:, :], sq_chunk, first, last, inc=last)
        st_sq.clear()
        t_sums = pe.n
        sqb_free[0], sqb_free[1] = pe, t_sums
        w("vector", pe, t_sums)
        vop("tensor_scalar", r_mu[:, :], psC[0:1, :], 1.0 / D, None, ALU.mult)
        vop("tensor_scalar", r_var[:, :], psD[0:1, :], 1.0 / D, None, ALU.mult)
        vchain()
        vop("tensor_tensor", r_sd[:, :], r_mu[:, :], r_mu[:, :], ALU.mult)
        vchain()
        t_var = vop("tensor_tensor", r_var[:, :], r_var[:, :], r_sd[:, :], ALU.subtract, inc=True)
        ps_free["psC"] = (ve, t_var)
        ps_free["psD"] = (ve, t_var)
        if want_G:
            vchain()
            vop("tensor_scalar", r_sd[:, :], r_var[:, :], 1.0 + EPS, None, ALU.mult)
            vchain()
            t_var = vop("tensor_scalar", r_var[:, :], r_sd[:, :], EPS * EPS, None,
                        ALU.add, inc=True)
        w("scalar", ve, t_var)
        w("scalar", pe, hil_free[0])
        sact(r_sd[:, :], r_var[:, :], AF.Ln,
             bias=0.0 if want_G else eps_c[0:1, :])
        w("scalar", sc, sc.n)
        t_apply_sc = sact(r_rsig[:, :], r_sd[:, :], AF.Exp, scale=-0.5, inc=True)
        return t_apply_sc

    def stats_bcast(t_apply_sc, psMu_name, psMu, psRs):
        w("vector", sc, t_apply_sc)
        ps_guard(psMu_name)
        bcast_hilo(r_mu[:, :], psMu, HIL0, 128)
        t_bc = bcast_hilo(r_rsig[:, :], psRs, HIL1, 128)
        return t_bc

    def stats_tail(want_G, psMu_name, psMu, psRs):
        return stats_bcast(stats_rows(want_G), psMu_name, psMu, psRs)

    last_oa_store = None
    px_tok = {}

    for wi in range(2):
        px_tok[(0, wi)] = dma("sync", c3(XW1[:, wi * 4096:(wi + 1) * 4096], DC, 512),
                              dcpn(xo_d[0])[:, :, wi * 512:(wi + 1) * 512])

    for bi in range(3):
        owned = bi < 2
        xsrc = xo_d[bi] if owned else xs_d
        mcol = mko_s if owned else mks_s
        qwins = list(range(NW)) if owned else [0]

        barrier()
        v_ones = vop("memset",
                     V_s.rearrange("p (t h c) -> p t h c", h=16, c=65)[:, :, :, 64:65],
                     1.0, inc=True)
        if "wv" in px_tok:
            t_wv = px_tok.pop("wv")
        else:
            t_wv = dma("sync", c3(WPS, DC, 1024), dcpn(zv_d))
        t_beff = dma("sync", BIAS[0:1, 0:1024], zbv_d)
        gidx = [0]

        def emit_V(wi):
            w("tensor", ve, v_ones)
            w("tensor", ve, xn_tok4[wi])
            wt("tensor", t_beff)
            wt("tensor", t_wv)
            for tt in range(4):
                tile = wi * 4 + tt
                for vb in range(2):
                    pname = "psA" if gidx[0] % 2 == 0 else "psB"
                    psP = PSN[pname]
                    ps_guard(pname)
                    for c in range(DC):
                        mm(psP[:, :],
                           XN[:, c * 2048 + tile * 128: c * 2048 + (tile + 1) * 128],
                           WPS[:, c * 1024 + vb * 512: c * 1024 + (vb + 1) * 512],
                           c == 0, False)
                    t_g = mm(psP[:, :], ones_r_bf[0:1, 0:128],
                             BIAS[0:1, vb * 512:(vb + 1) * 512], False, True, inc=True)
                    w("vector", pe, t_g)
                    dst = V_s.rearrange("p (t h c) -> p t h c", h=16, c=65)[
                        :, tile:tile + 1, 8 * vb:8 * (vb + 1), 0:64]
                    tk = vop("tensor_copy", dst,
                             psP.ap().rearrange("p (o h c) -> p o h c", o=1, c=64),
                             inc=True)
                    ps_free[pname] = (ve, tk)
                    gidx[0] += 1

        xn_tok4 = [0, 0, 0, 0]
        xwbuf = [XW1[:, 0:4096], XW1[:, 4096:8192],
                 XW3[:, 0:4096], XW3[:, 4096:8192]]
        pre_x = {}
        for wj in range(2, NW):
            pre_x[wj] = dma("sync", c3(xwbuf[wj], DC, 512),
                            dcpn(xsrc)[:, :, wj * 512:(wj + 1) * 512])
        xn_pend = None
        for wi in range(NW):
            xw = xwbuf[wi]
            if (bi, wi) in px_tok:
                t_x = px_tok.pop((bi, wi))
            else:
                t_x = pre_x[wi]
            stats_begin()
            for c in range(DC):
                stats_feed(xw[:, c * 512:(c + 1) * 512], SQB[:, c * 512:(c + 1) * 512],
                           0, 512, c == 0, c == DC - 1, prod_dma=t_x)
            if xn_pend is not None:
                xn_pend()
                xn_pend = None
            t_mid = stats_rows(False)
            if wi > 0:
                emit_V(wi - 1)
            t_bc = stats_bcast(t_mid, "psS0", psS0[:, 0:512], psS0[:, 512:1024])

            def xn_pend(wi=wi, xw=xw, t_bc=t_bc):
                w("vector", pe, t_bc)
                tk = 0
                for c in range(DC):
                    vop("tensor_tensor", xw[:, c * 512:(c + 1) * 512],
                        xw[:, c * 512:(c + 1) * 512], psS0[:, 0:512], ALU.subtract)
                    vchain()
                    tk = vop("tensor_tensor",
                             XN[:, c * 2048 + wi * 512: c * 2048 + (wi + 1) * 512],
                             xw[:, c * 512:(c + 1) * 512], psS0[:, 512:1024], ALU.mult,
                             inc=(c == DC - 1))
                xn_tok4[wi] = tk
                ps_free["psS0"] = (ve, tk)
        xn_pend()
        emit_V(NW - 1)
        wp_free_pe = pe.n

        for pi, (zsrc, dest) in enumerate([(zq_d, Q_s), (zk_d, K_s)]):
            wins = [0] if (bi == 2 and pi == 0) else list(range(NW))
            w("sync", pe, wp_free_pe)
            t_w = dma("sync", c3(WPS, DC, 1024), dcpn(zsrc))
            wt("tensor", t_w)
            w("tensor", ve, max(xn_tok4))
            for wi in wins:
                for oc in range(DC):
                    pname = "psA" if gidx[0] % 2 == 0 else "psB"
                    psP = PSN[pname]
                    ps_guard(pname)
                    for c in range(DC):
                        mm(psP[:, :], WPS[:, c * 1024 + oc * 128: c * 1024 + (oc + 1) * 128],
                           XN[:, c * 2048 + wi * 512: c * 2048 + (wi + 1) * 512], c == 0,
                           c == DC - 1, inc=(c == DC - 1))
                    t_g = pe.n
                    w("vector", pe, t_g)
                    tk = vop("tensor_scalar",
                             dest[:, oc * 2048 + wi * 512: oc * 2048 + (wi + 1) * 512],
                             psP[:, :], bcol[:, pi * DC + oc: pi * DC + oc + 1], None,
                             ALU.add, inc=True)
                    ps_free[pname] = (ve, tk)
                    gidx[0] += 1
            wp_free_pe = pe.n

        barrier()
        if DEBUG and bi == 0:
            dbg_q = dt("dbg_q", [128, 16384], BF16, kind="ExternalOutput").ap()
            dbg_k = dt("dbg_k", [128, 16384], BF16, kind="ExternalOutput").ap()
            dbg_v = dt("dbg_v", [128, 16640], BF16, kind="ExternalOutput").ap()
            dbg_bc = dt("dbg_bc", [128, 4 * DC], F32, kind="ExternalOutput").ap()
            dma("sync", dbg_q, Q_s[:, :])
            dma("sync", dbg_k, K_s[:, :])
            dma("sync", dbg_v, V_s[:, :])
            dma("sync", dbg_bc, bcol[:, :])
        w("gpsimd", pe, wp_free_pe)
        t_wo = dma("gpsimd", c3(WPS, DC, 1024), dcpn(wo_d))
        e_free = [0, 0]
        att_free = [ps_free["psA"], ps_free["psB"]]
        rb_read = [0, 0]
        shift_tok = None
        last_exp = [0, 0]
        pend = None
        for qb in qwins:
            odd_evac_tok = 0
            for hp in range(H // 2):
                oc = hp
                exp_toks = []
                for kt in range(16):
                    parS = kt % 2
                    psS = psS0 if parS == 0 else psS1
                    w("tensor", sc, last_exp[parS])
                    if qb == qwins[0] and hp == 0 and kt < 2:
                        ps_guard("psS0" if parS == 0 else "psS1")
                    mm(psS[:, 0:512],
                       K_s[0:64, oc * 2048 + kt * 128: oc * 2048 + (kt + 1) * 128],
                       Q_s[0:64, oc * 2048 + qb * 512: oc * 2048 + (qb + 1) * 512],
                       True, True)
                    mm(psS[:, 512:1024],
                       K_s[64:128, oc * 2048 + kt * 128: oc * 2048 + (kt + 1) * 128],
                       Q_s[64:128, oc * 2048 + qb * 512: oc * 2048 + (qb + 1) * 512],
                       True, True, inc=True)
                    s_tok = pe.n
                    mask_tok = 0
                    if qb == 0 and kt == 15:
                        w("vector", pe, s_tok)
                        wt("vector", t_mko if owned else t_mks)
                        vop("tensor_tensor", psS[:, 0:1], psS[:, 0:1],
                            mcol[:, :], ALU.add)
                        mask_tok = vop("tensor_tensor", psS[:, 512:513], psS[:, 512:513],
                                       mcol[:, :], ALU.add, inc=True)
                    w("scalar", pe, s_tok)
                    if mask_tok:
                        w("scalar", ve, mask_tok)
                    w("scalar", pe, e_free[kt // 8])
                    t_e = sact(E_BUF[kt // 8][:, (kt % 8) * 1024:(kt % 8 + 1) * 1024],
                               psS[:, :], AF.Exp, scale=1.0 / 32.0, inc=True)
                    exp_toks.append(t_e)
                    last_exp[parS] = t_e
                if pend is not None:
                    pend()
                    pend = None
                t_hilo = [0, 0]
                for h2 in range(2):
                    h = 2 * hp + h2
                    pnameA = "psA" if h2 == 0 else "psB"
                    psAtt = PSN[pnameA]
                    sA, tA = att_free[h2]
                    w("tensor", sA, tA)
                    for kt in range(16):
                        w("tensor", sc, exp_toks[kt])
                        inc = (kt == 15) or (h2 == 1 and kt == 7)
                        mm(psAtt[0:65, :],
                           V_s[:, kt * 1040 + h * 65: kt * 1040 + (h + 1) * 65],
                           E_BUF[kt // 8][:, (kt % 8) * 1024 + h2 * 512:
                                          (kt % 8) * 1024 + h2 * 512 + 512],
                           kt == 0, kt == 15, inc=inc)
                        if h2 == 1 and kt == 7:
                            e_free[0] = pe.n
                    pv_tok = pe.n
                    if h2 == 1:
                        e_free[1] = pv_tok
                    rrrow = r_G if h2 == 0 else r_rsig
                    hil = HIL0 if h2 == 0 else HIL1
                    w("vector", pe, pv_tok)
                    w("vector", pe, rb_read[h2])
                    vop("reciprocal", rrrow[0:1, :], psAtt[64:65, :])
                    vchain()
                    vop("tensor_copy", hil[0:1, 0:512], rrrow[0:1, :])
                    vchain()
                    t_hilo[h2] = vop("tensor_tensor", hil[0:1, 512:1024], rrrow[0:1, :],
                                     hil[0:1, 0:512], ALU.subtract, inc=True)

                def make_pend(hp=hp, oc=oc, t_hilo=t_hilo, qb=qb):
                    def pend_fn():
                        nonlocal odd_evac_tok, shift_tok
                        for h2 in range(2):
                            pnameA = "psA" if h2 == 0 else "psB"
                            psAtt = PSN[pnameA]
                            psbn = "psC" if h2 == 0 else "psD"
                            psb = PSN[psbn]
                            hil = HIL0 if h2 == 0 else HIL1
                            ps_guard(psbn)
                            w("tensor", ve, t_hilo[h2])
                            mm(psb[0:64, :], ones_r_bf[0:1, 0:64], hil[0:1, 0:512],
                               True, False)
                            tb = mm(psb[0:64, :], ones_r_bf[0:1, 0:64],
                                    hil[0:1, 512:1024], False, True, inc=True)
                            hil_free[0] = tb
                            rb_read[h2] = tb
                            w("vector", pe, tb)
                            vchain()
                            vop("tensor_copy", Rb_s[:, :], psb[0:64, :])
                            vchain()
                            if h2 == 0:
                                dst = ATT[0:64, oc * 2048 + qb * 512:
                                          oc * 2048 + (qb + 1) * 512]
                            else:
                                wt("vector", shift_tok)
                                dst = ATTO[0:64, oc * 512:(oc + 1) * 512]
                            tk = vop("tensor_tensor", dst, psAtt[0:64, :], Rb_s[:, :],
                                     ALU.mult, inc=True)
                            att_free[h2] = (ve, tk)
                            ps_free[pnameA] = (ve, tk)
                            ps_free[psbn] = (ve, tk)
                            if h2 == 1:
                                odd_evac_tok = tk
                    return pend_fn
                pend = make_pend()
            pend()
            pend = None
            w("sync", ve, odd_evac_tok)
            shift_tok = dma("sync",
                            c3(ATT[64:128, :], DC, 2048)[:, :, qb * 512:(qb + 1) * 512],
                            c3(ATTO[0:64, :], DC, 512))
        ps_free["psS0"] = (sc, last_exp[0])
        ps_free["psS1"] = (sc, last_exp[1])

        barrier()
        if DEBUG and bi == 0:
            dbg_att = dt("dbg_att", [128, 16384], BF16, kind="ExternalOutput").ap()
            dma("sync", dbg_att, ATT[:, :])
        t_w1 = dma("gpsimd", c3(W1B, DC, 2048), dcpn(w1_d))
        t_w2 = dma("gpsimd", c3(W2B, 16, 1024), dcpn(w2_d))
        x_tok = [xn_tok4[2], xn_tok4[3]]
        oa_pend = []
        for qi, qb in enumerate(qwins):
            if PARANOID & 1:
                barrier()
            par = qi % 2
            xw = XW3[:, par * 4096:(par + 1) * 4096]
            scr = XW3[:, (1 - par) * 4096:(2 - par) * 4096]
            w("sync", ve, x_tok[par])
            t_x = dma("sync", c3(xw, DC, 512), dcpn(xsrc)[:, :, qb * 512:(qb + 1) * 512])
            wt("tensor", t_wo)
            h_tok = 0
            stats_begin()
            lagged = []
            wo_bufs = [("psA", psA), ("psB", psB), ("psD", psD),
                       ("psS0", psS0[:, 0:512])]
            for oc2 in range(DC):
                pname, psP = wo_bufs[oc2 % 4]
                ps_guard(pname)
                for c in range(DC):
                    mm(psP[:, :], WPS[:, c * 1024 + oc2 * 128: c * 1024 + (oc2 + 1) * 128],
                       ATT[:, c * 2048 + qb * 512: c * 2048 + (qb + 1) * 512], c == 0,
                       c == DC - 1, inc=(c == DC - 1))
                t_g = pe.n
                if len(lagged) >= 4:
                    lagged.pop(0)()
                if oa_pend:
                    oa_pend.pop(0)()
                    vchain()
                w("vector", pe, t_g)
                wt("vector", t_x)
                tk = vop("scalar_tensor_tensor", H_F[:, oc2 * 512:(oc2 + 1) * 512], psP[:, :],
                         bcol[:, 5 * DC + oc2: 5 * DC + oc2 + 1],
                         xw[:, oc2 * 512:(oc2 + 1) * 512], ALU.add, ALU.add, inc=True)
                ps_free[pname] = (ve, tk)
                h_tok = tk
                lagged.append(lambda oc2=oc2, tk=tk: stats_feed(
                    H_F[:, oc2 * 512:(oc2 + 1) * 512],
                    SQB[:, oc2 * 512:(oc2 + 1) * 512], 0, 512,
                    oc2 == 0, oc2 == DC - 1, prod_ve=tk))
            for f in lagged:
                f()
            if qi == len(qwins) - 1 and bi < 2:
                nx = xo_d[bi + 1] if bi < 1 else xs_d
                w("sync", pe, pe.n)
                for wj in range(2):
                    px_tok[(bi + 1, wj)] = dma(
                        "sync", c3(XW1[:, wj * 4096:(wj + 1) * 4096], DC, 512),
                        dcpn(nx)[:, :, wj * 512:(wj + 1) * 512])
                px_tok["wv"] = dma("sync", c3(WPS, DC, 1024), dcpn(zv_d))
            t_bc = stats_tail(False, "psS0", psS0[:, 0:512], psS0[:, 512:1024])
            w("vector", pe, t_bc)
            w("vector", ve, x_tok[1 - par])
            wt("vector", last_oa_store)
            xnh_tok = 0
            xnh_toks = []
            for c in range(DC):
                vop("tensor_tensor", scr[:, c * 512:(c + 1) * 512],
                    H_F[:, c * 512:(c + 1) * 512], psS0[:, 0:512], ALU.subtract)
                vchain()
                xnh_tok = vop("scalar_tensor_tensor", XNH[:, c * 512:(c + 1) * 512],
                              scr[:, c * 512:(c + 1) * 512],
                              lncols[:, 3 * DC + c: 3 * DC + c + 1],
                              psS0[:, 512:1024], ALU.mult, ALU.mult,
                              inc=(c == DC - 1))
                xnh_toks.append(ve.n)
            x_tok[1 - par] = xnh_tok
            ps_free["psS0"] = (ve, xnh_tok)
            wt("tensor", t_w1)
            wt("tensor", t_w2)
            pre_tok = 0
            hid_free_pe = 0
            stats_begin()
            lagged2 = []
            for th in range(2):
                gel_toks = []
                for ic in range(16):
                    pname = "psA" if ic % 2 == 0 else "psB"
                    psP = PSN[pname]
                    ps_guard(pname)
                    for c in range(DC):
                        w("tensor", ve, xnh_toks[c])
                        mm(psP[:, 0:256], W1B[:, c * 2048 + ic * 128: c * 2048 + (ic + 1) * 128],
                           XNH[:, c * 512 + th * 256: c * 512 + th * 256 + 256], c == 0,
                           c == DC - 1, inc=(c == DC - 1))
                    t_g = pe.n
                    w("scalar", pe, t_g)
                    w("scalar", pe, hid_free_pe)
                    wt("scalar", last_oa_store)
                    t_gel = sact(HID[:, ic * 256:(ic + 1) * 256], psP[:, 0:256],
                                 AF.Gelu_apprx_tanh,
                                 bias=bcol[:, 2 * DC + ic: 2 * DC + ic + 1], inc=True)
                    gel_toks.append(t_gel)
                    ps_free[pname] = (sc, t_gel)
                for oc2 in range(DC):
                    pname, psP = wo_bufs[oc2 % 4]
                    ps_guard(pname)
                    w("tensor", sc, gel_toks[-1])
                    for ic in range(16):
                        mm(psP[:, 0:256],
                           W2B[:, ic * 1024 + oc2 * 128: ic * 1024 + (oc2 + 1) * 128],
                           HID[:, ic * 256:(ic + 1) * 256], ic == 0, ic == 15,
                           inc=(ic == 15))
                    t_g = pe.n
                    if len(lagged2) >= 4:
                        lagged2.pop(0)()
                    w("vector", pe, t_g)
                    off = oc2 * 512 + th * 256
                    vop("scalar_tensor_tensor", H_F[:, off:off + 256], psP[:, 0:256],
                        bcol[:, 4 * DC + oc2: 4 * DC + oc2 + 1],
                        H_F[:, off:off + 256], ALU.add, ALU.add)
                    vchain()
                    tk = vop("tensor_tensor", H_F[:, off:off + 256], H_F[:, off:off + 256],
                             xw[:, off:off + 256], ALU.add, inc=True)
                    ps_free[pname] = (ve, tk)
                    pre_tok = tk
                    lagged2.append(lambda off=off, th=th, oc2=oc2, tk=tk: stats_feed(
                        H_F[:, off:off + 256], SQB[:, off:off + 256],
                        th * 256, 256, oc2 == 0, oc2 == DC - 1, prod_ve=tk))
                hid_free_pe = pe.n
                for f in lagged2:
                    f()
                lagged2 = []
            t_bc = stats_tail(owned, "psS1", psS1[:, 0:512], psS1[:, 512:1024])

            def oa_piece(c, par=par, qb=qb, t_bc=t_bc, owned=owned):
                nonlocal last_oa_store
                if c == 0:
                    w("vector", pe, t_bc)
                    wt("vector", last_oa_store)
                vop("tensor_tensor", H_F[:, c * 512:(c + 1) * 512],
                    H_F[:, c * 512:(c + 1) * 512], psS1[:, 0:512], ALU.subtract)
                vchain()
                out_tok = vop("tensor_tensor", OUT_STG[:, c * 512:(c + 1) * 512],
                              H_F[:, c * 512:(c + 1) * 512], psS1[:, 512:1024],
                              ALU.mult, inc=True)
                if c == DC - 1:
                    ps_free["psS1"] = (ve, out_tok)
                    x_tok[par] = out_tok
                    if owned:
                        w("sync", ve, out_tok)
                        last_oa_store = dma(
                            "sync", dcpn(oa_d[bi])[:, :, qb * 512:(qb + 1) * 512],
                            c3(OUT_STG[:, 0:4096], DC, 512))
                    else:
                        vchain()
                        for cc in range(DC):
                            vop("tensor_copy", S1F[:, cc * 512:(cc + 1) * 512],
                                OUT_STG[:, cc * 512:(cc + 1) * 512],
                                inc=(cc == DC - 1))

            oa_pend = [(lambda c=c: oa_piece(c)) for c in range(DC)]
        for f in oa_pend:
            f()
        oa_pend = []

    barrier()
    vchain()
    s1_tok = 0
    for c in range(DC):
        s1_tok = vop("tensor_copy", S1BF[:, c * 512:(c + 1) * 512],
                     S1F[:, c * 512:(c + 1) * 512], inc=(c == DC - 1))
    t_ew = dma("sync", FRf[0:1, 2048:3072], ew_d)
    wt("vector", t_ew)
    ewv = FRf[0:1, 2048:3072].rearrange("a (t k) -> a t k", k=2)
    t_ms = vop("tensor_tensor", r_G[0:1, :].rearrange("a (t k) -> a t k", k=1),
               ewv[:, :, 0:1], ewv[:, :, 1:2], ALU.add, inc=True)
    w("tensor", ve, t_ms)
    ps_guard("psC")
    t_bcm = bcast_hilo(r_G[0:1, :], psC[:, :], HIL0, 128)
    w("vector", pe, t_bcm)
    t_msb = vop("tensor_copy", AR4f[:, 4096:4608], psC[:, :], inc=True)
    ps_free["psC"] = (ve, t_msb)

    mw1_free = [0, 0]
    gel_tok = []
    for icb in range(8):
        par = icb % 2
        buf = MW1B[:, par * 8192:(par + 1) * 8192]
        w("gpsimd", pe, mw1_free[par])
        t_wb = dma("gpsimd", c3(buf, DC, 1024), dcpn(mw1_d[:, icb * 1024:(icb + 1) * 1024]))
        for icl in range(8):
            pname = "psA" if (icb * 8 + icl) % 2 == 0 else "psB"
            psP = PSN[pname]
            ps_guard(pname)
            wt("tensor", t_wb)
            w("tensor", ve, s1_tok)
            for c in range(DC):
                mm(psP[:, :], buf[:, c * 1024 + icl * 128: c * 1024 + (icl + 1) * 128],
                   S1BF[:, c * 512:(c + 1) * 512], c == 0, c == DC - 1,
                   inc=(c == DC - 1))
            t_g = pe.n
            w("scalar", pe, t_g)
            ic64 = icb * 8 + icl
            dstH = HID2[ic64 // 32][:, (ic64 % 32) * 512:(ic64 % 32 + 1) * 512]
            t_gel = sact(dstH, psP[:, :], AF.Gelu_apprx_tanh, inc=True)
            gel_tok.append(t_gel)
            ps_free[pname] = (sc, t_gel)
        mw1_free[par] = pe.n

    psY = [psS0.ap()[:, 0:512], psS0.ap()[:, 512:1024], psS1.ap()[:, 0:512],
           psS1.ap()[:, 512:1024], psA.ap(), psB.ap(), psC.ap(), psD.ap()]
    for k in PSN:
        ps_guard_eng = ps_free[k]
        w("tensor", ps_guard_eng[0], ps_guard_eng[1])
    mw2_free = [0, 0, 0, 0]
    w2toks = {}
    y2_tok = 0
    for ic64 in range(64):
        for pf in range(ic64 if ic64 else 0, min(ic64 + 4, 64)):
            if pf not in w2toks:
                par_pf = pf % 4
                w("gpsimd", pe, mw2_free[par_pf])
                w2toks[pf] = dma("gpsimd", MW2B[:, par_pf * 1024:(par_pf + 1) * 1024],
                                 mw2_d[pf * 128:(pf + 1) * 128, :])
        par = ic64 % 4
        wbuf = MW2B[:, par * 1024:(par + 1) * 1024]
        wt("tensor", w2toks.pop(ic64))
        w("tensor", sc, gel_tok[ic64])
        rhs = HID2[ic64 // 32][:, (ic64 % 32) * 512:(ic64 % 32 + 1) * 512]
        for oc in range(8):
            y2_tok = mm(psY[oc], wbuf[:, oc * 128:(oc + 1) * 128], rhs,
                        ic64 == 0, ic64 == 63, inc=(oc == 7))
        mw2_free[par] = y2_tok

    w("vector", pe, y2_tok)
    w("vector", ve, t_msb)
    z_tok = 0
    for oc in range(8):
        vop("tensor_tensor", psY[oc], psY[oc], AR4f[:, 4096:4608], ALU.mult)
        vchain()
        z_tok = vop("tensor_tensor", S1F[:, oc * 512:(oc + 1) * 512], psY[oc],
                    S1F[:, oc * 512:(oc + 1) * 512], ALU.add, inc=True)
    for k in PSN:
        ps_free[k] = (ve, z_tok)
    stats_begin()
    for c in range(DC):
        stats_feed(S1F[:, c * 512:(c + 1) * 512], SQB[:, c * 512:(c + 1) * 512],
                   0, 512, c == 0, c == DC - 1, prod_ve=z_tok)
    t_bc = stats_tail(False, "psS0", psS0[:, 0:512], psS0[:, 512:1024])
    w("vector", pe, t_bc)
    ob_tok = 0
    for c in range(DC):
        vop("tensor_tensor", S1F[:, c * 512:(c + 1) * 512],
            S1F[:, c * 512:(c + 1) * 512], psS0[:, 0:512], ALU.subtract)
        vchain()
        ob_tok = vop("tensor_tensor", OBST[:, c * 512:(c + 1) * 512],
                     S1F[:, c * 512:(c + 1) * 512], psS0[:, 512:1024], ALU.mult,
                     inc=(c == DC - 1))
    w("sync", ve, ob_tok)
    dma("sync", dcpn(ob_d), c3(OBST[:, 0:4096], DC, 512))

    for s in LD + LG:
        w("sync", s, s.n)

    with nc.Block() as block:
        @block.sync
        def _(E):
            for f in ops["sync"]:
                f(E)

        @block.gpsimd
        def _(E):
            for f in ops["gpsimd"]:
                f(E)

        @block.tensor
        def _(E):
            for f in ops["tensor"]:
                f(E)

        @block.vector
        def _(E):
            for f in ops["vector"]:
                f(E)

        @block.scalar
        def _(E):
            for f in ops["scalar"]:
                f(E)

    return nc


def _prep_host(inputs):
    f = lambda k: np.asarray(inputs[k], dtype=np.float32)
    xT = np.ascontiguousarray(f("L0_attn_x").transpose(0, 2, 1))
    shared = {
        "wq": np.ascontiguousarray(f("L0_attn_Wq")),
        "wk": np.ascontiguousarray(f("L0_attn_Wk")),
        "wv": np.ascontiguousarray(f("L0_attn_Wv")),
        "wo": np.ascontiguousarray(f("L0_attn_Wo")),
        "w1": np.ascontiguousarray(f("L0_attn_W1")),
        "w2": np.ascontiguousarray(f("L0_attn_W2")),
        "mw1": np.ascontiguousarray(f("L0_ffn_w1")),
        "mw2": np.ascontiguousarray(f("L0_ffn_w2")),
        "bq": f("L0_attn_bq").reshape(1, D),
        "bk": f("L0_attn_bk").reshape(1, D),
        "bv": f("L0_attn_bv").reshape(1, D),
        "bo": f("L0_attn_bo").reshape(1, D),
        "b1": f("L0_attn_b1").reshape(1, 2 * D),
        "b2": f("L0_attn_b2").reshape(1, D),
    }
    for nm, key in (("lnq_s", "L0_attn_lnq_s"), ("lnk_s", "L0_attn_lnk_s"),
                    ("lnv_s", "L0_attn_lnv_s"), ("lnm_s", "L0_attn_lnm_s"),
                    ("lnq_b", "L0_attn_lnq_b"), ("lnk_b", "L0_attn_lnk_b"),
                    ("lnv_b", "L0_attn_lnv_b"), ("lnm_b", "L0_attn_lnm_b")):
        shared[nm] = np.ascontiguousarray(f(key).reshape(DC, 128).T)
    mko = np.zeros((128, 1), np.float32)
    mko[127, 0] = -1e9
    shared["mko"] = mko
    ew_all = f("L0_ffn_expert_weights")
    in_maps = []
    for c in range(8):
        mb, sw = c // 4, c % 4
        m = dict(shared)
        m["xo"] = np.ascontiguousarray(xT[2 * c:2 * c + 2])
        m["xs"] = np.ascontiguousarray(np.roll(xT[mb], -512 * sw, axis=1))
        m["ew"] = np.ascontiguousarray(ew_all[512 * c:512 * (c + 1)].reshape(1, 1024))
        mks = np.zeros((128, 1), np.float32)
        if sw == 0:
            mks[127, 0] = -1e9
        m["mks"] = mks
        in_maps.append(m)
    return in_maps


def kernel(**inputs):
    if "nc" not in _CACHE:
        _CACHE["nc"] = build_program()
    nc = _CACHE["nc"]
    in_maps = _prep_host(inputs)
    res = run_bass_kernel_spmd(nc, in_maps, core_ids=list(range(8)))
    out = np.empty((16, S, D), np.float32)
    for c in range(8):
        oa = res.results[c]["oa"]
        if c > 0:
            out[2 * c] = oa[0].T
            out[2 * c + 1] = oa[1].T
    flat01 = out[:2].reshape(2 * S, D)
    for c in range(8):
        ob = res.results[c]["ob"]
        flat01[512 * c:512 * (c + 1)] = ob.T
    return out


# version 18
# speedup vs baseline: 1.0694x; 1.0116x over previous
import numpy as np
import concourse.bass as bass
import concourse.mybir as mybir
from concourse.bass_utils import run_bass_kernel_spmd

F32 = mybir.dt.float32
BF16 = mybir.dt.bfloat16
AF = mybir.ActivationFunctionType
ALU = mybir.AluOpType

D = 1024
DC = 8
S = 2048
H = 16
W = 512
NW = 4
EPS = 1e-5
ENGS = ("sync", "gpsimd", "tensor", "vector", "scalar")

_CACHE = {}
DEBUG = False
HILO = True
PARANOID = 0


class Sem:
    __slots__ = ("h", "n", "name")

    def __init__(self, h, name):
        self.h = h
        self.n = 0
        self.name = name


def build_program():
    nc = bass.Bass("TRN2", target_bir_lowering=False)
    ops = {e: [] for e in ENGS}
    seen = {e: {} for e in ENGS}

    def on(eng, fn):
        ops[eng].append(fn)

    def w(eng, sem, cnt=None):
        if cnt is None:
            cnt = sem.n
        if cnt <= 0:
            return
        if seen[eng].get(sem.name, 0) >= cnt:
            return
        seen[eng][sem.name] = cnt
        on(eng, lambda E, s=sem.h, c=cnt: E.wait_ge(s, c))

    dt = nc.dram_tensor
    xo_d = dt("xo", [2, D, S], F32, kind="ExternalInput").ap()
    xs_d = dt("xs", [D, S], F32, kind="ExternalInput").ap()
    wq_d = dt("wq", [D, D], F32, kind="ExternalInput").ap()
    wk_d = dt("wk", [D, D], F32, kind="ExternalInput").ap()
    wv_d = dt("wv", [D, D], F32, kind="ExternalInput").ap()
    wo_d = dt("wo", [D, D], F32, kind="ExternalInput").ap()
    w1_d = dt("w1", [D, 2 * D], F32, kind="ExternalInput").ap()
    w2_d = dt("w2", [2 * D, D], F32, kind="ExternalInput").ap()
    mw1_d = dt("mw1", [D, 8 * D], F32, kind="ExternalInput").ap()
    mw2_d = dt("mw2", [8 * D, D], F32, kind="ExternalInput").ap()
    ln_d = {}
    for nm in ("lnq_s", "lnk_s", "lnv_s", "lnm_s", "lnq_b", "lnk_b", "lnv_b", "lnm_b"):
        ln_d[nm] = dt(nm, [128, DC], F32, kind="ExternalInput").ap()
    bq_d = dt("bq", [1, D], F32, kind="ExternalInput").ap()
    bk_d = dt("bk", [1, D], F32, kind="ExternalInput").ap()
    bv_d = dt("bv", [1, D], F32, kind="ExternalInput").ap()
    bo_d = dt("bo", [1, D], F32, kind="ExternalInput").ap()
    b1_d = dt("b1", [1, 2 * D], F32, kind="ExternalInput").ap()
    b2_d = dt("b2", [1, D], F32, kind="ExternalInput").ap()
    ew_d = dt("ew", [1, 1024], F32, kind="ExternalInput").ap()
    mko_d = dt("mko", [128, 1], F32, kind="ExternalInput").ap()
    mks_d = dt("mks", [128, 1], F32, kind="ExternalInput").ap()

    oa_d = dt("oa", [2, D, S], F32, kind="ExternalOutput").ap()
    ob_d = dt("ob", [D, W], F32, kind="ExternalOutput").ap()

    zq_d = dt("zq", [D, D], BF16, kind="Internal").ap()
    zk_d = dt("zk", [D, D], BF16, kind="Internal").ap()
    zv_d = dt("zv", [D, D], BF16, kind="Internal").ap()
    zbv_d = dt("zbv", [1, D], BF16, kind="Internal").ap()

    A = nc.alloc_sbuf_tensor
    AR1 = A("AR1", [128, 16384], BF16)
    AR2 = A("AR2", [128, 16384], BF16)
    AR3 = A("AR3", [128, 16640], BF16)
    AR4 = A("AR4", [128, 16384], BF16)
    FR = A("FR", [128, 32768], BF16)
    ones_r_bf = A("ones_r_bf", [1, 512], BF16)
    ones_c_bf = A("ones_c_bf", [128, 1], BF16)
    ones_r_f = A("ones_r_f", [1, 128], F32)
    ones_c_f = A("ones_c_f", [128, 1], F32)
    eps_c = A("eps_c", [128, 1], F32)
    mko_s = A("mko_s", [128, 1], F32)
    mks_s = A("mks_s", [128, 1], F32)
    lncols = A("lncols", [128, 8 * DC], F32)
    lnb_bf = A("lnb_bf", [128, 4 * DC], BF16)
    r_mu = A("r_mu", [1, 512], F32)
    r_rsig = A("r_rsig", [1, 512], F32)
    r_G = A("r_G", [1, 512], F32)
    r_sd = A("r_sd", [1, 512], F32)
    r_var = A("r_var", [1, 512], F32)
    Rb_s = A("Rb_s", [64, 512], F32)
    bcol = A("bcol", [128, 6 * DC], F32)

    PSAL = nc.alloc_psum_tensor
    psS0 = PSAL("psS0", [128, 1024], F32)
    psS1 = PSAL("psS1", [128, 1024], F32)
    psA = PSAL("psA", [128, 512], F32)
    psB = PSAL("psB", [128, 512], F32)
    psC = PSAL("psC", [128, 512], F32)
    psD = PSAL("psD", [128, 512], F32)
    PSN = {"psS0": psS0, "psS1": psS1, "psA": psA, "psB": psB, "psC": psC, "psD": psD}

    AR1f = AR1.ap().bitcast(F32)
    AR2f = AR2.ap().bitcast(F32)
    AR3f = AR3.ap().bitcast(F32)
    AR4f = AR4.ap().bitcast(F32)
    FRf = FR.ap().bitcast(F32)

    pe = Sem(nc.alloc_semaphore(name="pe"), "pe")
    ve = Sem(nc.alloc_semaphore(name="ve"), "ve")
    sc = Sem(nc.alloc_semaphore(name="sc"), "sc")
    LD = [Sem(nc.alloc_semaphore(name=f"ld{i}"), f"ld{i}") for i in range(6)]
    LG = [Sem(nc.alloc_semaphore(name=f"lg{i}"), f"lg{i}") for i in range(4)]
    QST = {"sync": [LD, 0], "gpsimd": [LG, 0]}
    ALLSEMS = (pe, ve, sc) + tuple(LD) + tuple(LG)

    def dma(eng, out_ap, in_ap, slow=False):
        q = QST[eng]
        s = q[0][q[1] % len(q[0])]
        q[1] += 1
        w(eng, s, s.n)
        on(eng, lambda E, o=out_ap, i=in_ap, sh=s.h, sl=slow:
           E.dma_start(out=o, in_=i, allow_slow_non_contiguous=sl).then_inc(sh, 16))
        s.n += 16
        return (s, s.n)

    def wt(eng, tok):
        if tok is not None:
            w(eng, tok[0], tok[1])

    def mm(out_ap, lhsT, rhs, start, stop, inc=False):
        if inc:
            on("tensor", lambda E, o=out_ap, l=lhsT, r=rhs, s1=start, s2=stop:
               E.matmul(o, l, r, start=s1, stop=s2).then_inc(pe.h, 1))
            pe.n += 1
            return pe.n
        on("tensor", lambda E, o=out_ap, l=lhsT, r=rhs, s1=start, s2=stop:
           E.matmul(o, l, r, start=s1, stop=s2))
        return pe.n

    def vop(name, *args, inc=True):
        on("vector", lambda E, n=name, a=args: getattr(E, n)(*a).then_inc(ve.h, 1))
        ve.n += 1
        return ve.n

    def vchain():
        w("vector", ve, ve.n)

    def sact(out_ap, in_ap, func, bias=0.0, scale=1.0, inc=True):
        on("scalar", lambda E, o=out_ap, i=in_ap, f=func, bi=bias, s=scale:
           E.activation(o, i, f, bias=bi, scale=s).then_inc(sc.h, 1))
        sc.n += 1
        return sc.n

    def barrier(engs=ENGS):
        for e in engs:
            for s in ALLSEMS:
                w(e, s, s.n)

    def c3(ap2d, c, n):
        return ap2d.rearrange("p (c n) -> p c n", c=c)

    def dcpn(ap2d):
        return ap2d.rearrange("(c p) n -> p c n", p=128)

    XN = FR.ap()[:, 0:16384]
    E_BUF = [FR.ap()[:, 0:8192], FR.ap()[:, 8192:16384]]
    W1B = FR.ap()[:, 0:16384]
    WPS = FR.ap()[:, 16384:24576]
    SQB = FR.ap()[:, 24576:28672]
    ATTO = FR.ap()[:, 24576:28672]
    BIAS = FR.ap()[:, 28672:32768]
    S1BF = FR.ap()[:, 0:4096]
    MW2B = FR.ap()[:, 16384:20480]

    K_s = AR1.ap()
    Q_s = AR2.ap()
    V_s = AR3.ap()
    ATT = AR4.ap()
    W2B = AR1.ap()
    XW3 = AR2f
    H_F = AR3f
    XNH = AR3.ap()[:, 8192:12288]
    HID = AR3.ap()[:, 12288:16384]
    XW1 = AR4f
    OUT_STG = AR3f[:, 4096:8192]
    S1F = AR4f
    MW1B = AR3.ap()
    HID2 = [AR1.ap(), AR2.ap()]
    OBST = AR2f

    ps_free = {k: (ve, 0) for k in PSN}

    def ps_guard(name):
        s, t = ps_free[name]
        w("tensor", s, t)

    sqb_free = [pe, 0]
    hil_free = [0]

    HIL0 = r_sd.ap().bitcast(BF16)
    HIL1 = r_var.ap().bitcast(BF16)

    def bcast_hilo(row_f32, ps_out, hil, m):
        if not HILO:
            t = vop("memset", hil[0:1, 0:1], 0.0, inc=True)
            w("tensor", ve, t)
            return mm(ps_out, ones_r_f[0:1, 0:m], row_f32, True, True, inc=True)
        vchain()
        w("vector", pe, hil_free[0])
        vop("tensor_copy", hil[0:1, 0:512], row_f32)
        vchain()
        t = vop("tensor_tensor", hil[0:1, 512:1024], row_f32, hil[0:1, 0:512],
                ALU.subtract, inc=True)
        w("tensor", ve, t)
        mm(ps_out, ones_r_bf[0:1, 0:m], hil[0:1, 0:512], True, False)
        t_bc = mm(ps_out, ones_r_bf[0:1, 0:m], hil[0:1, 512:1024], False, True, inc=True)
        hil_free[0] = t_bc
        return t_bc

    vop("memset", ones_r_bf[:, :], 1.0)
    vop("memset", ones_c_bf[:, :], 1.0)
    vop("memset", ones_r_f[:, :], 1.0)
    vop("memset", ones_c_f[:, :], 1.0)
    ve_const = vop("memset", eps_c[:, :], EPS, inc=True)
    for e in ("tensor", "scalar", "sync", "gpsimd"):
        w(e, ve, ve_const)

    ln_toks = []
    for i, nm in enumerate(("lnq_s", "lnk_s", "lnv_s", "lnm_s", "lnq_b", "lnk_b", "lnv_b", "lnm_b")):
        ln_toks.append(dma("sync", lncols[:, i * DC:(i + 1) * DC], ln_d[nm]))
    t_mko = dma("sync", mko_s[:, :], mko_d)
    t_mks = dma("sync", mks_s[:, :], mks_d)
    t_bq = dma("sync", FRf[0:1, 6144:7168], bq_d)
    t_bk = dma("sync", FRf[0:1, 7168:8192], bk_d)
    t_bv = dma("sync", FRf[0:1, 8192:9216], bv_d)
    t_b1 = dma("sync", FRf[0:1, 9216:11264], b1_d)
    t_b2 = dma("sync", FRf[0:1, 11264:12288], b2_d)
    t_bo = dma("sync", FRf[0:1, 12288:13312], bo_d)
    for t in ln_toks[4:8]:
        wt("vector", t)
    ve_lnb = vop("tensor_copy", lnb_bf[:, :], lncols[:, 4 * DC:8 * DC], inc=True)
    def row_to_cols(row_f32, bcol_off, n, gate_dma=None, gate_ve=0):
        ps_guard("psD")
        wt("tensor", gate_dma)
        w("tensor", ve, gate_ve)
        for c in range(n):
            mm(psD[:, c:c + 1], row_f32[0:1, c * 128:(c + 1) * 128],
               ones_c_f[0:1, 0:1], True, True, inc=(c == n - 1))
        t = pe.n
        w("vector", pe, t)
        tk = vop("tensor_copy", bcol[:, bcol_off:bcol_off + n], psD[:, 0:n], inc=True)
        ps_free["psD"] = (ve, tk)
        return t

    row_to_cols(FRf[0:1, 11264:12288], 4 * DC, DC, gate_dma=t_b2)
    row_to_cols(FRf[0:1, 12288:13312], 5 * DC, DC, gate_dma=t_bo)

    prep = [
        (wq_d, zq_d, 0, 6144, D, 0, t_bq),
        (wk_d, zk_d, 1, 7168, D, DC, t_bk),
        (wv_d, zv_d, 2, 8192, D, None, t_bv),
    ]
    beff_ps = [psA, psB, psC, psD]
    frrow_dma = None
    frrow_pe = 0
    scale_tok = [0, 0]
    tmp_tok = [0, 0]
    store_tok = [None, None]
    for (src, zdst, gi, brow_off, dout, beff_dst, t_brow) in prep:
        nb = dout // 512
        mm_tok = 0
        for c in range(DC):
            par = c % 2
            stg = AR1f[:, par * 2048: par * 2048 + dout]
            tmp = AR2.ap()[:, par * 2048: par * 2048 + dout]
            sbf = AR2.ap()[:, 4096 + par * 2048: 4096 + par * 2048 + dout]
            w("sync", ve, scale_tok[par])
            w("sync", pe, mm_tok)
            t_ld = dma("sync", c3(stg, nb, 512), c3(src[c * 128:(c + 1) * 128, :], nb, 512))
            wt("vector", t_ld)
            wt("vector", ln_toks[gi])
            wt("vector", store_tok[par])
            w("vector", ve, ve_lnb)
            w("vector", pe, tmp_tok[par])
            ve_tmp = vop("tensor_copy", tmp, stg, inc=True)
            sc_col = lncols[:, gi * DC + c: gi * DC + c + 1]
            ve_scl = vop("tensor_scalar", sbf, stg, sc_col, None, ALU.mult, inc=True)
            scale_tok[par] = ve_scl
            w("tensor", ve, ve_tmp)
            for nbi in range(nb):
                mm_tok = mm(beff_ps[nbi][0:1, :], lnb_bf[:, gi * DC + c: gi * DC + c + 1],
                            tmp[:, nbi * 512:(nbi + 1) * 512], c == 0, c == DC - 1,
                            inc=(nbi == nb - 1))
            tmp_tok[par] = mm_tok
            w("sync", ve, ve_scl)
            store_tok[par] = dma("sync", c3(zdst[c * 128:(c + 1) * 128, :], nb, 512),
                                 c3(sbf, nb, 512))
        w("vector", pe, mm_tok)
        wt("vector", t_brow)
        wt("vector", frrow_dma)
        w("vector", pe, frrow_pe)
        ve_beff = 0
        for nbi in range(nb):
            ve_beff = vop("tensor_tensor", FRf[0:1, nbi * 512:(nbi + 1) * 512],
                          beff_ps[nbi][0:1, :],
                          FRf[0:1, brow_off + nbi * 512: brow_off + (nbi + 1) * 512],
                          ALU.add, inc=(nbi == nb - 1))
        for k in ("psA", "psB", "psC", "psD")[:nb]:
            ps_free[k] = (ve, ve_beff)
        if beff_dst is None:
            w("gpsimd", ve, ve_beff)
            frrow_dma = dma("gpsimd", zbv_d, FRf[0:1, 0:dout])
        else:
            frrow_pe = row_to_cols(FRf[0:1, 0:dout], beff_dst, dout // 128,
                                   gate_ve=ve_beff)

    wt("gpsimd", frrow_dma)
    t_w1s = dma("gpsimd", c3(W1B, DC, 2048), dcpn(w1_d))
    wt("tensor", t_w1s)
    w("tensor", ve, ve_lnb)
    ps_guard("psA")
    ps_guard("psB")
    ps_guard("psC")
    ps_guard("psD")
    b1ps = [psA, psB, psC, psD]
    for c in range(DC):
        for nbi in range(4):
            mm(b1ps[nbi][0:1, :], lnb_bf[:, 3 * DC + c: 3 * DC + c + 1],
               W1B[:, c * 2048 + nbi * 512:(c * 2048 + (nbi + 1) * 512)], c == 0,
               c == DC - 1, inc=(c == DC - 1 and nbi == 3))
    t_b1mm = pe.n
    w("vector", pe, t_b1mm)
    wt("vector", t_b1)
    ve_b1 = 0
    for nbi in range(4):
        ve_b1 = vop("tensor_tensor", FRf[0:1, 14336 + nbi * 512: 14336 + (nbi + 1) * 512],
                    b1ps[nbi][0:1, :],
                    FRf[0:1, 9216 + nbi * 512: 9216 + (nbi + 1) * 512],
                    ALU.add, inc=(nbi == 3))
    for k in ("psA", "psB", "psC", "psD"):
        ps_free[k] = (ve, ve_b1)
    row_to_cols(FRf[0:1, 14336:16384], 2 * DC, 2 * DC, gate_ve=ve_b1)

    barrier()

    def stats_begin():
        ps_guard("psC")
        ps_guard("psD")

    st_sq = []

    def stats_feed(src_chunk, sq_chunk, t0, n, first, last, prod_ve=0,
                   prod_dma=None, sq_eng="vector"):
        w(sq_eng, sqb_free[0], sqb_free[1])
        if prod_dma is not None:
            wt(sq_eng, prod_dma)
        if prod_ve:
            w(sq_eng, ve, prod_ve)
        if sq_eng == "scalar":
            t_sq = sact(sq_chunk, src_chunk, AF.Square, inc=True)
            st_sq.append((sq_chunk, t0, n, sc, t_sq, first, last))
        else:
            t_sq = vop("tensor_tensor", sq_chunk, src_chunk, src_chunk, ALU.mult,
                       inc=True)
            st_sq.append((sq_chunk, t0, n, ve, t_sq, first, last))
        if prod_dma is not None:
            wt("tensor", prod_dma)
        if prod_ve:
            w("tensor", ve, prod_ve)
        mm(psC[0:1, t0:t0 + n], ones_c_f[:, :], src_chunk, first, last, inc=last)

    def stats_rows(want_G):
        w("tensor", st_sq[-1][3], st_sq[-1][4])
        ps_guard("psD")
        for (sq_chunk, t0, n, _se, t_sq, first, last) in st_sq:
            mm(psD[0:1, t0:t0 + n], ones_c_bf[:, :], sq_chunk, first, last, inc=last)
        st_sq.clear()
        t_sums = pe.n
        sqb_free[0], sqb_free[1] = pe, t_sums
        w("vector", pe, t_sums)
        vop("tensor_scalar", r_mu[:, :], psC[0:1, :], 1.0 / D, None, ALU.mult)
        vop("tensor_scalar", r_var[:, :], psD[0:1, :], 1.0 / D, None, ALU.mult)
        vchain()
        vop("tensor_tensor", r_sd[:, :], r_mu[:, :], r_mu[:, :], ALU.mult)
        vchain()
        t_var = vop("tensor_tensor", r_var[:, :], r_var[:, :], r_sd[:, :], ALU.subtract, inc=True)
        ps_free["psC"] = (ve, t_var)
        ps_free["psD"] = (ve, t_var)
        if want_G:
            vchain()
            vop("tensor_scalar", r_sd[:, :], r_var[:, :], 1.0 + EPS, None, ALU.mult)
            vchain()
            t_var = vop("tensor_scalar", r_var[:, :], r_sd[:, :], EPS * EPS, None,
                        ALU.add, inc=True)
        w("scalar", ve, t_var)
        w("scalar", pe, hil_free[0])
        sact(r_sd[:, :], r_var[:, :], AF.Ln,
             bias=0.0 if want_G else eps_c[0:1, :])
        w("scalar", sc, sc.n)
        t_apply_sc = sact(r_rsig[:, :], r_sd[:, :], AF.Exp, scale=-0.5, inc=True)
        return t_apply_sc

    def stats_bcast(t_apply_sc, psMu_name, psMu, psRs):
        w("vector", sc, t_apply_sc)
        ps_guard(psMu_name)
        bcast_hilo(r_mu[:, :], psMu, HIL0, 128)
        t_bc = bcast_hilo(r_rsig[:, :], psRs, HIL1, 128)
        return t_bc

    def stats_tail(want_G, psMu_name, psMu, psRs):
        return stats_bcast(stats_rows(want_G), psMu_name, psMu, psRs)

    last_oa_store = None
    px_tok = {}

    for wi in range(2):
        px_tok[(0, wi)] = dma("sync", c3(XW1[:, wi * 4096:(wi + 1) * 4096], DC, 512),
                              dcpn(xo_d[0])[:, :, wi * 512:(wi + 1) * 512])

    for bi in range(3):
        owned = bi < 2
        xsrc = xo_d[bi] if owned else xs_d
        mcol = mko_s if owned else mks_s
        qwins = list(range(NW)) if owned else [0]

        barrier()
        v_ones = vop("memset",
                     V_s.rearrange("p (t h c) -> p t h c", h=16, c=65)[:, :, :, 64:65],
                     1.0, inc=True)
        if "wv" in px_tok:
            t_wv = px_tok.pop("wv")
        else:
            t_wv = dma("sync", c3(WPS, DC, 1024), dcpn(zv_d))
        t_beff = dma("sync", BIAS[0:1, 0:1024], zbv_d)
        gidx = [0]

        def emit_V(wi):
            w("tensor", ve, v_ones)
            w("tensor", ve, xn_tok4[wi])
            wt("tensor", t_beff)
            wt("tensor", t_wv)
            for tt in range(4):
                tile = wi * 4 + tt
                for vb in range(2):
                    pname = "psA" if gidx[0] % 2 == 0 else "psB"
                    psP = PSN[pname]
                    ps_guard(pname)
                    for c in range(DC):
                        mm(psP[:, :],
                           XN[:, c * 2048 + tile * 128: c * 2048 + (tile + 1) * 128],
                           WPS[:, c * 1024 + vb * 512: c * 1024 + (vb + 1) * 512],
                           c == 0, False)
                    t_g = mm(psP[:, :], ones_r_bf[0:1, 0:128],
                             BIAS[0:1, vb * 512:(vb + 1) * 512], False, True, inc=True)
                    w("vector", pe, t_g)
                    dst = V_s.rearrange("p (t h c) -> p t h c", h=16, c=65)[
                        :, tile:tile + 1, 8 * vb:8 * (vb + 1), 0:64]
                    tk = vop("tensor_copy", dst,
                             psP.ap().rearrange("p (o h c) -> p o h c", o=1, c=64),
                             inc=True)
                    ps_free[pname] = (ve, tk)
                    gidx[0] += 1

        xn_tok4 = [0, 0, 0, 0]
        xwbuf = [XW1[:, 0:4096], XW1[:, 4096:8192],
                 XW3[:, 0:4096], XW3[:, 4096:8192]]
        pre_x = {}
        for wj in range(2, NW):
            pre_x[wj] = dma("sync", c3(xwbuf[wj], DC, 512),
                            dcpn(xsrc)[:, :, wj * 512:(wj + 1) * 512])
        xn_pend = None
        for wi in range(NW):
            xw = xwbuf[wi]
            if (bi, wi) in px_tok:
                t_x = px_tok.pop((bi, wi))
            else:
                t_x = pre_x[wi]
            stats_begin()
            for c in range(DC):
                stats_feed(xw[:, c * 512:(c + 1) * 512], SQB[:, c * 512:(c + 1) * 512],
                           0, 512, c == 0, c == DC - 1, prod_dma=t_x,
                           sq_eng="scalar")
            if xn_pend is not None:
                xn_pend()
                xn_pend = None
            t_mid = stats_rows(False)
            if wi > 0:
                emit_V(wi - 1)
            t_bc = stats_bcast(t_mid, "psS0", psS0[:, 0:512], psS0[:, 512:1024])

            def xn_pend(wi=wi, xw=xw, t_bc=t_bc):
                w("vector", pe, t_bc)
                tk = 0
                for c in range(DC):
                    vop("tensor_tensor", xw[:, c * 512:(c + 1) * 512],
                        xw[:, c * 512:(c + 1) * 512], psS0[:, 0:512], ALU.subtract)
                    vchain()
                    tk = vop("tensor_tensor",
                             XN[:, c * 2048 + wi * 512: c * 2048 + (wi + 1) * 512],
                             xw[:, c * 512:(c + 1) * 512], psS0[:, 512:1024], ALU.mult,
                             inc=(c == DC - 1))
                xn_tok4[wi] = tk
                ps_free["psS0"] = (ve, tk)
        xn_pend()
        emit_V(NW - 1)
        wp_free_pe = pe.n

        for pi, (zsrc, dest) in enumerate([(zq_d, Q_s), (zk_d, K_s)]):
            wins = [0] if (bi == 2 and pi == 0) else list(range(NW))
            w("sync", pe, wp_free_pe)
            t_w = dma("sync", c3(WPS, DC, 1024), dcpn(zsrc))
            wt("tensor", t_w)
            w("tensor", ve, max(xn_tok4))
            for wi in wins:
                for oc in range(DC):
                    pname = "psA" if gidx[0] % 2 == 0 else "psB"
                    psP = PSN[pname]
                    ps_guard(pname)
                    for c in range(DC):
                        mm(psP[:, :], WPS[:, c * 1024 + oc * 128: c * 1024 + (oc + 1) * 128],
                           XN[:, c * 2048 + wi * 512: c * 2048 + (wi + 1) * 512], c == 0,
                           c == DC - 1, inc=(c == DC - 1))
                    t_g = pe.n
                    w("vector", pe, t_g)
                    tk = vop("tensor_scalar",
                             dest[:, oc * 2048 + wi * 512: oc * 2048 + (wi + 1) * 512],
                             psP[:, :], bcol[:, pi * DC + oc: pi * DC + oc + 1], None,
                             ALU.add, inc=True)
                    ps_free[pname] = (ve, tk)
                    gidx[0] += 1
            wp_free_pe = pe.n

        barrier()
        if DEBUG and bi == 0:
            dbg_q = dt("dbg_q", [128, 16384], BF16, kind="ExternalOutput").ap()
            dbg_k = dt("dbg_k", [128, 16384], BF16, kind="ExternalOutput").ap()
            dbg_v = dt("dbg_v", [128, 16640], BF16, kind="ExternalOutput").ap()
            dbg_bc = dt("dbg_bc", [128, 4 * DC], F32, kind="ExternalOutput").ap()
            dma("sync", dbg_q, Q_s[:, :])
            dma("sync", dbg_k, K_s[:, :])
            dma("sync", dbg_v, V_s[:, :])
            dma("sync", dbg_bc, bcol[:, :])
        w("gpsimd", pe, wp_free_pe)
        t_wo = dma("gpsimd", c3(WPS, DC, 1024), dcpn(wo_d))
        e_free = [0, 0]
        att_free = [ps_free["psA"], ps_free["psB"]]
        rb_read = [0, 0]
        shift_tok = None
        last_exp = [0, 0]
        pend = None
        for qb in qwins:
            odd_evac_tok = 0
            for hp in range(H // 2):
                oc = hp
                exp_toks = []
                for kt in range(16):
                    parS = kt % 2
                    psS = psS0 if parS == 0 else psS1
                    w("tensor", sc, last_exp[parS])
                    if qb == qwins[0] and hp == 0 and kt < 2:
                        ps_guard("psS0" if parS == 0 else "psS1")
                    mm(psS[:, 0:512],
                       K_s[0:64, oc * 2048 + kt * 128: oc * 2048 + (kt + 1) * 128],
                       Q_s[0:64, oc * 2048 + qb * 512: oc * 2048 + (qb + 1) * 512],
                       True, True)
                    mm(psS[:, 512:1024],
                       K_s[64:128, oc * 2048 + kt * 128: oc * 2048 + (kt + 1) * 128],
                       Q_s[64:128, oc * 2048 + qb * 512: oc * 2048 + (qb + 1) * 512],
                       True, True, inc=True)
                    s_tok = pe.n
                    mask_tok = 0
                    if qb == 0 and kt == 15:
                        w("vector", pe, s_tok)
                        wt("vector", t_mko if owned else t_mks)
                        vop("tensor_tensor", psS[:, 0:1], psS[:, 0:1],
                            mcol[:, :], ALU.add)
                        mask_tok = vop("tensor_tensor", psS[:, 512:513], psS[:, 512:513],
                                       mcol[:, :], ALU.add, inc=True)
                    w("scalar", pe, s_tok)
                    if mask_tok:
                        w("scalar", ve, mask_tok)
                    w("scalar", pe, e_free[kt // 8])
                    t_e = sact(E_BUF[kt // 8][:, (kt % 8) * 1024:(kt % 8 + 1) * 1024],
                               psS[:, :], AF.Exp, scale=1.0 / 32.0, inc=True)
                    exp_toks.append(t_e)
                    last_exp[parS] = t_e
                if pend is not None:
                    pend()
                    pend = None
                t_hilo = [0, 0]
                for h2 in range(2):
                    h = 2 * hp + h2
                    pnameA = "psA" if h2 == 0 else "psB"
                    psAtt = PSN[pnameA]
                    sA, tA = att_free[h2]
                    w("tensor", sA, tA)
                    for kt in range(16):
                        w("tensor", sc, exp_toks[kt])
                        inc = (kt == 15) or (h2 == 1 and kt == 7)
                        mm(psAtt[0:65, :],
                           V_s[:, kt * 1040 + h * 65: kt * 1040 + (h + 1) * 65],
                           E_BUF[kt // 8][:, (kt % 8) * 1024 + h2 * 512:
                                          (kt % 8) * 1024 + h2 * 512 + 512],
                           kt == 0, kt == 15, inc=inc)
                        if h2 == 1 and kt == 7:
                            e_free[0] = pe.n
                    pv_tok = pe.n
                    if h2 == 1:
                        e_free[1] = pv_tok
                    rrrow = r_G if h2 == 0 else r_rsig
                    hil = HIL0 if h2 == 0 else HIL1
                    w("vector", pe, pv_tok)
                    w("vector", pe, rb_read[h2])
                    vop("reciprocal", rrrow[0:1, :], psAtt[64:65, :])
                    vchain()
                    vop("tensor_copy", hil[0:1, 0:512], rrrow[0:1, :])
                    vchain()
                    t_hilo[h2] = vop("tensor_tensor", hil[0:1, 512:1024], rrrow[0:1, :],
                                     hil[0:1, 0:512], ALU.subtract, inc=True)

                def make_pend(hp=hp, oc=oc, t_hilo=t_hilo, qb=qb):
                    def pend_fn():
                        nonlocal odd_evac_tok, shift_tok
                        for h2 in range(2):
                            pnameA = "psA" if h2 == 0 else "psB"
                            psAtt = PSN[pnameA]
                            psbn = "psC" if h2 == 0 else "psD"
                            psb = PSN[psbn]
                            hil = HIL0 if h2 == 0 else HIL1
                            ps_guard(psbn)
                            w("tensor", ve, t_hilo[h2])
                            mm(psb[0:64, :], ones_r_bf[0:1, 0:64], hil[0:1, 0:512],
                               True, False)
                            tb = mm(psb[0:64, :], ones_r_bf[0:1, 0:64],
                                    hil[0:1, 512:1024], False, True, inc=True)
                            hil_free[0] = tb
                            rb_read[h2] = tb
                            w("vector", pe, tb)
                            vchain()
                            vop("tensor_copy", Rb_s[:, :], psb[0:64, :])
                            vchain()
                            if h2 == 0:
                                dst = ATT[0:64, oc * 2048 + qb * 512:
                                          oc * 2048 + (qb + 1) * 512]
                            else:
                                wt("vector", shift_tok)
                                dst = ATTO[0:64, oc * 512:(oc + 1) * 512]
                            tk = vop("tensor_tensor", dst, psAtt[0:64, :], Rb_s[:, :],
                                     ALU.mult, inc=True)
                            att_free[h2] = (ve, tk)
                            ps_free[pnameA] = (ve, tk)
                            ps_free[psbn] = (ve, tk)
                            if h2 == 1:
                                odd_evac_tok = tk
                    return pend_fn
                pend = make_pend()
            pend()
            pend = None
            w("sync", ve, odd_evac_tok)
            shift_tok = dma("sync",
                            c3(ATT[64:128, :], DC, 2048)[:, :, qb * 512:(qb + 1) * 512],
                            c3(ATTO[0:64, :], DC, 512))
        ps_free["psS0"] = (sc, last_exp[0])
        ps_free["psS1"] = (sc, last_exp[1])

        barrier()
        if DEBUG and bi == 0:
            dbg_att = dt("dbg_att", [128, 16384], BF16, kind="ExternalOutput").ap()
            dma("sync", dbg_att, ATT[:, :])
        t_w1 = dma("gpsimd", c3(W1B, DC, 2048), dcpn(w1_d))
        t_w2 = dma("gpsimd", c3(W2B, 16, 1024), dcpn(w2_d))
        x_tok = [xn_tok4[2], xn_tok4[3]]
        oa_pend = []
        for qi, qb in enumerate(qwins):
            if PARANOID & 1:
                barrier()
            par = qi % 2
            xw = XW3[:, par * 4096:(par + 1) * 4096]
            scr = XW3[:, (1 - par) * 4096:(2 - par) * 4096]
            w("sync", ve, x_tok[par])
            t_x = dma("sync", c3(xw, DC, 512), dcpn(xsrc)[:, :, qb * 512:(qb + 1) * 512])
            wt("tensor", t_wo)
            h_tok = 0
            stats_begin()
            lagged = []
            wo_bufs = [("psA", psA), ("psB", psB), ("psD", psD),
                       ("psS0", psS0[:, 0:512])]
            for oc2 in range(DC):
                pname, psP = wo_bufs[oc2 % 4]
                ps_guard(pname)
                for c in range(DC):
                    mm(psP[:, :], WPS[:, c * 1024 + oc2 * 128: c * 1024 + (oc2 + 1) * 128],
                       ATT[:, c * 2048 + qb * 512: c * 2048 + (qb + 1) * 512], c == 0,
                       c == DC - 1, inc=(c == DC - 1))
                t_g = pe.n
                if len(lagged) >= 4:
                    lagged.pop(0)()
                if oa_pend:
                    oa_pend.pop(0)()
                    vchain()
                w("vector", pe, t_g)
                wt("vector", t_x)
                tk = vop("scalar_tensor_tensor", H_F[:, oc2 * 512:(oc2 + 1) * 512], psP[:, :],
                         bcol[:, 5 * DC + oc2: 5 * DC + oc2 + 1],
                         xw[:, oc2 * 512:(oc2 + 1) * 512], ALU.add, ALU.add, inc=True)
                ps_free[pname] = (ve, tk)
                h_tok = tk
                lagged.append(lambda oc2=oc2, tk=tk: stats_feed(
                    H_F[:, oc2 * 512:(oc2 + 1) * 512],
                    SQB[:, oc2 * 512:(oc2 + 1) * 512], 0, 512,
                    oc2 == 0, oc2 == DC - 1, prod_ve=tk, sq_eng="scalar"))
            for f in lagged:
                f()
            if qi == len(qwins) - 1 and bi < 2:
                nx = xo_d[bi + 1] if bi < 1 else xs_d
                w("sync", pe, pe.n)
                for wj in range(2):
                    px_tok[(bi + 1, wj)] = dma(
                        "sync", c3(XW1[:, wj * 4096:(wj + 1) * 4096], DC, 512),
                        dcpn(nx)[:, :, wj * 512:(wj + 1) * 512])
                px_tok["wv"] = dma("sync", c3(WPS, DC, 1024), dcpn(zv_d))
            t_bc = stats_tail(False, "psS0", psS0[:, 0:512], psS0[:, 512:1024])
            w("vector", pe, t_bc)
            w("vector", ve, x_tok[1 - par])
            wt("vector", last_oa_store)
            xnh_tok = 0
            xnh_toks = []
            for c in range(DC):
                vop("tensor_tensor", scr[:, c * 512:(c + 1) * 512],
                    H_F[:, c * 512:(c + 1) * 512], psS0[:, 0:512], ALU.subtract)
                vchain()
                xnh_tok = vop("scalar_tensor_tensor", XNH[:, c * 512:(c + 1) * 512],
                              scr[:, c * 512:(c + 1) * 512],
                              lncols[:, 3 * DC + c: 3 * DC + c + 1],
                              psS0[:, 512:1024], ALU.mult, ALU.mult,
                              inc=(c == DC - 1))
                xnh_toks.append(ve.n)
            x_tok[1 - par] = xnh_tok
            ps_free["psS0"] = (ve, xnh_tok)
            wt("tensor", t_w1)
            wt("tensor", t_w2)
            pre_tok = 0
            hid_free_pe = 0
            stats_begin()
            lagged2 = []
            for th in range(2):
                gel_toks = []
                for ic in range(16):
                    pname = "psA" if ic % 2 == 0 else "psB"
                    psP = PSN[pname]
                    ps_guard(pname)
                    for c in range(DC):
                        w("tensor", ve, xnh_toks[c])
                        mm(psP[:, 0:256], W1B[:, c * 2048 + ic * 128: c * 2048 + (ic + 1) * 128],
                           XNH[:, c * 512 + th * 256: c * 512 + th * 256 + 256], c == 0,
                           c == DC - 1, inc=(c == DC - 1))
                    t_g = pe.n
                    w("scalar", pe, t_g)
                    w("scalar", pe, hid_free_pe)
                    wt("scalar", last_oa_store)
                    t_gel = sact(HID[:, ic * 256:(ic + 1) * 256], psP[:, 0:256],
                                 AF.Gelu_apprx_tanh,
                                 bias=bcol[:, 2 * DC + ic: 2 * DC + ic + 1], inc=True)
                    gel_toks.append(t_gel)
                    ps_free[pname] = (sc, t_gel)
                for oc2 in range(DC):
                    pname, psP = wo_bufs[oc2 % 4]
                    ps_guard(pname)
                    w("tensor", sc, gel_toks[-1])
                    for ic in range(16):
                        mm(psP[:, 0:256],
                           W2B[:, ic * 1024 + oc2 * 128: ic * 1024 + (oc2 + 1) * 128],
                           HID[:, ic * 256:(ic + 1) * 256], ic == 0, ic == 15,
                           inc=(ic == 15))
                    t_g = pe.n
                    if len(lagged2) >= 4:
                        lagged2.pop(0)()
                    w("vector", pe, t_g)
                    off = oc2 * 512 + th * 256
                    vop("scalar_tensor_tensor", H_F[:, off:off + 256], psP[:, 0:256],
                        bcol[:, 4 * DC + oc2: 4 * DC + oc2 + 1],
                        H_F[:, off:off + 256], ALU.add, ALU.add)
                    vchain()
                    tk = vop("tensor_tensor", H_F[:, off:off + 256], H_F[:, off:off + 256],
                             xw[:, off:off + 256], ALU.add, inc=True)
                    ps_free[pname] = (ve, tk)
                    pre_tok = tk
                    lagged2.append(lambda off=off, th=th, oc2=oc2, tk=tk: stats_feed(
                        H_F[:, off:off + 256], SQB[:, off:off + 256],
                        th * 256, 256, oc2 == 0, oc2 == DC - 1, prod_ve=tk))
                hid_free_pe = pe.n
                for f in lagged2:
                    f()
                lagged2 = []
            t_bc = stats_tail(owned, "psS1", psS1[:, 0:512], psS1[:, 512:1024])

            def oa_piece(c, par=par, qb=qb, t_bc=t_bc, owned=owned):
                nonlocal last_oa_store
                if c == 0:
                    w("vector", pe, t_bc)
                    wt("vector", last_oa_store)
                vop("tensor_tensor", H_F[:, c * 512:(c + 1) * 512],
                    H_F[:, c * 512:(c + 1) * 512], psS1[:, 0:512], ALU.subtract)
                vchain()
                out_tok = vop("tensor_tensor", OUT_STG[:, c * 512:(c + 1) * 512],
                              H_F[:, c * 512:(c + 1) * 512], psS1[:, 512:1024],
                              ALU.mult, inc=True)
                if c == DC - 1:
                    ps_free["psS1"] = (ve, out_tok)
                    x_tok[par] = out_tok
                    if owned:
                        w("sync", ve, out_tok)
                        last_oa_store = dma(
                            "sync", dcpn(oa_d[bi])[:, :, qb * 512:(qb + 1) * 512],
                            c3(OUT_STG[:, 0:4096], DC, 512))
                    else:
                        vchain()
                        for cc in range(DC):
                            vop("tensor_copy", S1F[:, cc * 512:(cc + 1) * 512],
                                OUT_STG[:, cc * 512:(cc + 1) * 512],
                                inc=(cc == DC - 1))

            oa_pend = [(lambda c=c: oa_piece(c)) for c in range(DC)]
        for f in oa_pend:
            f()
        oa_pend = []

    barrier()
    vchain()
    s1_tok = 0
    for c in range(DC):
        s1_tok = vop("tensor_copy", S1BF[:, c * 512:(c + 1) * 512],
                     S1F[:, c * 512:(c + 1) * 512], inc=(c == DC - 1))
    t_ew = dma("sync", FRf[0:1, 2048:3072], ew_d)
    wt("vector", t_ew)
    ewv = FRf[0:1, 2048:3072].rearrange("a (t k) -> a t k", k=2)
    t_ms = vop("tensor_tensor", r_G[0:1, :].rearrange("a (t k) -> a t k", k=1),
               ewv[:, :, 0:1], ewv[:, :, 1:2], ALU.add, inc=True)
    w("tensor", ve, t_ms)
    ps_guard("psC")
    t_bcm = bcast_hilo(r_G[0:1, :], psC[:, :], HIL0, 128)
    w("vector", pe, t_bcm)
    t_msb = vop("tensor_copy", AR4f[:, 4096:4608], psC[:, :], inc=True)
    ps_free["psC"] = (ve, t_msb)

    mw1_free = [0, 0]
    gel_tok = []
    for icb in range(8):
        par = icb % 2
        buf = MW1B[:, par * 8192:(par + 1) * 8192]
        w("gpsimd", pe, mw1_free[par])
        t_wb = dma("gpsimd", c3(buf, DC, 1024), dcpn(mw1_d[:, icb * 1024:(icb + 1) * 1024]))
        for icl in range(8):
            pname = "psA" if (icb * 8 + icl) % 2 == 0 else "psB"
            psP = PSN[pname]
            ps_guard(pname)
            wt("tensor", t_wb)
            w("tensor", ve, s1_tok)
            for c in range(DC):
                mm(psP[:, :], buf[:, c * 1024 + icl * 128: c * 1024 + (icl + 1) * 128],
                   S1BF[:, c * 512:(c + 1) * 512], c == 0, c == DC - 1,
                   inc=(c == DC - 1))
            t_g = pe.n
            w("scalar", pe, t_g)
            ic64 = icb * 8 + icl
            dstH = HID2[ic64 // 32][:, (ic64 % 32) * 512:(ic64 % 32 + 1) * 512]
            t_gel = sact(dstH, psP[:, :], AF.Gelu_apprx_tanh, inc=True)
            gel_tok.append(t_gel)
            ps_free[pname] = (sc, t_gel)
        mw1_free[par] = pe.n

    psY = [psS0.ap()[:, 0:512], psS0.ap()[:, 512:1024], psS1.ap()[:, 0:512],
           psS1.ap()[:, 512:1024], psA.ap(), psB.ap(), psC.ap(), psD.ap()]
    for k in PSN:
        ps_guard_eng = ps_free[k]
        w("tensor", ps_guard_eng[0], ps_guard_eng[1])
    mw2_free = [0, 0, 0, 0]
    w2toks = {}
    y2_tok = 0
    for ic64 in range(64):
        for pf in range(ic64 if ic64 else 0, min(ic64 + 4, 64)):
            if pf not in w2toks:
                par_pf = pf % 4
                w("gpsimd", pe, mw2_free[par_pf])
                w2toks[pf] = dma("gpsimd", MW2B[:, par_pf * 1024:(par_pf + 1) * 1024],
                                 mw2_d[pf * 128:(pf + 1) * 128, :])
        par = ic64 % 4
        wbuf = MW2B[:, par * 1024:(par + 1) * 1024]
        wt("tensor", w2toks.pop(ic64))
        w("tensor", sc, gel_tok[ic64])
        rhs = HID2[ic64 // 32][:, (ic64 % 32) * 512:(ic64 % 32 + 1) * 512]
        for oc in range(8):
            y2_tok = mm(psY[oc], wbuf[:, oc * 128:(oc + 1) * 128], rhs,
                        ic64 == 0, ic64 == 63, inc=(oc == 7))
        mw2_free[par] = y2_tok

    w("vector", pe, y2_tok)
    w("vector", ve, t_msb)
    z_tok = 0
    for oc in range(8):
        vop("tensor_tensor", psY[oc], psY[oc], AR4f[:, 4096:4608], ALU.mult)
        vchain()
        z_tok = vop("tensor_tensor", S1F[:, oc * 512:(oc + 1) * 512], psY[oc],
                    S1F[:, oc * 512:(oc + 1) * 512], ALU.add, inc=True)
    for k in PSN:
        ps_free[k] = (ve, z_tok)
    stats_begin()
    for c in range(DC):
        stats_feed(S1F[:, c * 512:(c + 1) * 512], SQB[:, c * 512:(c + 1) * 512],
                   0, 512, c == 0, c == DC - 1, prod_ve=z_tok)
    t_bc = stats_tail(False, "psS0", psS0[:, 0:512], psS0[:, 512:1024])
    w("vector", pe, t_bc)
    ob_tok = 0
    for c in range(DC):
        vop("tensor_tensor", S1F[:, c * 512:(c + 1) * 512],
            S1F[:, c * 512:(c + 1) * 512], psS0[:, 0:512], ALU.subtract)
        vchain()
        ob_tok = vop("tensor_tensor", OBST[:, c * 512:(c + 1) * 512],
                     S1F[:, c * 512:(c + 1) * 512], psS0[:, 512:1024], ALU.mult,
                     inc=(c == DC - 1))
    w("sync", ve, ob_tok)
    dma("sync", dcpn(ob_d), c3(OBST[:, 0:4096], DC, 512))

    for s in LD + LG:
        w("sync", s, s.n)

    with nc.Block() as block:
        @block.sync
        def _(E):
            for f in ops["sync"]:
                f(E)

        @block.gpsimd
        def _(E):
            for f in ops["gpsimd"]:
                f(E)

        @block.tensor
        def _(E):
            for f in ops["tensor"]:
                f(E)

        @block.vector
        def _(E):
            for f in ops["vector"]:
                f(E)

        @block.scalar
        def _(E):
            for f in ops["scalar"]:
                f(E)

    return nc


def _prep_host(inputs):
    f = lambda k: np.asarray(inputs[k], dtype=np.float32)
    xT = np.ascontiguousarray(f("L0_attn_x").transpose(0, 2, 1))
    shared = {
        "wq": np.ascontiguousarray(f("L0_attn_Wq")),
        "wk": np.ascontiguousarray(f("L0_attn_Wk")),
        "wv": np.ascontiguousarray(f("L0_attn_Wv")),
        "wo": np.ascontiguousarray(f("L0_attn_Wo")),
        "w1": np.ascontiguousarray(f("L0_attn_W1")),
        "w2": np.ascontiguousarray(f("L0_attn_W2")),
        "mw1": np.ascontiguousarray(f("L0_ffn_w1")),
        "mw2": np.ascontiguousarray(f("L0_ffn_w2")),
        "bq": f("L0_attn_bq").reshape(1, D),
        "bk": f("L0_attn_bk").reshape(1, D),
        "bv": f("L0_attn_bv").reshape(1, D),
        "bo": f("L0_attn_bo").reshape(1, D),
        "b1": f("L0_attn_b1").reshape(1, 2 * D),
        "b2": f("L0_attn_b2").reshape(1, D),
    }
    for nm, key in (("lnq_s", "L0_attn_lnq_s"), ("lnk_s", "L0_attn_lnk_s"),
                    ("lnv_s", "L0_attn_lnv_s"), ("lnm_s", "L0_attn_lnm_s"),
                    ("lnq_b", "L0_attn_lnq_b"), ("lnk_b", "L0_attn_lnk_b"),
                    ("lnv_b", "L0_attn_lnv_b"), ("lnm_b", "L0_attn_lnm_b")):
        shared[nm] = np.ascontiguousarray(f(key).reshape(DC, 128).T)
    mko = np.zeros((128, 1), np.float32)
    mko[127, 0] = -1e9
    shared["mko"] = mko
    ew_all = f("L0_ffn_expert_weights")
    in_maps = []
    for c in range(8):
        mb, sw = c // 4, c % 4
        m = dict(shared)
        m["xo"] = np.ascontiguousarray(xT[2 * c:2 * c + 2])
        m["xs"] = np.ascontiguousarray(np.roll(xT[mb], -512 * sw, axis=1))
        m["ew"] = np.ascontiguousarray(ew_all[512 * c:512 * (c + 1)].reshape(1, 1024))
        mks = np.zeros((128, 1), np.float32)
        if sw == 0:
            mks[127, 0] = -1e9
        m["mks"] = mks
        in_maps.append(m)
    return in_maps


def kernel(**inputs):
    if "nc" not in _CACHE:
        _CACHE["nc"] = build_program()
    nc = _CACHE["nc"]
    in_maps = _prep_host(inputs)
    res = run_bass_kernel_spmd(nc, in_maps, core_ids=list(range(8)))
    out = np.empty((16, S, D), np.float32)
    for c in range(8):
        oa = res.results[c]["oa"]
        if c > 0:
            out[2 * c] = oa[0].T
            out[2 * c + 1] = oa[1].T
    flat01 = out[:2].reshape(2 * S, D)
    for c in range(8):
        ob = res.results[c]["ob"]
        flat01[512 * c:512 * (c + 1)] = ob.T
    return out
